# version 06
# baseline (speedup 1.0000x reference)
import contextlib
import os
import numpy as np
import ml_dtypes

import concourse.bass as bass
import concourse.mybir as mybir
from concourse.bass_utils import run_bass_kernel_spmd

F32 = mybir.dt.float32
BF16 = mybir.dt.bfloat16
I32 = mybir.dt.int32
AX = mybir.AxisListType.X
OP = mybir.AluOpType
AF = mybir.ActivationFunctionType
bf16 = ml_dtypes.bfloat16

B, L, D, H, HD, HID = 2, 2048, 1024, 16, 64, 3072
TT, DC, NG, NP = 16, 8, 4, 8
NBLK, PPB = 4, 2
BW = PPB * 128
EPS = 1e-5
EPS4 = 1e-5 * float(127 ** 4)
QK_SCALE = 0.125
GB_OFFS = [-384, -256, -128, 0, 128, 256, 384, 512]

DEBUG_TAPS = ()


def host_constants():
    t = np.arange(L, dtype=np.float64)
    invf = 10000.0 ** (-np.arange(0, HD, 2, dtype=np.float64) / HD)
    emb = np.concatenate([t[:, None] * invf[None, :]] * 2, axis=1)
    cos, sin = np.cos(emb), np.sin(emb)

    def tmaj(a):
        return np.ascontiguousarray(
            a.reshape(TT, 128, HD).transpose(1, 0, 2).reshape(128, TT * HD)
        ).astype(bf16)

    sin_s = sin.copy()
    sin_s[:, :HD // 2] *= -1.0
    return {
        "c_cosq": tmaj(cos),
        "c_sinq": tmaj(sin_s),
        "c_eye": np.eye(128, dtype=np.float32).astype(bf16),
        "c_eye8": (8.0 * np.eye(128, dtype=np.float32)).astype(bf16),
        "c_onesT": np.ones((128, 64), dtype=np.float32).astype(bf16),
        "c_onesrow": np.ones((64, 128), dtype=np.float32).astype(bf16),
        "c_negrow": np.full((64, 128), -1.0, dtype=np.float32).astype(bf16),
        "c_recipbc": np.ascontiguousarray(np.broadcast_to(
            1.0 / (np.arange(L, dtype=np.float64) + 1.0), (64, L))
        ).astype(np.float32),
    }


CONST_SPECS = [
    ("c_cosq", [128, TT * HD], BF16), ("c_sinq", [128, TT * HD], BF16),
    ("c_eye", [128, 128], BF16), ("c_eye8", [128, 128], BF16),
    ("c_onesT", [128, 64], BF16),
    ("c_onesrow", [64, 128], BF16), ("c_negrow", [64, 128], BF16),
    ("c_recipbc", [64, L], F32),
]


def host_gb_tiles(biases):
    base = 1024
    ext = np.zeros((H, 4096), dtype=np.float32)
    ext[:, :base] = -1e30
    ext[:, base:base + 513] = np.asarray(biases, dtype=np.float32)
    oi = np.asarray(GB_OFFS)[:, None, None]
    j = np.arange(128)[None, :, None]
    i = np.arange(512)[None, None, :]
    return np.ascontiguousarray(ext[:, base + oi + i - j])


def gb_off_index(g, jt):
    off = 512 * g - 128 * jt
    return GB_OFFS.index(off) if off <= 512 + 127 else None


def build_program():
    nc = bass.Bass("TRN2", target_bir_lowering=False,
                   detect_race_conditions=False)

    xin = nc.dram_tensor("xin", [B, L, D], F32, kind="ExternalInput").ap()
    wqT = nc.dram_tensor("wqT", [D, D], F32, kind="ExternalInput").ap()
    wkT = nc.dram_tensor("wkT", [D, D], F32, kind="ExternalInput").ap()
    wvT = nc.dram_tensor("wvT", [D, D], F32, kind="ExternalInput").ap()
    woT = nc.dram_tensor("woT", [D, D], F32, kind="ExternalInput").ap()
    wgT = nc.dram_tensor("wgT", [D, HID], I32, kind="ExternalInput").ap()
    wuT = nc.dram_tensor("wuT", [D, HID], I32, kind="ExternalInput").ap()
    wdT = nc.dram_tensor("wdT", [HID, D], I32, kind="ExternalInput").ap()
    gbt = nc.dram_tensor("gbt", [H, 8, 128, 512], F32, kind="ExternalInput").ap()
    biasd = nc.dram_tensor("biasd", [H, 513], F32, kind="ExternalInput").ap()
    taud = nc.dram_tensor("taud", [1, H], F32, kind="ExternalInput").ap()
    cn = {}
    for name, shape, dt in CONST_SPECS:
        cn[name] = nc.dram_tensor(name, shape, dt, kind="ExternalInput").ap()

    out = nc.dram_tensor("out", [B, L, D], F32, kind="ExternalOutput").ap()
    x2d = nc.dram_tensor("x2d", [L, D], F32, kind="Internal").ap()
    wqB = nc.dram_tensor("wqB", [D * D], BF16, kind="Internal").ap()
    wkB = nc.dram_tensor("wkB", [D * D], BF16, kind="Internal").ap()
    wvB = nc.dram_tensor("wvB", [D * D], BF16, kind="Internal").ap()
    woB = nc.dram_tensor("woB", [D * D], BF16, kind="Internal").ap()
    wgB = nc.dram_tensor("wgB", [D * HID], BF16, kind="Internal").ap()
    wuB = nc.dram_tensor("wuB", [D * HID], BF16, kind="Internal").ap()
    wdB = nc.dram_tensor("wdB", [HID * D], BF16, kind="Internal").ap()
    gbB = nc.dram_tensor("gbB", [H * 8 * 128 * 512], BF16, kind="Internal").ap()
    dbg = {}
    dbg_shapes = {"stat1": [128, 256], "xraw": [128, 14336],
                  "xhatfm": [128, DC * L], "qfm0": [128, PPB * L],
                  "kfm0": [128, PPB * L], "vtm0": [128, TT * BW],
                  "attnfm": [128, DC * L], "x2": [L, D], "serow00": [64, 512],
                  "phat00": [128, 16 * 512], "hfm1": [128, 24 * 512],
                  "ffn_d": [4, 128, D]}
    for name in DEBUG_TAPS:
        dbg[name] = nc.dram_tensor("dbg_" + name, dbg_shapes[name], F32,
                                   kind="ExternalOutput").ap()

    ENGS = ("sync", "gpsimd", "tensor", "scalar", "vector")
    prog = {k: [] for k in ENGS}
    cnt = {}
    semh = {}

    class Sem:
        def __init__(self, name):
            self.name = name

    (s_wl, s_wg, s_xl, s_gb, s_mm, s_sc, s_ve, s_st, s_wh, s_x2, s_x3,
     s_w2) = (Sem(n) for n in (
         "s_wl", "s_wg", "s_xl", "s_gb", "s_mm", "s_sc", "s_ve", "s_st",
         "s_wh", "s_x2", "s_x3", "s_w2"))
    ALL_SEMS = [s_wl, s_wg, s_xl, s_gb, s_mm, s_sc, s_ve, s_st, s_wh, s_x2,
                s_x3, s_w2]

    KSTOP = os.environ.get("KSTOP", "")
    stopped = [False]

    def ckpt(name):
        if KSTOP and name == KSTOP:
            stopped[0] = True

    def E(eng, fn):
        if not stopped[0]:
            prog[eng].append(fn)

    def inc(instr, sr, k=1):
        instr.then_inc(semh[sr.name], k)

    def bump(sr, k=1):
        cnt[sr.name] = cnt.get(sr.name, 0) + k
        return cnt[sr.name]

    def tgt(sr):
        return cnt.get(sr.name, 0)

    def wait(eng, sr, n=None):
        n = tgt(sr) if n is None else n
        if n > 0:
            E(eng, lambda e, s=sr, n=n: e.wait_ge(semh[s.name], n))

    with contextlib.ExitStack() as stack:
        def sbuf(name, shape, dtype):
            return stack.enter_context(nc.sbuf_tensor(name, shape, dtype))

        def psum(name, shape, dtype=F32):
            return stack.enter_context(nc.psum_tensor(name, shape, dtype))

        eye = sbuf("eye", [128, 128], BF16)
        eye8 = sbuf("eye8", [128, 128], BF16)
        onesT = sbuf("onesT", [128, 64], BF16)
        onesrow = sbuf("onesrow", [64, 128], BF16)
        negrow = sbuf("negrow", [64, 128], BF16)
        cosq = sbuf("cosq", [128, TT * HD], BF16)
        sinq = sbuf("sinq", [128, TT * HD], BF16)
        recipbc = sbuf("recipbc", [64, L], F32)
        taurow = sbuf("taurow", [1, H], F32)
        taupair = sbuf("taupair", [64, NP], F32)
        biassb = sbuf("biassb", [16, 513], F32)
        maxgbrow = sbuf("maxgbrow", [1, 128], F32)
        colmb = sbuf("colmb", [128, 1], BF16)

        xhat_fm = sbuf("xhat_fm", [128, DC * L], BF16)
        u64f = sbuf("u64f", [128, 14 * 1024], F32)
        u64b = u64f[:, :].bitcast(BF16)
        PHA, PHB = 0, 8192
        QFM, KFM, VTM = 16384, 20480, 24576
        WD0, WG0, WU0 = 0, 24576, 26624
        u74 = sbuf("u74", [128, 31744], BF16)
        AFM, GBS = 0, 16384
        WQS, WKS, WVS = 24576, 26624, 28672
        CTA, CTB, WOS, HFM = 30720, 31232, 16384, 0

        serow = sbuf("serow", [64, 512], F32)
        csrow = sbuf("csrow", [64, 512], F32)
        rowb = sbuf("rowb", [64, 1536], BF16)
        vpre = sbuf("vpre", [64, (NG + 1) * 128], F32)
        vtailf = sbuf("vtailf", [64, NG * 128], F32)
        mrow = sbuf("mrow", [1, 32], F32)
        ccol = sbuf("ccol", [128, 4], F32)
        colb = sbuf("colb", [128, 4], BF16)
        negmc = sbuf("negmc", [128, 1], F32)
        xio = sbuf("xio", [128, 2 * D], F32)
        xhat_tm = sbuf("xhat_tm", [128, 2 * D], BF16)
        lnsc = sbuf("lnsc", [128, D], F32)
        stat = sbuf("stat", [128, 256], F32)
        ropeA = sbuf("ropeA", [128, BW], F32)
        ropeB = sbuf("ropeB", [128, BW], F32)
        outsb = sbuf("outsb", [128, 2 * D], F32)
        gusb = sbuf("gusb", [128, 512], BF16)
        ropeQ = sbuf("ropeQ", [128, BW], BF16)
        ropeK = sbuf("ropeK", [128, BW], BF16)

        ps = [psum(f"ps{i}", [128, 512]) for i in range(6)]
        ps_r = psum("ps_r", [128, 512])
        ps_t = psum("ps_t", [128, 1024], BF16)
        ps_tf = ps_t[:, :].bitcast(F32)

        for sr in ALL_SEMS:
            semh[sr.name] = stack.enter_context(nc.semaphore(sr.name))

        def fmview(base_ap, nchunk, chunklen, sub0, subn):
            v = base_ap.rearrange("p (c m) -> p c m", m=chunklen)
            return v[:, 0:nchunk, sub0:sub0 + subn]

        CONST_DSTS = [("c_cosq", cosq), ("c_sinq", sinq),
                      ("c_recipbc", recipbc), ("c_eye", eye),
                      ("c_eye8", eye8), ("c_onesT", onesT),
                      ("c_onesrow", onesrow), ("c_negrow", negrow)]

        def _consts(sy):
            for name, dst in CONST_DSTS:
                inc(sy.dma_start(out=dst[:, :], in_=cn[name][:, :]), s_wl, 16)
            inc(sy.dma_start(out=taurow[:, :], in_=taud[:, :]), s_wl, 16)
            inc(sy.dma_start(out=biassb[:, :], in_=biasd[:, :]), s_wl, 16)
        E("sync", _consts)
        bump(s_wl, 16 * (len(CONST_DSTS) + 2))

        def _taus(gp):
            for p in range(NP):
                inc(gp.dma_start(
                    out=taupair[0:32, p:p + 1],
                    in_=taud[0:1, 2 * p:2 * p + 1].partition_broadcast(32)),
                    s_wg, 16)
                inc(gp.dma_start(
                    out=taupair[32:64, p:p + 1],
                    in_=taud[0:1, 2 * p + 1:2 * p + 2].partition_broadcast(32)),
                    s_wg, 16)
        E("gpsimd", _taus)
        bump(s_wg, 16 * 16)

        GBCH = 8 * 128 * 512
        CONV_SEQ = [("wq", wqT, wqB, 1), ("wk", wkT, wkB, 1),
                    ("wv", wvT, wvB, 1)]
        CONV_SEQ += [(("gb", h), gbt, gbB, None) for h in range(H)]
        CONV_SEQ += [("wo", woT, woB, 1), ("wd", wdT, wdB, 4),
                     ("wg", wgT, wgB, 4), ("wu", wuT, wuB, 4)]
        CV = {}

        cv_gates = {"wv", ("gb", 1), ("gb", 15)}

        def _conv(gp, base=tgt(s_wg)):
            cum = base
            for key, src, dst, nch in CONV_SEQ:
                n = int(np.prod(src.tensor.shape))
                if nch is None:
                    h = key[1]
                    inc(gp.dma_start(out=dst[h * GBCH:(h + 1) * GBCH],
                                     in_=bass.AP(src.tensor, h * GBCH,
                                                 [[1, GBCH]])), s_wg, 16)
                    cum += 16
                else:
                    c = n // nch
                    for j in range(nch):
                        inc(gp.dma_start(out=dst[j * c:(j + 1) * c],
                                         in_=bass.AP(src.tensor, j * c,
                                                     [[1, c]])), s_wg, 16)
                    cum += 16 * nch
                if key in cv_gates:
                    gp.wait_ge(semh[s_wg.name], cum)
        E("gpsimd", _conv)
        for key, src, dst, nch in CONV_SEQ:
            bump(s_wg, 16 * (1 if nch is None else nch))
            CV[key] = tgt(s_wg)
        wait("vector", s_wl)
        wait("vector", s_wg, 16 * 16)

        def _maxgb(v):
            v.drain()
            v.memset(ccol[:, 0:1], float(EPS))
            v.memset(ccol[:, 1:2], float(EPS4))
            v.memset(ccol[:, 2:3], 1e-38)
            v.memset(ccol[:, 3:4], -2.0794415416798357)
            v.memset(colmb[:, :], 0.0)
            v.reduce_max(stat[0:16, 0:1], biassb[:, :], axis=AX,
                         apply_absolute_value=True)
            v.drain()
            inc(v.tensor_copy(colmb[0:16, 0:1], stat[0:16, 0:1]), s_ve)
        E("vector", _maxgb)
        bump(s_ve)
        wait("tensor", s_ve)

        def _maxgbt(t):
            inc(t.transpose(ps_t[0:1, 0:128], colmb[:, :], eye[:, :]), s_mm)
        E("tensor", _maxgbt)
        bump(s_mm)
        wait("vector", s_mm)

        def _maxgbr(v):
            inc(v.tensor_copy(maxgbrow[0:1, 0:128], ps_t[0:1, 0:128]), s_ve)
        E("vector", _maxgbr)
        bump(s_ve)
        st8_init_p1evac = tgt(s_ve)

        def ln_tail(colbase, idxs, w, eps):
            wait("vector", s_sc)

            def _var(v, idxs=tuple(idxs), w=w, colbase=colbase):
                n = len(idxs)
                for k in range(n):
                    c = colbase + k * 12
                    v.tensor_scalar(stat[:, c + 4:c + 5], stat[:, c + 3:c + 4],
                                    1.0 / w, None, op0=OP.mult)
                    v.tensor_scalar(stat[:, c + 5:c + 6], stat[:, c + 2:c + 3],
                                    1.0 / w, None, op0=OP.mult)
                if n < 4:
                    v.drain()
                for k in range(n):
                    c = colbase + k * 12
                    v.tensor_tensor(stat[:, c + 6:c + 7], stat[:, c + 4:c + 5],
                                    stat[:, c + 4:c + 5], op=OP.mult)
                if n < 4:
                    v.drain()
                last = None
                for k in range(n):
                    c = colbase + k * 12
                    last = v.tensor_sub(stat[:, c + 7:c + 8],
                                        stat[:, c + 5:c + 6], stat[:, c + 6:c + 7])
                inc(last, s_ve)
            E("vector", _var)
            bump(s_ve)
            wait("scalar", s_ve)

            def _rstd(sc, idxs=tuple(idxs), colbase=colbase,
                      eps_ap=(ccol[:, 0:1] if eps == EPS else ccol[:, 1:2])):
                last = None
                for k in range(len(idxs)):
                    c = colbase + k * 12
                    sc.activation(out=stat[:, c + 8:c + 9], in_=stat[:, c + 7:c + 8],
                                  func=AF.Ln, bias=eps_ap)
                    last = sc.activation(out=stat[:, c + 0:c + 1],
                                         in_=stat[:, c + 8:c + 9],
                                         func=AF.Exp, scale=-0.5)
                inc(last, s_sc)
            E("scalar", _rstd)
            bump(s_sc)
            wait("vector", s_sc)

            def _bias(v, idxs=tuple(idxs), colbase=colbase):
                last = None
                for k in range(len(idxs)):
                    c = colbase + k * 12
                    last = v.tensor_tensor(stat[:, c + 1:c + 2],
                                           stat[:, c + 4:c + 5],
                                           stat[:, c + 0:c + 1], op=OP.mult)
                inc(last, s_ve)
            E("vector", _bias)
            bump(s_ve)
            wait("scalar", s_ve)

        st8 = {}

        st8["p1evac"] = st8_init_p1evac
        for b in range(B):
            def xsrc(i):
                if i < 14:
                    return u64f[:, i * 1024:(i + 1) * 1024]
                return xio[:, (i - 14) * D:(i - 13) * D]

            wait("sync", s_mm)
            wait("sync", s_sc)
            wait("sync", s_ve)

            def _ldx(sy, b=b):
                last = None
                for i in range(TT):
                    last = sy.dma_start(out=xsrc(i),
                                        in_=xin[b, i * 128:(i + 1) * 128, :])
                    inc(last, s_xl, 16)
            E("sync", _ldx)
            bump(s_xl, 16 * TT)
            wait("scalar", s_xl)
            wait("vector", s_xl)

            def _sq1(sc):
                last = None
                for i in range(TT):
                    c = i * 12
                    last = sc.activation(out=lnsc[:, :], in_=xsrc(i), func=AF.Square,
                                         accum_out=stat[:, c + 2:c + 3])
                inc(last, s_sc)
            E("scalar", _sq1)
            bump(s_sc)

            def _ns1(v):
                last = None
                for i in range(TT):
                    c = i * 12
                    last = v.reduce_sum(stat[:, c + 3:c + 4], xsrc(i), axis=AX,
                                        negate=True)
                inc(last, s_ve)
            E("vector", _ns1)
            bump(s_ve)
            ln_tail(0, range(TT), D, EPS)

            for i in range(TT):
                mmp = st8.get(("xtm_mm", i % 2), 0)

                def _ap(sc, i=i, mmp=mmp):
                    if mmp > 0:
                        sc.wait_ge(semh[s_mm.name], mmp)
                    c = i * 12
                    inc(sc.activation(out=xhat_tm[:, (i % 2) * D:(i % 2 + 1) * D],
                                      in_=xsrc(i), func=AF.Identity,
                                      scale=stat[:, c + 0:c + 1],
                                      bias=stat[:, c + 1:c + 2]), s_sc)
                E("scalar", _ap)
                bump(s_sc)
                sc_now = tgt(s_sc)
                ve_prev_evac = st8.get("p1evac", 0)

                def _tp(t, i=i, sc_now=sc_now, vp=ve_prev_evac):
                    t.wait_ge(semh[s_sc.name], sc_now)
                    if vp > 0:
                        t.wait_ge(semh[s_ve.name], vp)
                    last = None
                    for c in range(DC):
                        last = t.transpose(ps_t[:, c * 128:(c + 1) * 128],
                                           xhat_tm[:, (i % 2) * D + c * 128:
                                                   (i % 2) * D + (c + 1) * 128],
                                           eye[:, :])
                    inc(last, s_mm)
                E("tensor", _tp)
                bump(s_mm)
                st8[("xtm_mm", i % 2)] = tgt(s_mm)
                wait("vector", s_mm)

                def _ev(v, i=i):
                    dst = fmview(xhat_fm[:, :], DC, L, i * 128, 128)
                    inc(v.tensor_copy(dst, ps_t[:, :].rearrange(
                        "p (c n) -> p c n", n=128)), s_ve)
                E("vector", _ev)
                bump(s_ve)
                st8["p1evac"] = tgt(s_ve)
            P1_EVAC_ALL = tgt(s_ve)
            P1_APPLY_ALL = tgt(s_sc)

            if b == 0 and ("stat1" in dbg or "xraw" in dbg):
                wait("gpsimd", s_ve)
                wait("gpsimd", s_sc)

                def _dst(sy):
                    if "stat1" in dbg:
                        inc(sy.dma_start(out=dbg["stat1"][:, :], in_=stat[:, :]),
                            s_gb, 16)
                    if "xraw" in dbg:
                        inc(sy.dma_start(out=dbg["xraw"][:, :], in_=u64f[:, :]),
                            s_gb, 16)
                E("gpsimd", _dst)
                if "stat1" in dbg:
                    bump(s_gb, 16)
                if "xraw" in dbg:
                    bump(s_gb, 16)
            ckpt(f"p1_{b}")
            if b == 0 and "xhatfm" in dbg:
                wait("gpsimd", s_ve)

                def _d1(sy):
                    inc(sy.dma_start(out=dbg["xhatfm"][:, :], in_=xhat_fm[:, :]),
                        s_gb, 16)
                E("gpsimd", _d1)
                bump(s_gb, 16)

            for bi in range(NBLK):
                col0 = bi * BW
                wait("sync", s_mm)

                def _wblk(sy, col0=col0, cvw=CV["wv"]):
                    sy.wait_ge(semh[s_wg.name], cvw)
                    for wsrc, off in ((wqB, WQS), (wkB, WKS), (wvB, WVS)):
                        inc(sy.dma_start(
                            out=u74[:, off:off + DC * BW],
                            in_=bass.AP(wsrc.tensor, col0,
                                        [[D, 128], [128 * D, DC], [1, BW]])),
                            s_wl, 16)
                E("sync", _wblk)
                bump(s_wl, 48)
                wait("tensor", s_wl)
                wait("tensor", s_ve, P1_EVAC_ALL)
                wait("tensor", s_sc)

                for it in range(TT):
                    pq, pk, pv = ps[3 * (it % 2)], ps[3 * (it % 2) + 1], ps[3 * (it % 2) + 2]
                    vp = st8.get(("projve", it % 2), 0)

                    def _proj(t, it=it, pq=pq, pk=pk, pv=pv, vp=vp):
                        if vp > 0:
                            t.wait_ge(semh[s_ve.name], vp)
                        last = None
                        for kc in range(DC):
                            lhs = xhat_fm[:, kc * L + it * 128:
                                          kc * L + (it + 1) * 128]
                            t.matmul(pq[:, 0:BW], lhs,
                                     u74[:, WQS + kc * BW:WQS + (kc + 1) * BW],
                                     start=(kc == 0), stop=(kc == DC - 1))
                            t.matmul(pk[:, 0:BW], lhs,
                                     u74[:, WKS + kc * BW:WKS + (kc + 1) * BW],
                                     start=(kc == 0), stop=(kc == DC - 1))
                            last = t.matmul(pv[:, 0:BW], lhs,
                                            u74[:, WVS + kc * BW:WVS + (kc + 1) * BW],
                                            start=(kc == 0), stop=(kc == DC - 1))
                        inc(last, s_mm)
                    E("tensor", _proj)
                    bump(s_mm)
                    wait("vector", s_mm)

                    def _rope(v, it=it, pq=pq, pk=pk, pv=pv):
                        for src, ctab, stab, dst in ((pq, cosq, sinq, ropeQ),
                                                     (pk, cosq, sinq, ropeK)):
                            i0 = src[:, 0:BW].rearrange("p (h k) -> p h k", k=HD)
                            tc = ctab[:, it * HD:(it + 1) * HD]
                            ts_ = stab[:, it * HD:(it + 1) * HD]
                            cbc = bass.AP(tc.tensor, tc.offset,
                                          [list(tc.ap[0]), [0, BW // HD], [1, HD]])
                            sbc0 = bass.AP(ts_.tensor, ts_.offset,
                                           [list(ts_.ap[0]), [0, BW // HD], [1, 32]])
                            sbc1 = bass.AP(ts_.tensor, ts_.offset + 32,
                                           [list(ts_.ap[0]), [0, BW // HD], [1, 32]])
                            v.tensor_tensor(
                                ropeA[:, :].rearrange("p (h k) -> p h k", k=HD),
                                i0, cbc, op=OP.mult)
                            ih = src[:, 0:BW].rearrange("p (h t k) -> p h t k",
                                                        t=2, k=32)
                            rh = ropeB[:, :].rearrange("p (h t k) -> p h t k",
                                                       t=2, k=32)
                            v.tensor_tensor(rh[:, :, 0, :], ih[:, :, 1, :], sbc0,
                                            op=OP.mult)
                            v.tensor_tensor(rh[:, :, 1, :], ih[:, :, 0, :], sbc1,
                                            op=OP.mult)
                            v.tensor_add(dst[:, :], ropeA[:, :], ropeB[:, :])
                        last = v.tensor_copy(
                            u64b[:, VTM + it * BW:VTM + (it + 1) * BW], pv[:, 0:BW])
                        inc(last, s_ve)
                    E("vector", _rope)
                    bump(s_ve)
                    st8[("projve", it % 2)] = tgt(s_ve)
                    wait("tensor", s_ve)

                    vp2 = st8.get("qkevac", 0)

                    def _tpqk(t, vp2=vp2):
                        if vp2 > 0:
                            t.wait_ge(semh[s_ve.name], vp2)
                        last = None
                        for c in range(PPB):
                            t.transpose(ps_t[:, c * 128:(c + 1) * 128],
                                        ropeQ[:, c * 128:(c + 1) * 128], eye[:, :])
                            last = t.transpose(ps_t[:, 256 + c * 128:256 + (c + 1) * 128],
                                               ropeK[:, c * 128:(c + 1) * 128],
                                               eye[:, :])
                        inc(last, s_mm)
                    E("tensor", _tpqk)
                    bump(s_mm)
                    wait("vector", s_mm)

                    def _evqk(v, it=it):
                        dq = fmview(u64b[:, QFM:QFM + PPB * L], PPB, L, it * 128, 128)
                        dk = fmview(u64b[:, KFM:KFM + PPB * L], PPB, L, it * 128, 128)
                        v.tensor_copy(dq, ps_t[:, 0:256].rearrange(
                            "p (c n) -> p c n", n=128))
                        inc(v.tensor_copy(dk, ps_t[:, 256:512].rearrange(
                            "p (c n) -> p c n", n=128)), s_ve)
                    E("vector", _evqk)
                    bump(s_ve)
                    st8["qkevac"] = tgt(s_ve)
                VE_BLOCK_QKV = tgt(s_ve)

                if b == 0 and bi == 0:
                    for nm, off, ln in (("qfm0", QFM, PPB * L), ("kfm0", KFM, PPB * L),
                                        ("vtm0", VTM, TT * BW)):
                        if nm in dbg:
                            wait("gpsimd", s_ve)

                            def _dq(sy, nm=nm, off=off, ln=ln):
                                inc(sy.dma_start(out=dbg[nm][:, :],
                                                 in_=u64b[:, off:off + ln]), s_gb, 16)
                            E("gpsimd", _dq)
                            bump(s_gb, 16)

                ckpt(f"proj_{b}_{bi}")
                for pp in range(PPB):
                    p = bi * PPB + pp
                    hA, hB = 2 * p, 2 * p + 1
                    qof = QFM + pp * L
                    kof = KFM + pp * L

                    wait("sync", s_mm)

                    def _gbl(sy, hA=hA, hB=hB, cvg=CV[("gb", hB)]):
                        sy.wait_ge(semh[s_wg.name], cvg)
                        for hh, half in ((hA, 0), (hB, 1)):
                            inc(sy.dma_start(
                                out=u74[:, GBS + half * 4096:GBS + (half + 1) * 4096],
                                in_=bass.AP(gbB.tensor, hh * 8 * 128 * 512,
                                            [[512, 128], [128 * 512, 8], [1, 512]])),
                                s_gb, 16)
                    E("sync", _gbl)
                    bump(s_gb, 32)
                    GB_THIS = tgt(s_gb)

                    wait("vector", s_ve, VE_BLOCK_QKV)
                    wait("scalar", s_ve, VE_BLOCK_QKV)
                    for mi, of in ((0, qof), (1, kof)):
                        for c in range(4):
                            scp_m = tgt(s_mm)

                            def _msqr(sc, of=of, c=c, scp_m=scp_m):
                                if scp_m > 0:
                                    sc.wait_ge(semh[s_mm.name], scp_m)
                                inc(sc.activation(out=gusb[:, :],
                                                  in_=u64b[:, of + c * 512:
                                                           of + (c + 1) * 512],
                                                  func=AF.Square), s_sc)
                            E("scalar", _msqr)
                            bump(s_sc)
                            wait("tensor", s_sc)
                            vep_m = tgt(s_ve)

                            def _msum(t, mi=mi, c=c, vep_m=vep_m):
                                bank = ps_tf if (c % 2 == 0) else ps[0][:, :]
                                if vep_m > 0:
                                    t.wait_ge(semh[s_ve.name], vep_m)
                                inc(t.matmul(bank[0:1, 0:512], onesT[:, 0:1],
                                             gusb[:, :], start=True, stop=True),
                                    s_mm)
                            E("tensor", _msum)
                            bump(s_mm)
                            wait("vector", s_mm)

                            def _mred(v, mi=mi, c=c):
                                bank = ps_tf if (c % 2 == 0) else ps[0][:, :]
                                inc(v.reduce_max(stat[0:1, 240 + 4 * mi + c:
                                                      241 + 4 * mi + c],
                                                 bank[0:1, 0:512], axis=AX), s_ve)
                            E("vector", _mred)
                            bump(s_ve)

                    def _mfin(v, hA=hA, hB=hB):
                        v.drain()
                        v.reduce_max(mrow[0:1, 0:1], stat[0:1, 240:244], axis=AX)
                        v.reduce_max(mrow[0:1, 1:2], stat[0:1, 244:248], axis=AX)
                        v.drain()
                        v.tensor_tensor(mrow[0:1, 2:3], mrow[0:1, 0:1],
                                        mrow[0:1, 1:2], op=OP.mult)
                        inc(v.tensor_tensor(mrow[0:1, 3:4],
                                            maxgbrow[0:1, hA:hA + 1],
                                            maxgbrow[0:1, hB:hB + 1], op=OP.max),
                            s_ve)
                    E("vector", _mfin)
                    bump(s_ve)
                    wait("scalar", s_ve)

                    def _msq(sc):
                        sc.activation(out=mrow[0:1, 4:5], in_=mrow[0:1, 2:3],
                                      func=AF.Ln, bias=ccol[0:1, 2:3])
                        inc(sc.activation(out=mrow[0:1, 5:6], in_=mrow[0:1, 4:5],
                                          func=AF.Exp, scale=0.5,
                                          bias=ccol[0:1, 3:4]), s_sc)
                    E("scalar", _msq)
                    bump(s_sc)
                    wait("vector", s_sc)

                    def _mneg(v):
                        v.tensor_add(mrow[0:1, 6:7], mrow[0:1, 5:6], mrow[0:1, 3:4])
                        v.drain()
                        v.tensor_scalar(mrow[0:1, 7:8], mrow[0:1, 6:7], -1.0, None,
                                        op0=OP.mult)
                        v.drain()
                        inc(v.tensor_copy(colb[0:1, 2:3], mrow[0:1, 7:8]), s_ve)
                    E("vector", _mneg)
                    bump(s_ve)
                    wait("tensor", s_ve)

                    def _mbc(t):
                        inc(t.matmul(ps_tf[:, 2:3], onesrow[0:1, :], colb[0:1, 2:3],
                                     start=True, stop=True), s_mm)
                    E("tensor", _mbc)
                    bump(s_mm)
                    wait("vector", s_mm)

                    def _mev(v):
                        inc(v.tensor_copy(negmc[:, :], ps_tf[:, 2:3]), s_ve)
                    E("vector", _mev)
                    bump(s_ve)
                    NEGM_VE = tgt(s_ve)


                    def _vch(t, pp=pp):
                        last = None
                        for g4 in range(NG):
                            for jj in range(4):
                                jt = 4 * g4 + jj
                                last = t.matmul(
                                    ps_r[0:33, g4 * 128:(g4 + 1) * 128],
                                    onesT[:, 0:33],
                                    u64b[:, VTM + jt * BW + pp * 128:
                                         VTM + jt * BW + (pp + 1) * 128],
                                    start=(jj == 0), stop=(jj == 3))
                        inc(last, s_mm)
                    E("tensor", _vch)
                    bump(s_mm)

                    wait("vector", s_mm)

                    def _vev(v):
                        v.tensor_copy(vpre[0:33, 128:256], ps_r[0:33, 0:128])
                        for g4 in range(1, NG):
                            v.tensor_copy(vpre[0:33, 0:128],
                                          ps_r[0:33, g4 * 128:(g4 + 1) * 128])
                            v.drain()
                            v.tensor_add(vpre[0:33, (g4 + 1) * 128:(g4 + 2) * 128],
                                         vpre[0:33, g4 * 128:(g4 + 1) * 128],
                                         vpre[0:33, 0:128])
                            v.drain()
                        last = None
                        for g4 in range(NG):
                            last = v.tensor_sub(
                                vtailf[0:33, g4 * 128:(g4 + 1) * 128],
                                vpre[0:33, NG * 128:(NG + 1) * 128],
                                vpre[0:33, (g4 + 1) * 128:(g4 + 2) * 128])
                        inc(last, s_ve)
                    E("vector", _vev)
                    bump(s_ve)

                    for g in range(NG):
                        JT = 4 * g + 4
                        i0 = g * 512

                        def phA(jt):
                            return u64b[:, PHA + jt * 512:PHA + (jt + 1) * 512]

                        def phB(jt):
                            return u64b[:, PHB + jt * 512:PHB + (jt + 1) * 512]

                        exp_base = tgt(s_sc)
                        prev_pv_sc = st8.get("grp_sc_ready", 0)

                        def _st(t, g=g, JT=JT, i0=i0, qof=qof, kof=kof,
                                exp_base=exp_base, gbn=GB_THIS, negve=NEGM_VE):
                            t.wait_ge(semh[s_gb.name], gbn)
                            t.wait_ge(semh[s_ve.name], negve)
                            for jt in range(JT):
                                if jt >= 2:
                                    t.wait_ge(semh[s_sc.name], exp_base + jt - 1)
                                bA = ps[2 * (jt % 2)]
                                bB = ps[2 * (jt % 2) + 1]
                                lA = u64b[0:64, kof + jt * 128:kof + (jt + 1) * 128]
                                lB = u64b[64:128, kof + jt * 128:kof + (jt + 1) * 128]
                                rA = u64b[0:64, qof + i0:qof + i0 + 512]
                                rB = u64b[64:128, qof + i0:qof + i0 + 512]
                                oi = gb_off_index(g, jt)
                                t.matmul(bA[:, :], lA, rA, start=True,
                                         stop=(oi is None), tile_position=(0, 0))
                                lastm = t.matmul(bB[:, :], lB, rB, start=True,
                                                 stop=(oi is None),
                                                 tile_position=(64, 0))
                                if oi is not None:
                                    t.matmul(bA[:, :], eye8[:, :],
                                             u74[:, GBS + oi * 512:
                                                  GBS + (oi + 1) * 512],
                                             start=False, stop=True)
                                    lastm = t.matmul(
                                        bB[:, :], eye8[:, :],
                                        u74[:, GBS + 4096 + oi * 512:
                                             GBS + 4096 + (oi + 1) * 512],
                                        start=False, stop=True)
                                inc(lastm, s_mm)
                                if jt >= 1:
                                    t.wait_ge(semh[s_sc.name], exp_base + jt)
                                    t.matmul(ps[4][0:32, :], onesT[:, 0:32],
                                             phA(jt - 1), start=(jt == 1),
                                             stop=False, tile_position=(0, 0))
                                    t.matmul(ps_r[32:64, :], onesT[:, 0:32],
                                             phB(jt - 1), start=(jt == 1),
                                             stop=False, tile_position=(0, 32))
                            t.wait_ge(semh[s_sc.name], exp_base + JT)
                            t.matmul(ps[4][0:32, :], onesT[:, 0:32], phA(JT - 1),
                                     start=(JT == 1), stop=True,
                                     tile_position=(0, 0))
                            inc(t.matmul(ps_r[32:64, :], onesT[:, 0:32], phB(JT - 1),
                                         start=(JT == 1), stop=True,
                                         tile_position=(0, 32)), s_mm)
                        E("tensor", _st)
                        bump(s_mm, JT + 1)
                        mm_st0 = tgt(s_mm) - JT - 1

                        def _exp(sc, JT=JT, mm0=mm_st0, scr=prev_pv_sc):
                            if scr > 0:
                                sc.wait_ge(semh[s_mm.name], scr)
                            for jt in range(JT):
                                sc.wait_ge(semh[s_mm.name], mm0 + jt + 1)
                                bA = ps[2 * (jt % 2)]
                                bB = ps[2 * (jt % 2) + 1]
                                sc.activation(out=phA(jt), in_=bA[:, :], func=AF.Exp,
                                              scale=QK_SCALE, bias=negmc[:, 0:1])
                                inc(sc.activation(out=phB(jt), in_=bB[:, :],
                                                  func=AF.Exp, scale=QK_SCALE,
                                                  bias=negmc[:, 0:1]), s_sc)
                        E("scalar", _exp)
                        bump(s_sc, JT)

                        wait("vector", s_mm)

                        def _rows1(v):
                            v.tensor_copy(serow[0:32, :], ps[4][0:32, :])
                            inc(v.tensor_copy(serow[32:64, :], ps_r[32:64, :]), s_ve)
                        E("vector", _rows1)
                        bump(s_ve)
                        wait("scalar", s_ve)

                        def _rows2(sc):
                            sc.activation(out=lnsc[0:64, 0:512], in_=serow[:, :],
                                          func=AF.Ln)
                            inc(sc.activation(out=lnsc[0:64, 0:512],
                                              in_=lnsc[0:64, 0:512],
                                              func=AF.Exp, scale=-1.0), s_sc)
                        E("scalar", _rows2)
                        bump(s_sc)
                        wait("vector", s_sc)

                        def _rows3(v, p=p, i0=i0):
                            v.tensor_scalar(csrow[:, :], recipbc[:, i0:i0 + 512],
                                            taupair[:, p:p + 1], None, op0=OP.mult)
                            v.tensor_relu(lnsc[0:64, 512:1024], csrow[:, :])
                            v.tensor_tensor(csrow[:, :], csrow[:, :], serow[:, :],
                                            op=OP.mult)
                            v.tensor_tensor(lnsc[0:64, 512:1024],
                                            lnsc[0:64, 512:1024], serow[:, :],
                                            op=OP.mult)
                            v.tensor_copy(rowb[0:1, 0:512], csrow[0:1, :])
                            v.tensor_copy(rowb[32:33, 0:512], csrow[32:33, :])
                            v.tensor_copy(rowb[0:1, 512:1024], lnsc[0:1, 512:1024])
                            v.tensor_copy(rowb[32:33, 512:1024],
                                          lnsc[32:33, 512:1024])
                            v.tensor_copy(rowb[0:1, 1024:1536], lnsc[0:1, 0:512])
                            v.tensor_copy(rowb[32:33, 1024:1536],
                                          lnsc[32:33, 0:512])
                            v.tensor_copy(csrow[0:1, :], rowb[0:1, 0:512])
                            v.tensor_copy(csrow[32:33, :], rowb[32:33, 0:512])
                            v.tensor_copy(lnsc[0:1, 512:1024], rowb[0:1, 512:1024])
                            inc(v.tensor_copy(lnsc[32:33, 512:1024],
                                              rowb[32:33, 512:1024]), s_ve)
                        E("vector", _rows3)
                        bump(s_ve)
                        wait("tensor", s_ve)

                        def _ct1(t):
                            t.matmul(ps_tf[:, :], negrow[0:1, :], rowb[0:1, 0:512],
                                     start=True, stop=True)
                            inc(t.matmul(ps[0][:, :], negrow[32:33, :],
                                         rowb[32:33, 0:512], start=True, stop=True),
                                s_mm)
                        E("tensor", _ct1)
                        bump(s_mm)
                        wait("vector", s_mm)

                        def _ce1(v):
                            v.tensor_copy(u74[:, CTA:CTA + 512], ps_tf[:, :])
                            inc(v.tensor_copy(u74[:, CTB:CTB + 512], ps[0][:, :]),
                                s_ve)
                        E("vector", _ce1)
                        bump(s_ve)

                        def _max(v, JT=JT):
                            last = None
                            for jt in range(JT):
                                v.tensor_tensor(phA(jt), phA(jt),
                                                u74[:, CTA:CTA + 512], op=OP.max)
                                last = v.tensor_tensor(phB(jt), phB(jt),
                                                       u74[:, CTB:CTB + 512],
                                                       op=OP.max)
                            inc(last, s_ve)
                        E("vector", _max)
                        bump(s_ve)
                        wait("tensor", s_ve)

                        def _pv(t, JT=JT, pp=pp, g=g):
                            for jt in range(JT):
                                vA = u64b[:, VTM + jt * BW + pp * 128:
                                          VTM + jt * BW + pp * 128 + 64]
                                vB = u64b[:, VTM + jt * BW + pp * 128 + 64:
                                          VTM + jt * BW + (pp + 1) * 128]
                                t.matmul(ps[4][0:64, :], vA, phA(jt),
                                         start=(jt == 0), stop=False,
                                         tile_position=(0, 0))
                                t.matmul(ps_r[64:128, :], vB, phB(jt),
                                         start=(jt == 0), stop=False,
                                         tile_position=(0, 64))
                            t.matmul(ps[4][0:64, :],
                                     vpre[0:1, (g + 1) * 128:(g + 1) * 128 + 64],
                                     csrow[0:1, :], start=False, stop=False,
                                     tile_position=(0, 0))
                            t.matmul(ps_r[64:128, :],
                                     vpre[32:33, (g + 1) * 128 + 64:(g + 2) * 128],
                                     csrow[32:33, :], start=False, stop=False,
                                     tile_position=(32, 64))
                            t.matmul(ps[4][0:64, :],
                                     vtailf[0:1, g * 128:g * 128 + 64],
                                     lnsc[0:1, 512:1024], start=False, stop=True,
                                     tile_position=(0, 0))
                            t.matmul(ps_r[64:128, :],
                                     vtailf[32:33, g * 128 + 64:(g + 1) * 128],
                                     lnsc[32:33, 512:1024], start=False, stop=True,
                                     tile_position=(32, 64))
                            t.matmul(ps_tf[0:64, :], onesrow[0:1, 0:64],
                                     rowb[0:1, 1024:1536], start=True, stop=True,
                                     tile_position=(0, 0))
                            inc(t.matmul(ps_tf[64:128, :], onesrow[32:33, 0:64],
                                         rowb[32:33, 1024:1536], start=True,
                                         stop=True, tile_position=(32, 64)), s_mm)
                        E("tensor", _pv)
                        bump(s_mm)
                        st8["grp_sc_ready"] = tgt(s_mm)
                        wait("vector", s_mm)

                        def _oev(v, p=p, i0=i0):
                            v.tensor_copy(lnsc[:, 0:512], ps_tf[:, 0:512])
                            v.tensor_tensor(
                                u74[:, AFM + p * L + i0:AFM + p * L + i0 + 512]
                                .rearrange("p n -> p n") if False else
                                u74[0:64, AFM + p * L + i0:AFM + p * L + i0 + 512],
                                ps[4][0:64, :], lnsc[0:64, 0:512], op=OP.mult)
                            inc(v.tensor_tensor(
                                u74[64:128, AFM + p * L + i0:AFM + p * L + i0 + 512],
                                ps_r[64:128, :], lnsc[64:128, 0:512], op=OP.mult),
                                s_ve)
                        E("vector", _oev)
                        bump(s_ve)
                        ckpt(f"grp_{b}_{p}_{g}")

                        if b == 0 and p == 0 and g == 0:
                            if "serow00" in dbg:
                                wait("gpsimd", s_ve)

                                def _ds(sy):
                                    inc(sy.dma_start(out=dbg["serow00"][:, :],
                                                     in_=serow[:, :]), s_gb, 16)
                                E("gpsimd", _ds)
                                bump(s_gb, 16)
                            if "phat00" in dbg:
                                wait("gpsimd", s_sc, tgt(s_sc))

                                def _dp(sy):
                                    inc(sy.dma_start(out=dbg["phat00"][:, :],
                                                     in_=u64b[:, PHA:PHA + 8192]),
                                        s_gb, 16)
                                E("gpsimd", _dp)
                                bump(s_gb, 16)
                    ckpt(f"pair_{b}_{p}")
            ATTN_VE_DONE = tgt(s_ve)
            ATTN_MM_DONE = tgt(s_mm)

            if b == 0 and "attnfm" in dbg:
                wait("gpsimd", s_ve)

                def _da(gp):
                    inc(gp.dma_start(out=dbg["attnfm"][:, :],
                                     in_=u74[:, AFM:AFM + DC * L]), s_wg, 16)
                E("gpsimd", _da)
                bump(s_wg, 16)

            ckpt(f"attn_{b}")
            wait("sync", s_mm)

            def _wo(sy, cvw=CV["wu"]):
                sy.wait_ge(semh[s_wg.name], cvw)
                inc(sy.dma_start(
                    out=u74[:, WOS:WOS + DC * D],
                    in_=bass.AP(woB.tensor, 0, [[D, 128], [128 * D, DC], [1, D]])),
                    s_wl, 16)
            E("sync", _wo)
            bump(s_wl, 16)
            ckpt(f"p3wo_{b}")
            wait("tensor", s_wl)
            wait("tensor", s_ve, ATTN_VE_DONE)
            wait("sync", s_sc)
            wait("sync", s_ve)
            for it in range(TT):
                vp = st8.get(("x2ve", it % 2), 0)
                psem = s_x2 if it % 2 == 0 else s_x3

                def _lx(sy, b=b, it=it, vp=vp, psem=psem):
                    if vp > 0:
                        sy.wait_ge(semh[s_ve.name], vp)
                    inc(sy.dma_start(out=xio[:, (it % 2) * D:(it % 2 + 1) * D],
                                     in_=xin[b, it * 128:(it + 1) * 128, :]),
                        psem, 16)
                E("sync", _lx)
                bump(psem, 16)
                XLT = tgt(psem)

                scp = st8.get(("wosc", it % 2), 0)

                def _wop(t, it=it, scp=scp):
                    if scp > 0:
                        t.wait_ge(semh[s_sc.name], scp)
                    b0 = ps[2 * (it % 2)]
                    b1 = ps[2 * (it % 2) + 1]
                    last = None
                    for kc in range(DC):
                        lhs = u74[:, AFM + kc * L + it * 128:
                                  AFM + kc * L + (it + 1) * 128]
                        t.matmul(b0[:, :], lhs,
                                 u74[:, WOS + kc * D:WOS + kc * D + 512],
                                 start=(kc == 0), stop=(kc == DC - 1))
                        last = t.matmul(b1[:, :], lhs,
                                        u74[:, WOS + kc * D + 512:WOS + (kc + 1) * D],
                                        start=(kc == 0), stop=(kc == DC - 1))
                    inc(last, s_mm)
                E("tensor", _wop)
                bump(s_mm)
                ckpt(f"p3mm_{b}_{it}")
                wait("scalar", s_mm)
                wait("scalar", s_ve)
                wait("vector", s_mm)
                cc = 192 + (it % 2) * 24

                def _2sq(sc, it=it, cc=cc):
                    b0 = ps[2 * (it % 2)]
                    b1 = ps[2 * (it % 2) + 1]
                    sc.activation(out=lnsc[:, 0:512], in_=b0[:, :], func=AF.Square,
                                  accum_out=stat[:, cc + 2:cc + 3])
                    inc(sc.activation(out=lnsc[:, 0:512], in_=b1[:, :],
                                      func=AF.Square,
                                      accum_out=stat[:, cc + 3:cc + 4]), s_sc)
                E("scalar", _2sq)
                bump(s_sc)
                wait("vector", s_sc)

                def _2ns(v, it=it, cc=cc):
                    b0 = ps[2 * (it % 2)]
                    b1 = ps[2 * (it % 2) + 1]
                    v.reduce_sum(stat[:, cc + 4:cc + 5], b0[:, :], axis=AX,
                                 negate=True)
                    inc(v.reduce_sum(stat[:, cc + 5:cc + 6], b1[:, :], axis=AX,
                                     negate=True), s_ve)
                E("vector", _2ns)
                bump(s_ve)
                ckpt(f"p3s1_{b}_{it}")
                wait("vector", s_sc)

                def _2var(v, cc=cc):
                    v.tensor_add(stat[:, cc + 6:cc + 7], stat[:, cc + 2:cc + 3],
                                 stat[:, cc + 3:cc + 4])
                    v.tensor_add(stat[:, cc + 7:cc + 8], stat[:, cc + 4:cc + 5],
                                 stat[:, cc + 5:cc + 6])
                    v.drain()
                    v.tensor_scalar(stat[:, cc + 8:cc + 9], stat[:, cc + 7:cc + 8],
                                    1.0 / D, None, op0=OP.mult)
                    v.tensor_scalar(stat[:, cc + 9:cc + 10], stat[:, cc + 6:cc + 7],
                                    1.0 / D, None, op0=OP.mult)
                    v.drain()
                    v.tensor_tensor(stat[:, cc + 10:cc + 11], stat[:, cc + 8:cc + 9],
                                    stat[:, cc + 8:cc + 9], op=OP.mult)
                    v.drain()
                    inc(v.tensor_sub(stat[:, cc + 11:cc + 12],
                                     stat[:, cc + 9:cc + 10],
                                     stat[:, cc + 10:cc + 11]), s_ve)
                E("vector", _2var)
                bump(s_ve)
                ckpt(f"p3s2_{b}_{it}")
                wait("scalar", s_ve)

                def _2rs(sc, cc=cc):
                    sc.activation(out=stat[:, cc + 12:cc + 13],
                                  in_=stat[:, cc + 11:cc + 12], func=AF.Ln,
                                  bias=ccol[:, 0:1])
                    inc(sc.activation(out=stat[:, cc + 0:cc + 1],
                                      in_=stat[:, cc + 12:cc + 13], func=AF.Exp,
                                      scale=-0.5), s_sc)
                E("scalar", _2rs)
                bump(s_sc)
                ckpt(f"p3s3_{b}_{it}")
                wait("vector", s_sc)

                def _2b(v, cc=cc):
                    inc(v.tensor_tensor(stat[:, cc + 1:cc + 2], stat[:, cc + 8:cc + 9],
                                        stat[:, cc + 0:cc + 1], op=OP.mult), s_ve)
                E("vector", _2b)
                bump(s_ve)
                wait("scalar", s_ve)

                def _2ap(sc, it=it, cc=cc):
                    b0 = ps[2 * (it % 2)]
                    b1 = ps[2 * (it % 2) + 1]
                    sc.activation(out=lnsc[:, 0:512], in_=b0[:, :], func=AF.Identity,
                                  scale=stat[:, cc + 0:cc + 1],
                                  bias=stat[:, cc + 1:cc + 2])
                    inc(sc.activation(out=lnsc[:, 512:1024], in_=b1[:, :],
                                      func=AF.Identity,
                                      scale=stat[:, cc + 0:cc + 1],
                                      bias=stat[:, cc + 1:cc + 2]), s_sc)
                E("scalar", _2ap)
                bump(s_sc)
                st8[("wosc", it % 2)] = tgt(s_sc)
                ckpt(f"p3ap_{b}_{it}")
                wait("vector", s_sc)
                wait("vector", psem, XLT)
                stp = st8.get(("x2st", it % 2), 0)
                if stp > 0:
                    wait("vector", s_st, stp)

                def _x2(v, it=it):
                    inc(v.tensor_add(outsb[:, (it % 2) * D:(it % 2 + 1) * D],
                                     xio[:, (it % 2) * D:(it % 2 + 1) * D],
                                     lnsc[:, 0:D]), s_ve)
                E("vector", _x2)
                bump(s_ve)
                st8[("x2ve", it % 2)] = tgt(s_ve)
                ckpt(f"p3x2_{b}_{it}")
                c3 = it * 12
                wait("scalar", s_ve)

                def _3sq(sc, it=it, c3=c3):
                    inc(sc.activation(out=lnsc[:, 0:D],
                                      in_=outsb[:, (it % 2) * D:(it % 2 + 1) * D],
                                      func=AF.Square,
                                      accum_out=stat[:, c3 + 2:c3 + 3]), s_sc)
                E("scalar", _3sq)
                bump(s_sc)

                def _3ns(v, it=it, c3=c3):
                    inc(v.reduce_sum(stat[:, c3 + 3:c3 + 4],
                                     outsb[:, (it % 2) * D:(it % 2 + 1) * D],
                                     axis=AX, negate=True), s_ve)
                E("vector", _3ns)
                bump(s_ve)
                wait("sync", s_ve)

                def _sx2(sy, it=it):
                    inc(sy.dma_start(out=x2d[it * 128:(it + 1) * 128, :],
                                     in_=outsb[:, (it % 2) * D:(it % 2 + 1) * D]),
                        s_st, 16)
                E("sync", _sx2)
                bump(s_st, 16)
                st8[("x2st", it % 2)] = tgt(s_st)
                ckpt(f"p3t_{b}_{it}")
            wait("vector", s_sc)
            ln_tail(0, range(TT), D, EPS)

            if b == 0 and "x2" in dbg:
                wait("gpsimd", s_st)

                def _dx2(sy):
                    last = None
                    for it in range(TT):
                        last = sy.dma_start(
                            out=dbg["x2"][it * 128:(it + 1) * 128, :],
                            in_=x2d[it * 128:(it + 1) * 128, :])
                        inc(last, s_gb, 16)
                E("sync", _dx2)
                bump(s_gb, 16 * TT)

            ckpt(f"p3_{b}")
            wait("sync", s_mm)

            def _wdl(sy, cvw=CV["wu"]):
                sy.wait_ge(semh[s_wg.name], cvw)
                inc(sy.dma_start(
                    out=u64b[:, WD0:WD0 + 24 * D],
                    in_=bass.AP(wdB.tensor, 0, [[D, 128], [128 * D, 24], [1, D]])),
                    s_wl, 16)
            E("sync", _wdl)
            bump(s_wl, 16)
            WL_WD = tgt(s_wl)
            wait("sync", s_st)

            for ic in range(NG):
                for k in range(4):
                    it = 4 * ic + k
                    scp = st8.get(("l3sc", k % 2), 0)
                    mmp3 = st8.get(("xtm_mm", it % 2), 0)
                    vp3 = st8.get(("x3ve", it % 2), 0)

                    psem = s_x2 if it % 2 == 0 else s_x3

                    def _lx3(sy, it=it, scp=scp, vp3=vp3, psem=psem):
                        if scp > 0:
                            sy.wait_ge(semh[s_sc.name], scp)
                        if vp3 > 0:
                            sy.wait_ge(semh[s_ve.name], vp3)
                        inc(sy.dma_start(out=xio[:, (it % 2) * D:(it % 2 + 1) * D],
                                         in_=x2d[it * 128:(it + 1) * 128, :]),
                            psem, 16)
                    E("sync", _lx3)
                    bump(psem, 16)
                    wait("scalar", psem)

                    def _3ap(sc, it=it, mmp3=mmp3):
                        if mmp3 > 0:
                            sc.wait_ge(semh[s_mm.name], mmp3)
                        c = it * 12
                        inc(sc.activation(
                            out=xhat_tm[:, (it % 2) * D:(it % 2 + 1) * D],
                            in_=xio[:, (it % 2) * D:(it % 2 + 1) * D],
                            func=AF.Identity, scale=stat[:, c + 0:c + 1],
                            bias=stat[:, c + 1:c + 2]), s_sc)
                    E("scalar", _3ap)
                    bump(s_sc)
                    st8[("l3sc", k % 2)] = tgt(s_sc)
                    sc_now = tgt(s_sc)
                    vp = st8.get("p1evac", 0)

                    def _tp3(t, it=it, sc_now=sc_now, vp=vp):
                        t.wait_ge(semh[s_sc.name], sc_now)
                        if vp > 0:
                            t.wait_ge(semh[s_ve.name], vp)
                        last = None
                        for c in range(DC):
                            last = t.transpose(ps_t[:, c * 128:(c + 1) * 128],
                                               xhat_tm[:, (it % 2) * D + c * 128:
                                                       (it % 2) * D + (c + 1) * 128],
                                               eye[:, :])
                        inc(last, s_mm)
                    E("tensor", _tp3)
                    bump(s_mm)
                    st8[("xtm_mm", it % 2)] = tgt(s_mm)
                    wait("vector", s_mm)

                    def _ev3(v, it=it):
                        dst = fmview(xhat_fm[:, :], DC, L, it * 128, 128)
                        inc(v.tensor_copy(dst, ps_t[:, :].rearrange(
                            "p (c n) -> p c n", n=128)), s_ve)
                    E("vector", _ev3)
                    bump(s_ve)
                    st8["p1evac"] = tgt(s_ve)

                wait("tensor", s_ve)
                for hc in range(24):
                    buf = hc % 2
                    mmp = st8.get(("gumm", buf), 0)

                    wsem = s_wh if hc % 2 == 0 else s_w2

                    def _wgl(sy, hc=hc, buf=buf, mmp=mmp, cvw=CV["wu"],
                             wsem=wsem):
                        sy.wait_ge(semh[s_wg.name], cvw)
                        if mmp > 0:
                            sy.wait_ge(semh[s_mm.name], mmp)
                        inc(sy.dma_start(
                            out=u64b[:, WG0 + buf * 1024:WG0 + (buf + 1) * 1024],
                            in_=bass.AP(wgB.tensor, hc * 128,
                                        [[HID, 128], [128 * HID, DC], [1, 128]])),
                            wsem, 16)
                        inc(sy.dma_start(
                            out=u64b[:, WU0 + buf * 1024:WU0 + (buf + 1) * 1024],
                            in_=bass.AP(wuB.tensor, hc * 128,
                                        [[HID, 128], [128 * HID, DC], [1, 128]])),
                            wsem, 16)
                    E("scalar", _wgl)
                    bump(wsem, 32)
                    wait("tensor", wsem)
                    vep = st8.get(("guve", hc % 2), 0)

                    def _gu(t, ic=ic, hc=hc, buf=buf, vep=vep):
                        bg = ps[2 * (hc % 2)]
                        bu = ps[2 * (hc % 2) + 1]
                        if vep > 0:
                            t.wait_ge(semh[s_ve.name], vep)
                        last = None
                        for kc in range(DC):
                            rhs = xhat_fm[:, kc * L + ic * 512:kc * L + (ic + 1) * 512]
                            t.matmul(bg[:, :],
                                     u64b[:, WG0 + buf * 1024 + kc * 128:
                                          WG0 + buf * 1024 + (kc + 1) * 128],
                                     rhs, start=(kc == 0), stop=(kc == DC - 1))
                            last = t.matmul(bu[:, :],
                                            u64b[:, WU0 + buf * 1024 + kc * 128:
                                                 WU0 + buf * 1024 + (kc + 1) * 128],
                                            rhs, start=(kc == 0), stop=(kc == DC - 1))
                        inc(last, s_mm)
                    E("tensor", _gu)
                    bump(s_mm)
                    st8[("gumm", buf)] = tgt(s_mm)
                    wait("scalar", s_mm)

                    hmp = st8.get("hmve", 0)

                    def _si(sc, hc=hc, hmp=hmp):
                        if hmp > 0:
                            sc.wait_ge(semh[s_ve.name], hmp)
                        bg = ps[2 * (hc % 2)]
                        inc(sc.activation(out=gusb[:, :], in_=bg[:, :], func=AF.Silu,
                                          scale=1.0 / 127.0), s_sc)
                    E("scalar", _si)
                    bump(s_sc)
                    wait("vector", s_sc)

                    def _hm(v, hc=hc):
                        bu = ps[2 * (hc % 2) + 1]
                        inc(v.tensor_tensor(
                            u74[:, HFM + hc * 512:HFM + (hc + 1) * 512],
                            bu[:, :], gusb[:, :], op=OP.mult), s_ve)
                    E("vector", _hm)
                    bump(s_ve)
                    st8[("guve", hc % 2)] = tgt(s_ve)
                    st8["hmve"] = tgt(s_ve)

                if b == 0 and ic == 1 and "hfm1" in dbg:
                    wait("gpsimd", s_ve)

                    def _dh(sy):
                        inc(sy.dma_start(out=dbg["hfm1"][:, :],
                                         in_=u74[:, HFM:HFM + 24 * 512]), s_gb, 16)
                    E("gpsimd", _dh)
                    bump(s_gb, 16)
                wait("tensor", s_wl, WL_WD)
                for k in range(4):
                    it = 4 * ic + k
                    scp = st8.get("dnsc", 0)

                    def _dn(t, k=k, scp=scp):
                        if scp > 0:
                            t.wait_ge(semh[s_sc.name], scp)
                        last = None
                        for hc in range(24):
                            lhs = u74[:, HFM + hc * 512 + k * 128:
                                      HFM + hc * 512 + (k + 1) * 128]
                            t.matmul(ps[4][:, :], lhs,
                                     u64b[:, WD0 + hc * D:WD0 + hc * D + 512],
                                     start=(hc == 0), stop=(hc == 23))
                            last = t.matmul(ps[5][:, :], lhs,
                                            u64b[:, WD0 + hc * D + 512:
                                                 WD0 + (hc + 1) * D],
                                            start=(hc == 0), stop=(hc == 23))
                        inc(last, s_mm)
                    E("tensor", _dn)
                    bump(s_mm)
                    wait("scalar", s_mm)
                    wait("scalar", s_ve)
                    wait("vector", s_mm)
                    cc = 192 + (k % 2) * 24

                    def _4sq(sc, cc=cc):
                        sc.activation(out=lnsc[:, 0:512], in_=ps[4][:, :],
                                      func=AF.Square,
                                      accum_out=stat[:, cc + 2:cc + 3])
                        inc(sc.activation(out=lnsc[:, 0:512], in_=ps[5][:, :],
                                          func=AF.Square,
                                          accum_out=stat[:, cc + 3:cc + 4]), s_sc)
                    E("scalar", _4sq)
                    bump(s_sc)
                    wait("vector", s_sc)

                    def _4ns(v, cc=cc):
                        v.reduce_sum(stat[:, cc + 4:cc + 5], ps[4][:, :], axis=AX,
                                     negate=True)
                        inc(v.reduce_sum(stat[:, cc + 5:cc + 6], ps[5][:, :],
                                         axis=AX, negate=True), s_ve)
                    E("vector", _4ns)
                    bump(s_ve)
                    wait("vector", s_sc)

                    def _4var(v, cc=cc):
                        v.tensor_add(stat[:, cc + 6:cc + 7], stat[:, cc + 2:cc + 3],
                                     stat[:, cc + 3:cc + 4])
                        v.tensor_add(stat[:, cc + 7:cc + 8], stat[:, cc + 4:cc + 5],
                                     stat[:, cc + 5:cc + 6])
                        v.drain()
                        v.tensor_scalar(stat[:, cc + 8:cc + 9],
                                        stat[:, cc + 7:cc + 8], 1.0 / D, None,
                                        op0=OP.mult)
                        v.tensor_scalar(stat[:, cc + 9:cc + 10],
                                        stat[:, cc + 6:cc + 7], 1.0 / D, None,
                                        op0=OP.mult)
                        v.drain()
                        v.tensor_tensor(stat[:, cc + 10:cc + 11],
                                        stat[:, cc + 8:cc + 9],
                                        stat[:, cc + 8:cc + 9], op=OP.mult)
                        v.drain()
                        inc(v.tensor_sub(stat[:, cc + 11:cc + 12],
                                         stat[:, cc + 9:cc + 10],
                                         stat[:, cc + 10:cc + 11]), s_ve)
                    E("vector", _4var)
                    bump(s_ve)
                    wait("scalar", s_ve)

                    def _4rs(sc, cc=cc):
                        sc.activation(out=stat[:, cc + 12:cc + 13],
                                      in_=stat[:, cc + 11:cc + 12], func=AF.Ln,
                                      bias=ccol[:, 1:2])
                        inc(sc.activation(out=stat[:, cc + 0:cc + 1],
                                          in_=stat[:, cc + 12:cc + 13], func=AF.Exp,
                                          scale=-0.5), s_sc)
                    E("scalar", _4rs)
                    bump(s_sc)
                    wait("vector", s_sc)

                    def _4b(v, cc=cc):
                        inc(v.tensor_tensor(stat[:, cc + 1:cc + 2],
                                            stat[:, cc + 8:cc + 9],
                                            stat[:, cc + 0:cc + 1], op=OP.mult),
                            s_ve)
                    E("vector", _4b)
                    bump(s_ve)
                    wait("scalar", s_ve)

                    def _4ap(sc, cc=cc):
                        sc.activation(out=lnsc[:, 0:512], in_=ps[4][:, :],
                                      func=AF.Identity, scale=stat[:, cc + 0:cc + 1],
                                      bias=stat[:, cc + 1:cc + 2])
                        inc(sc.activation(out=lnsc[:, 512:1024], in_=ps[5][:, :],
                                          func=AF.Identity,
                                          scale=stat[:, cc + 0:cc + 1],
                                          bias=stat[:, cc + 1:cc + 2]), s_sc)
                    E("scalar", _4ap)
                    bump(s_sc)
                    st8["dnsc"] = tgt(s_sc)
                    if b == 0 and ic == 1 and "ffn_d" in dbg:
                        wait("gpsimd", s_sc)

                        def _df(sy, k=k):
                            inc(sy.dma_start(out=dbg["ffn_d"][k, :, :],
                                             in_=lnsc[:, 0:D]), s_gb, 16)
                        E("gpsimd", _df)
                        bump(s_gb, 16)
                        wait("vector", s_gb)

                    vp4 = st8.get(("x3ve", it % 2), 0)
                    scp4 = st8.get(("l3sc", it % 2), 0)

                    psem = s_x2 if it % 2 == 0 else s_x3

                    def _lx4(sy, it=it, vp4=vp4, scp4=scp4, psem=psem):
                        if vp4 > 0:
                            sy.wait_ge(semh[s_ve.name], vp4)
                        if scp4 > 0:
                            sy.wait_ge(semh[s_sc.name], scp4)
                        inc(sy.dma_start(out=xio[:, (it % 2) * D:(it % 2 + 1) * D],
                                         in_=x2d[it * 128:(it + 1) * 128, :]),
                            psem, 16)
                    E("sync", _lx4)
                    bump(psem, 16)
                    wait("vector", psem)
                    wait("vector", s_sc)
                    stp = st8.get(("x3st", it % 2), 0)
                    if stp > 0:
                        wait("vector", s_st, stp)

                    def _x3(v, it=it):
                        inc(v.tensor_add(outsb[:, (it % 2) * D:(it % 2 + 1) * D],
                                         xio[:, (it % 2) * D:(it % 2 + 1) * D],
                                         lnsc[:, 0:D]), s_ve)
                    E("vector", _x3)
                    bump(s_ve)
                    st8[("x3ve", it % 2)] = tgt(s_ve)
                    wait("sync", s_ve)

                    def _so(sy, b=b, it=it):
                        inc(sy.dma_start(out=out[b, it * 128:(it + 1) * 128, :],
                                         in_=outsb[:, (it % 2) * D:(it % 2 + 1) * D]),
                            s_st, 16)
                    E("sync", _so)
                    bump(s_st, 16)
                    st8[("x3st", it % 2)] = tgt(s_st)
                ckpt(f"ffn_{b}_{ic}")

        wait("sync", s_st)

        block = stack.enter_context(nc.Block())

        def run(eng_name):
            def body(e):
                for fn in prog[eng_name]:
                    fn(e)
            return body
        block.sync(run("sync"))
        block.gpsimd(run("gpsimd"))
        block.tensor(run("tensor"))
        block.scalar(run("scalar"))
        block.vector(run("vector"))

    return nc


def shard_inputs(L0_attn_x, L0_attn_Wq, L0_attn_Wk, L0_attn_Wv, L0_attn_Wo,
                 L0_attn_biases, L0_attn_tau, L0_ffn_w_gate, L0_ffn_w_up,
                 L0_ffn_w_down):
    shared = {
        "wqT": np.ascontiguousarray(np.asarray(L0_attn_Wq).T),
        "wkT": np.ascontiguousarray(np.asarray(L0_attn_Wk).T),
        "wvT": np.ascontiguousarray(np.asarray(L0_attn_Wv).T),
        "woT": np.ascontiguousarray(np.asarray(L0_attn_Wo).T),
        "wgT": np.ascontiguousarray(np.asarray(L0_ffn_w_gate).T),
        "wuT": np.ascontiguousarray(np.asarray(L0_ffn_w_up).T),
        "wdT": np.ascontiguousarray(np.asarray(L0_ffn_w_down).T),
        "gbt": host_gb_tiles(np.asarray(L0_attn_biases)),
        "biasd": np.ascontiguousarray(np.asarray(L0_attn_biases, dtype=np.float32)),
        "taud": np.ascontiguousarray(
            np.asarray(L0_attn_tau, dtype=np.float32)[None, :]),
    }
    shared.update(host_constants())
    x = np.ascontiguousarray(np.asarray(L0_attn_x, dtype=np.float32))
    return [dict(shared, xin=x[2 * c:2 * c + 2]) for c in range(8)]


def kernel(**inputs):
    maps = shard_inputs(**inputs)
    nc = build_program()
    res = run_bass_kernel_spmd(nc, maps, core_ids=list(range(8)))
    return np.concatenate([r["out"] for r in res.results], axis=0)


if __name__ == "__main__":
    nc = build_program()
    print("built ok,", nc.n_instructions() if hasattr(nc, "n_instructions") else "")


# version 07 (final)
# speedup vs baseline: 1.1871x; 1.1871x over previous
import contextlib
import os
import numpy as np
import ml_dtypes

import concourse.bass as bass
import concourse.mybir as mybir
from concourse.bass_utils import run_bass_kernel_spmd

F32 = mybir.dt.float32
BF16 = mybir.dt.bfloat16
I32 = mybir.dt.int32
AX = mybir.AxisListType.X
OP = mybir.AluOpType
AF = mybir.ActivationFunctionType
bf16 = ml_dtypes.bfloat16

B, L, D, H, HD, HID = 2, 2048, 1024, 16, 64, 3072
TT, DC, NG, NP = 16, 8, 4, 8
NBLK, PPB = 4, 2
BW = PPB * 128
EPS = 1e-5
EPS4 = 1e-5 * float(127 ** 4)
QK_SCALE = 0.125
GB_OFFS = [-384, -256, -128, 0, 128, 256, 384, 512]

DEBUG_TAPS = ()


def host_constants():
    t = np.arange(L, dtype=np.float64)
    invf = 10000.0 ** (-np.arange(0, HD, 2, dtype=np.float64) / HD)
    emb = np.concatenate([t[:, None] * invf[None, :]] * 2, axis=1)
    cos, sin = np.cos(emb), np.sin(emb)

    def tmaj(a):
        return np.ascontiguousarray(
            a.reshape(TT, 128, HD).transpose(1, 0, 2).reshape(128, TT * HD)
        ).astype(bf16)

    sin_s = sin.copy()
    sin_s[:, :HD // 2] *= -1.0
    return {
        "c_cosq": tmaj(cos),
        "c_sinq": tmaj(sin_s),
        "c_eye": np.eye(128, dtype=np.float32).astype(bf16),
        "c_eye8": (8.0 * np.eye(128, dtype=np.float32)).astype(bf16),
        "c_onesT": np.ones((128, 64), dtype=np.float32).astype(bf16),
        "c_onesrow": np.ones((64, 128), dtype=np.float32).astype(bf16),
        "c_negrow": np.full((64, 128), -1.0, dtype=np.float32).astype(bf16),
        "c_recipbc": np.ascontiguousarray(np.broadcast_to(
            1.0 / (np.arange(L, dtype=np.float64) + 1.0), (64, L))
        ).astype(np.float32),
    }


CONST_SPECS = [
    ("c_cosq", [128, TT * HD], BF16), ("c_sinq", [128, TT * HD], BF16),
    ("c_eye", [128, 128], BF16), ("c_eye8", [128, 128], BF16),
    ("c_onesT", [128, 64], BF16),
    ("c_onesrow", [64, 128], BF16), ("c_negrow", [64, 128], BF16),
    ("c_recipbc", [64, L], F32),
]


def host_gb_tiles(biases):
    base = 1024
    ext = np.zeros((H, 4096), dtype=np.float32)
    ext[:, :base] = -1e30
    ext[:, base:base + 513] = np.asarray(biases, dtype=np.float32)
    oi = np.asarray(GB_OFFS)[:, None, None]
    j = np.arange(128)[None, :, None]
    i = np.arange(512)[None, None, :]
    return np.ascontiguousarray(ext[:, base + oi + i - j])


def gb_off_index(g, jt):
    off = 512 * g - 128 * jt
    return GB_OFFS.index(off) if off <= 512 + 127 else None


def build_program():
    nc = bass.Bass("TRN2", target_bir_lowering=False,
                   detect_race_conditions=False)

    xin = nc.dram_tensor("xin", [B, L, D], F32, kind="ExternalInput").ap()
    wqT = nc.dram_tensor("wqT", [D, D], F32, kind="ExternalInput").ap()
    wkT = nc.dram_tensor("wkT", [D, D], F32, kind="ExternalInput").ap()
    wvT = nc.dram_tensor("wvT", [D, D], F32, kind="ExternalInput").ap()
    woT = nc.dram_tensor("woT", [D, D], F32, kind="ExternalInput").ap()
    wgT = nc.dram_tensor("wgT", [D, HID], I32, kind="ExternalInput").ap()
    wuT = nc.dram_tensor("wuT", [D, HID], I32, kind="ExternalInput").ap()
    wdT = nc.dram_tensor("wdT", [HID, D], I32, kind="ExternalInput").ap()
    gbt = nc.dram_tensor("gbt", [H, 8, 128, 512], F32, kind="ExternalInput").ap()
    biasd = nc.dram_tensor("biasd", [H, 513], F32, kind="ExternalInput").ap()
    taud = nc.dram_tensor("taud", [1, H], F32, kind="ExternalInput").ap()
    cn = {}
    for name, shape, dt in CONST_SPECS:
        cn[name] = nc.dram_tensor(name, shape, dt, kind="ExternalInput").ap()

    out = nc.dram_tensor("out", [B, L, D], F32, kind="ExternalOutput").ap()
    x2d = nc.dram_tensor("x2d", [L, D], F32, kind="Internal").ap()
    wqB = nc.dram_tensor("wqB", [D * D], BF16, kind="Internal").ap()
    wkB = nc.dram_tensor("wkB", [D * D], BF16, kind="Internal").ap()
    wvB = nc.dram_tensor("wvB", [D * D], BF16, kind="Internal").ap()
    woB = nc.dram_tensor("woB", [D * D], BF16, kind="Internal").ap()
    wgB = nc.dram_tensor("wgB", [D * HID], BF16, kind="Internal").ap()
    wuB = nc.dram_tensor("wuB", [D * HID], BF16, kind="Internal").ap()
    wdB = nc.dram_tensor("wdB", [HID * D], BF16, kind="Internal").ap()
    gbB = nc.dram_tensor("gbB", [H * 8 * 128 * 512], BF16, kind="Internal").ap()
    dbg = {}
    dbg_shapes = {"stat1": [128, 256], "xraw": [128, 14336],
                  "xhatfm": [128, DC * L], "qfm0": [128, PPB * L],
                  "kfm0": [128, PPB * L], "vtm0": [128, TT * BW],
                  "attnfm": [128, DC * L], "x2": [L, D], "serow00": [64, 512],
                  "phat00": [128, 16 * 512], "hfm1": [128, 24 * 512],
                  "ffn_d": [4, 128, D]}
    for name in DEBUG_TAPS:
        dbg[name] = nc.dram_tensor("dbg_" + name, dbg_shapes[name], F32,
                                   kind="ExternalOutput").ap()

    ENGS = ("sync", "gpsimd", "tensor", "scalar", "vector")
    prog = {k: [] for k in ENGS}
    cnt = {}
    semh = {}

    class Sem:
        def __init__(self, name):
            self.name = name

    (s_wl, s_wg, s_xl, s_gb, s_mm, s_sc, s_ve, s_st, s_wh, s_x2, s_x3,
     s_w2) = (Sem(n) for n in (
         "s_wl", "s_wg", "s_xl", "s_gb", "s_mm", "s_sc", "s_ve", "s_st",
         "s_wh", "s_x2", "s_x3", "s_w2"))
    ALL_SEMS = [s_wl, s_wg, s_xl, s_gb, s_mm, s_sc, s_ve, s_st, s_wh, s_x2,
                s_x3, s_w2]

    KSTOP = os.environ.get("KSTOP", "")
    stopped = [False]

    def ckpt(name):
        if KSTOP and name == KSTOP:
            stopped[0] = True

    def E(eng, fn):
        if not stopped[0]:
            prog[eng].append(fn)

    def inc(instr, sr, k=1):
        instr.then_inc(semh[sr.name], k)

    def bump(sr, k=1):
        cnt[sr.name] = cnt.get(sr.name, 0) + k
        return cnt[sr.name]

    def tgt(sr):
        return cnt.get(sr.name, 0)

    def wait(eng, sr, n=None):
        n = tgt(sr) if n is None else n
        if n > 0:
            E(eng, lambda e, s=sr, n=n: e.wait_ge(semh[s.name], n))

    with contextlib.ExitStack() as stack:
        def sbuf(name, shape, dtype):
            return stack.enter_context(nc.sbuf_tensor(name, shape, dtype))

        def psum(name, shape, dtype=F32):
            return stack.enter_context(nc.psum_tensor(name, shape, dtype))

        eye = sbuf("eye", [128, 128], BF16)
        eye8 = sbuf("eye8", [128, 128], BF16)
        onesT = sbuf("onesT", [128, 64], BF16)
        onesrow = sbuf("onesrow", [64, 128], BF16)
        negrow = sbuf("negrow", [64, 128], BF16)
        cosq = sbuf("cosq", [128, TT * HD], BF16)
        sinq = sbuf("sinq", [128, TT * HD], BF16)
        recipbc = sbuf("recipbc", [64, L], F32)
        taurow = sbuf("taurow", [1, H], F32)
        taupair = sbuf("taupair", [64, NP], F32)
        biassb = sbuf("biassb", [16, 513], F32)
        maxgbrow = sbuf("maxgbrow", [1, 128], F32)
        colmb = sbuf("colmb", [128, 1], BF16)

        xhat_fm = sbuf("xhat_fm", [128, DC * L], BF16)
        u64f = sbuf("u64f", [128, 14 * 1024], F32)
        u64b = u64f[:, :].bitcast(BF16)
        PHA, PHB = 0, 8192
        QFM, KFM, VTM = 16384, 20480, 24576
        WD0, WG0, WU0 = 0, 24576, 26624
        u74 = sbuf("u74", [128, 31744], BF16)
        AFM, GBS = 0, 16384
        WQS, WKS, WVS = 24576, 26624, 28672
        CTA, CTB, WOS, HFM = 30720, 31232, 16384, 0

        serow = sbuf("serow", [64, 512], F32)
        csrow = sbuf("csrow", [64, 512], F32)
        rowb = sbuf("rowb", [64, 1536], BF16)
        vpre = sbuf("vpre", [64, (NG + 1) * 128], F32)
        vtailf = sbuf("vtailf", [64, NG * 128], F32)
        mrow = sbuf("mrow", [1, 32], F32)
        ccol = sbuf("ccol", [128, 4], F32)
        colb = sbuf("colb", [128, 4], BF16)
        negmc = sbuf("negmc", [128, 1], F32)
        xio = sbuf("xio", [128, 2 * D], F32)
        xhat_tm = sbuf("xhat_tm", [128, 2 * D], BF16)
        lnsc = sbuf("lnsc", [128, D], F32)
        stat = sbuf("stat", [128, 256], F32)
        ropeA = sbuf("ropeA", [128, BW], F32)
        ropeB = sbuf("ropeB", [128, BW], F32)
        outsb = sbuf("outsb", [128, 2 * D], F32)
        gusb = sbuf("gusb", [128, 512], BF16)
        ropeQ = sbuf("ropeQ", [128, BW], BF16)
        ropeK = sbuf("ropeK", [128, BW], BF16)

        ps = [psum(f"ps{i}", [128, 512]) for i in range(6)]
        ps_r = psum("ps_r", [128, 512])
        ps_t = psum("ps_t", [128, 1024], BF16)
        ps_tf = ps_t[:, :].bitcast(F32)

        for sr in ALL_SEMS:
            semh[sr.name] = stack.enter_context(nc.semaphore(sr.name))

        def fmview(base_ap, nchunk, chunklen, sub0, subn):
            v = base_ap.rearrange("p (c m) -> p c m", m=chunklen)
            return v[:, 0:nchunk, sub0:sub0 + subn]

        CONST_DSTS = [("c_cosq", cosq), ("c_sinq", sinq),
                      ("c_recipbc", recipbc), ("c_eye", eye),
                      ("c_eye8", eye8), ("c_onesT", onesT),
                      ("c_onesrow", onesrow), ("c_negrow", negrow)]

        def _consts(sy):
            for name, dst in CONST_DSTS:
                inc(sy.dma_start(out=dst[:, :], in_=cn[name][:, :]), s_wl, 16)
            inc(sy.dma_start(out=taurow[:, :], in_=taud[:, :]), s_wl, 16)
            inc(sy.dma_start(out=biassb[:, :], in_=biasd[:, :]), s_wl, 16)
        E("sync", _consts)
        bump(s_wl, 16 * (len(CONST_DSTS) + 2))

        def _taus(gp):
            for p in range(NP):
                inc(gp.dma_start(
                    out=taupair[0:32, p:p + 1],
                    in_=taud[0:1, 2 * p:2 * p + 1].partition_broadcast(32)),
                    s_wg, 16)
                inc(gp.dma_start(
                    out=taupair[32:64, p:p + 1],
                    in_=taud[0:1, 2 * p + 1:2 * p + 2].partition_broadcast(32)),
                    s_wg, 16)
        E("gpsimd", _taus)
        bump(s_wg, 16 * 16)

        GBCH = 8 * 128 * 512
        CONV_SEQ = [("wq", wqT, wqB, 1), ("wk", wkT, wkB, 1),
                    ("wv", wvT, wvB, 1)]
        CONV_SEQ += [(("gb", h), gbt, gbB, None) for h in range(H)]
        CONV_SEQ += [("wo", woT, woB, 1), ("wd", wdT, wdB, 4),
                     ("wg", wgT, wgB, 4), ("wu", wuT, wuB, 4)]
        CV = {}

        cv_gates = {"wv", ("gb", 1), ("gb", 15)}

        def _conv(gp, base=tgt(s_wg)):
            cum = base
            for key, src, dst, nch in CONV_SEQ:
                n = int(np.prod(src.tensor.shape))
                if nch is None:
                    h = key[1]
                    inc(gp.dma_start(out=dst[h * GBCH:(h + 1) * GBCH],
                                     in_=bass.AP(src.tensor, h * GBCH,
                                                 [[1, GBCH]])), s_wg, 16)
                    cum += 16
                else:
                    c = n // nch
                    for j in range(nch):
                        inc(gp.dma_start(out=dst[j * c:(j + 1) * c],
                                         in_=bass.AP(src.tensor, j * c,
                                                     [[1, c]])), s_wg, 16)
                    cum += 16 * nch
                if key in cv_gates:
                    gp.wait_ge(semh[s_wg.name], cum)
        E("gpsimd", _conv)
        for key, src, dst, nch in CONV_SEQ:
            bump(s_wg, 16 * (1 if nch is None else nch))
            CV[key] = tgt(s_wg)
        wait("vector", s_wl)
        wait("vector", s_wg, 16 * 16)

        def _maxgb(v):
            v.drain()
            v.memset(ccol[:, 0:1], float(EPS))
            v.memset(ccol[:, 1:2], float(EPS4))
            v.memset(ccol[:, 2:3], 1e-38)
            v.memset(ccol[:, 3:4], -2.0794415416798357)
            v.memset(colmb[:, :], 0.0)
            v.reduce_max(stat[0:16, 0:1], biassb[:, :], axis=AX,
                         apply_absolute_value=True)
            v.drain()
            inc(v.tensor_copy(colmb[0:16, 0:1], stat[0:16, 0:1]), s_ve)
        E("vector", _maxgb)
        bump(s_ve)
        wait("tensor", s_ve)

        def _maxgbt(t):
            inc(t.transpose(ps_t[0:1, 0:128], colmb[:, :], eye[:, :]), s_mm)
        E("tensor", _maxgbt)
        bump(s_mm)
        wait("vector", s_mm)

        def _maxgbr(v):
            inc(v.tensor_copy(maxgbrow[0:1, 0:128], ps_t[0:1, 0:128]), s_ve)
        E("vector", _maxgbr)
        bump(s_ve)
        st8_init_p1evac = tgt(s_ve)

        def ln_tail(colbase, idxs, w, eps):
            wait("vector", s_sc)

            def _var(v, idxs=tuple(idxs), w=w, colbase=colbase):
                n = len(idxs)
                for k in range(n):
                    c = colbase + k * 12
                    v.tensor_scalar(stat[:, c + 4:c + 5], stat[:, c + 3:c + 4],
                                    1.0 / w, None, op0=OP.mult)
                    v.tensor_scalar(stat[:, c + 5:c + 6], stat[:, c + 2:c + 3],
                                    1.0 / w, None, op0=OP.mult)
                if n < 4:
                    v.drain()
                for k in range(n):
                    c = colbase + k * 12
                    v.tensor_tensor(stat[:, c + 6:c + 7], stat[:, c + 4:c + 5],
                                    stat[:, c + 4:c + 5], op=OP.mult)
                if n < 4:
                    v.drain()
                last = None
                for k in range(n):
                    c = colbase + k * 12
                    last = v.tensor_sub(stat[:, c + 7:c + 8],
                                        stat[:, c + 5:c + 6], stat[:, c + 6:c + 7])
                inc(last, s_ve)
            E("vector", _var)
            bump(s_ve)
            wait("scalar", s_ve)

            def _rstd(sc, idxs=tuple(idxs), colbase=colbase,
                      eps_ap=(ccol[:, 0:1] if eps == EPS else ccol[:, 1:2])):
                last = None
                for k in range(len(idxs)):
                    c = colbase + k * 12
                    sc.activation(out=stat[:, c + 8:c + 9], in_=stat[:, c + 7:c + 8],
                                  func=AF.Ln, bias=eps_ap)
                    last = sc.activation(out=stat[:, c + 0:c + 1],
                                         in_=stat[:, c + 8:c + 9],
                                         func=AF.Exp, scale=-0.5)
                inc(last, s_sc)
            E("scalar", _rstd)
            bump(s_sc)
            wait("vector", s_sc)

            def _bias(v, idxs=tuple(idxs), colbase=colbase):
                last = None
                for k in range(len(idxs)):
                    c = colbase + k * 12
                    last = v.tensor_tensor(stat[:, c + 1:c + 2],
                                           stat[:, c + 4:c + 5],
                                           stat[:, c + 0:c + 1], op=OP.mult)
                inc(last, s_ve)
            E("vector", _bias)
            bump(s_ve)
            wait("scalar", s_ve)

        st8 = {}

        st8["p1evac"] = st8_init_p1evac
        for b in range(B):
            def xsrc(i):
                if i < 14:
                    return u64f[:, i * 1024:(i + 1) * 1024]
                return xio[:, (i - 14) * D:(i - 13) * D]

            wait("sync", s_mm)
            wait("sync", s_sc)
            wait("sync", s_ve)

            def _ldx(sy, b=b):
                last = None
                for i in range(TT):
                    last = sy.dma_start(out=xsrc(i),
                                        in_=xin[b, i * 128:(i + 1) * 128, :])
                    inc(last, s_xl, 16)
            E("sync", _ldx)
            bump(s_xl, 16 * TT)
            wait("scalar", s_xl)
            wait("vector", s_xl)

            def _sq1(sc):
                last = None
                for i in range(TT):
                    c = i * 12
                    last = sc.activation(out=lnsc[:, :], in_=xsrc(i), func=AF.Square,
                                         accum_out=stat[:, c + 2:c + 3])
                inc(last, s_sc)
            E("scalar", _sq1)
            bump(s_sc)

            def _ns1(v):
                last = None
                for i in range(TT):
                    c = i * 12
                    last = v.reduce_sum(stat[:, c + 3:c + 4], xsrc(i), axis=AX,
                                        negate=True)
                inc(last, s_ve)
            E("vector", _ns1)
            bump(s_ve)
            ln_tail(0, range(TT), D, EPS)

            for i in range(TT):
                mmp = st8.get(("xtm_mm", i % 2), 0)

                def _ap(sc, i=i, mmp=mmp):
                    if mmp > 0:
                        sc.wait_ge(semh[s_mm.name], mmp)
                    c = i * 12
                    inc(sc.activation(out=xhat_tm[:, (i % 2) * D:(i % 2 + 1) * D],
                                      in_=xsrc(i), func=AF.Identity,
                                      scale=stat[:, c + 0:c + 1],
                                      bias=stat[:, c + 1:c + 2]), s_sc)
                E("scalar", _ap)
                bump(s_sc)
                sc_now = tgt(s_sc)
                ve_prev_evac = st8.get("p1evac", 0)

                def _tp(t, i=i, sc_now=sc_now, vp=ve_prev_evac):
                    t.wait_ge(semh[s_sc.name], sc_now)
                    if vp > 0:
                        t.wait_ge(semh[s_ve.name], vp)
                    last = None
                    for c in range(DC):
                        last = t.transpose(ps_t[:, c * 128:(c + 1) * 128],
                                           xhat_tm[:, (i % 2) * D + c * 128:
                                                   (i % 2) * D + (c + 1) * 128],
                                           eye[:, :])
                    inc(last, s_mm)
                E("tensor", _tp)
                bump(s_mm)
                st8[("xtm_mm", i % 2)] = tgt(s_mm)
                wait("vector", s_mm)

                def _ev(v, i=i):
                    dst = fmview(xhat_fm[:, :], DC, L, i * 128, 128)
                    inc(v.tensor_copy(dst, ps_t[:, :].rearrange(
                        "p (c n) -> p c n", n=128)), s_ve)
                E("vector", _ev)
                bump(s_ve)
                st8["p1evac"] = tgt(s_ve)
            P1_EVAC_ALL = tgt(s_ve)
            P1_APPLY_ALL = tgt(s_sc)

            if b == 0 and ("stat1" in dbg or "xraw" in dbg):
                wait("gpsimd", s_ve)
                wait("gpsimd", s_sc)

                def _dst(sy):
                    if "stat1" in dbg:
                        inc(sy.dma_start(out=dbg["stat1"][:, :], in_=stat[:, :]),
                            s_gb, 16)
                    if "xraw" in dbg:
                        inc(sy.dma_start(out=dbg["xraw"][:, :], in_=u64f[:, :]),
                            s_gb, 16)
                E("gpsimd", _dst)
                if "stat1" in dbg:
                    bump(s_gb, 16)
                if "xraw" in dbg:
                    bump(s_gb, 16)
            ckpt(f"p1_{b}")
            if b == 0 and "xhatfm" in dbg:
                wait("gpsimd", s_ve)

                def _d1(sy):
                    inc(sy.dma_start(out=dbg["xhatfm"][:, :], in_=xhat_fm[:, :]),
                        s_gb, 16)
                E("gpsimd", _d1)
                bump(s_gb, 16)

            for bi in range(NBLK):
                col0 = bi * BW
                wait("sync", s_mm)

                def _wblk(sy, col0=col0, cvw=CV["wv"]):
                    sy.wait_ge(semh[s_wg.name], cvw)
                    for wsrc, off in ((wqB, WQS), (wkB, WKS), (wvB, WVS)):
                        inc(sy.dma_start(
                            out=u74[:, off:off + DC * BW],
                            in_=bass.AP(wsrc.tensor, col0,
                                        [[D, 128], [128 * D, DC], [1, BW]])),
                            s_wl, 16)
                E("sync", _wblk)
                bump(s_wl, 48)
                wait("tensor", s_wl)
                wait("tensor", s_ve, P1_EVAC_ALL)
                wait("tensor", s_sc)

                for it in range(TT):
                    pq, pk, pv = ps[3 * (it % 2)], ps[3 * (it % 2) + 1], ps[3 * (it % 2) + 2]
                    vp = st8.get(("projve", it % 2), 0)

                    def _proj(t, it=it, pq=pq, pk=pk, pv=pv, vp=vp):
                        if vp > 0:
                            t.wait_ge(semh[s_ve.name], vp)
                        last = None
                        for kc in range(DC):
                            lhs = xhat_fm[:, kc * L + it * 128:
                                          kc * L + (it + 1) * 128]
                            t.matmul(pq[:, 0:BW], lhs,
                                     u74[:, WQS + kc * BW:WQS + (kc + 1) * BW],
                                     start=(kc == 0), stop=(kc == DC - 1))
                            t.matmul(pk[:, 0:BW], lhs,
                                     u74[:, WKS + kc * BW:WKS + (kc + 1) * BW],
                                     start=(kc == 0), stop=(kc == DC - 1))
                            last = t.matmul(pv[:, 0:BW], lhs,
                                            u74[:, WVS + kc * BW:WVS + (kc + 1) * BW],
                                            start=(kc == 0), stop=(kc == DC - 1))
                        inc(last, s_mm)
                    E("tensor", _proj)
                    bump(s_mm)
                    wait("vector", s_mm)

                    def _rope(v, it=it, pq=pq, pk=pk, pv=pv):
                        for src, ctab, stab, dst in ((pq, cosq, sinq, ropeQ),
                                                     (pk, cosq, sinq, ropeK)):
                            i0 = src[:, 0:BW].rearrange("p (h k) -> p h k", k=HD)
                            tc = ctab[:, it * HD:(it + 1) * HD]
                            ts_ = stab[:, it * HD:(it + 1) * HD]
                            cbc = bass.AP(tc.tensor, tc.offset,
                                          [list(tc.ap[0]), [0, BW // HD], [1, HD]])
                            sbc0 = bass.AP(ts_.tensor, ts_.offset,
                                           [list(ts_.ap[0]), [0, BW // HD], [1, 32]])
                            sbc1 = bass.AP(ts_.tensor, ts_.offset + 32,
                                           [list(ts_.ap[0]), [0, BW // HD], [1, 32]])
                            v.tensor_tensor(
                                ropeA[:, :].rearrange("p (h k) -> p h k", k=HD),
                                i0, cbc, op=OP.mult)
                            ih = src[:, 0:BW].rearrange("p (h t k) -> p h t k",
                                                        t=2, k=32)
                            rh = ropeB[:, :].rearrange("p (h t k) -> p h t k",
                                                       t=2, k=32)
                            v.tensor_tensor(rh[:, :, 0, :], ih[:, :, 1, :], sbc0,
                                            op=OP.mult)
                            v.tensor_tensor(rh[:, :, 1, :], ih[:, :, 0, :], sbc1,
                                            op=OP.mult)
                            v.tensor_add(dst[:, :], ropeA[:, :], ropeB[:, :])
                        last = v.tensor_copy(
                            u64b[:, VTM + it * BW:VTM + (it + 1) * BW], pv[:, 0:BW])
                        inc(last, s_ve)
                    E("vector", _rope)
                    bump(s_ve)
                    st8[("projve", it % 2)] = tgt(s_ve)
                    wait("tensor", s_ve)

                    vp2 = st8.get("qkevac", 0)

                    def _tpqk(t, vp2=vp2):
                        if vp2 > 0:
                            t.wait_ge(semh[s_ve.name], vp2)
                        last = None
                        for c in range(PPB):
                            t.transpose(ps_t[:, c * 128:(c + 1) * 128],
                                        ropeQ[:, c * 128:(c + 1) * 128], eye[:, :])
                            last = t.transpose(ps_t[:, 256 + c * 128:256 + (c + 1) * 128],
                                               ropeK[:, c * 128:(c + 1) * 128],
                                               eye[:, :])
                        inc(last, s_mm)
                    E("tensor", _tpqk)
                    bump(s_mm)
                    wait("vector", s_mm)

                    def _evqk(v, it=it):
                        dq = fmview(u64b[:, QFM:QFM + PPB * L], PPB, L, it * 128, 128)
                        dk = fmview(u64b[:, KFM:KFM + PPB * L], PPB, L, it * 128, 128)
                        v.tensor_copy(dq, ps_t[:, 0:256].rearrange(
                            "p (c n) -> p c n", n=128))
                        inc(v.tensor_copy(dk, ps_t[:, 256:512].rearrange(
                            "p (c n) -> p c n", n=128)), s_ve)
                    E("vector", _evqk)
                    bump(s_ve)
                    st8["qkevac"] = tgt(s_ve)
                VE_BLOCK_QKV = tgt(s_ve)

                if b == 0 and bi == 0:
                    for nm, off, ln in (("qfm0", QFM, PPB * L), ("kfm0", KFM, PPB * L),
                                        ("vtm0", VTM, TT * BW)):
                        if nm in dbg:
                            wait("gpsimd", s_ve)

                            def _dq(sy, nm=nm, off=off, ln=ln):
                                inc(sy.dma_start(out=dbg[nm][:, :],
                                                 in_=u64b[:, off:off + ln]), s_gb, 16)
                            E("gpsimd", _dq)
                            bump(s_gb, 16)

                ckpt(f"proj_{b}_{bi}")
                wait("scalar", s_st)
                wait("tensor", s_st)
                for pp in range(PPB):
                    p = bi * PPB + pp
                    hA, hB = 2 * p, 2 * p + 1
                    qof = QFM + pp * L
                    kof = KFM + pp * L

                    wait("sync", s_mm)

                    def _gbl(sy, hA=hA, hB=hB, cvg=CV[("gb", hB)]):
                        sy.wait_ge(semh[s_wg.name], cvg)
                        for hh, half in ((hA, 0), (hB, 1)):
                            inc(sy.dma_start(
                                out=u74[:, GBS + half * 4096:GBS + (half + 1) * 4096],
                                in_=bass.AP(gbB.tensor, hh * 8 * 128 * 512,
                                            [[512, 128], [128 * 512, 8], [1, 512]])),
                                s_gb, 16)
                    E("sync", _gbl)
                    bump(s_gb, 32)
                    GB_THIS = tgt(s_gb)

                    wait("vector", s_ve, VE_BLOCK_QKV)
                    wait("scalar", s_ve, VE_BLOCK_QKV)
                    ovb = outsb[:, :].bitcast(BF16)
                    for mi, of in ((0, qof), (1, kof)):
                        scp_m = tgt(s_mm)
                        vep_m = tgt(s_ve)

                        def _msqr(sc, of=of, scp_m=scp_m, ovb=ovb):
                            if scp_m > 0:
                                sc.wait_ge(semh[s_mm.name], scp_m)
                            for c in range(4):
                                inc(sc.activation(out=ovb[:, c * 512:
                                                          (c + 1) * 512],
                                                  in_=u64b[:, of + c * 512:
                                                           of + (c + 1) * 512],
                                                  func=AF.Square), s_sc)
                        E("scalar", _msqr)
                        sq0 = tgt(s_sc)
                        bump(s_sc, 4)

                        ve0 = tgt(s_ve)

                        def _msum(t, vep_m=vep_m, sq0=sq0, ve0=ve0, ovb=ovb):
                            if vep_m > 0:
                                t.wait_ge(semh[s_ve.name], vep_m)
                            for c in range(4):
                                bank = ps_tf if (c % 2 == 0) else ps[0][:, :]
                                t.wait_ge(semh[s_sc.name], sq0 + c + 1)
                                if c >= 2:
                                    t.wait_ge(semh[s_ve.name], ve0 + c - 1)
                                inc(t.matmul(bank[0:1, 0:512],
                                             onesT[:, 0:1],
                                             ovb[:, c * 512:(c + 1) * 512],
                                             start=True, stop=True), s_mm)
                        E("tensor", _msum)
                        mm0 = tgt(s_mm)
                        bump(s_mm, 4)

                        def _mred(v, mi=mi, mm0=mm0):
                            for c in range(4):
                                bank = ps_tf if (c % 2 == 0) else ps[0][:, :]
                                v.wait_ge(semh[s_mm.name], mm0 + c + 1)
                                inc(v.reduce_max(stat[0:1, 240 + 4 * mi + c:
                                                      241 + 4 * mi + c],
                                                 bank[0:1, 0:512], axis=AX),
                                    s_ve)
                        E("vector", _mred)
                        bump(s_ve, 4)

                    def _mfin(v, hA=hA, hB=hB):
                        v.drain()
                        v.reduce_max(mrow[0:1, 0:1], stat[0:1, 240:244], axis=AX)
                        v.reduce_max(mrow[0:1, 1:2], stat[0:1, 244:248], axis=AX)
                        v.drain()
                        v.tensor_tensor(mrow[0:1, 2:3], mrow[0:1, 0:1],
                                        mrow[0:1, 1:2], op=OP.mult)
                        inc(v.tensor_tensor(mrow[0:1, 3:4],
                                            maxgbrow[0:1, hA:hA + 1],
                                            maxgbrow[0:1, hB:hB + 1], op=OP.max),
                            s_ve)
                    E("vector", _mfin)
                    bump(s_ve)
                    wait("scalar", s_ve)

                    def _msq(sc):
                        sc.activation(out=mrow[0:1, 4:5], in_=mrow[0:1, 2:3],
                                      func=AF.Ln, bias=ccol[0:1, 2:3])
                        inc(sc.activation(out=mrow[0:1, 5:6], in_=mrow[0:1, 4:5],
                                          func=AF.Exp, scale=0.5,
                                          bias=ccol[0:1, 3:4]), s_sc)
                    E("scalar", _msq)
                    bump(s_sc)
                    wait("vector", s_sc)

                    def _mneg(v):
                        v.tensor_add(mrow[0:1, 6:7], mrow[0:1, 5:6], mrow[0:1, 3:4])
                        v.drain()
                        v.tensor_scalar(mrow[0:1, 7:8], mrow[0:1, 6:7], -1.0, None,
                                        op0=OP.mult)
                        v.drain()
                        inc(v.tensor_copy(colb[0:1, 2:3], mrow[0:1, 7:8]), s_ve)
                    E("vector", _mneg)
                    bump(s_ve)
                    wait("tensor", s_ve)

                    def _mbc(t):
                        inc(t.matmul(ps_tf[:, 2:3], onesrow[0:1, :], colb[0:1, 2:3],
                                     start=True, stop=True), s_mm)
                    E("tensor", _mbc)
                    bump(s_mm)
                    wait("vector", s_mm)

                    def _mev(v):
                        inc(v.tensor_copy(negmc[:, :], ps_tf[:, 2:3]), s_ve)
                    E("vector", _mev)
                    bump(s_ve)
                    NEGM_VE = tgt(s_ve)


                    def _vch(t, pp=pp):
                        last = None
                        for g4 in range(NG):
                            for jj in range(4):
                                jt = 4 * g4 + jj
                                last = t.matmul(
                                    ps_r[0:33, g4 * 128:(g4 + 1) * 128],
                                    onesT[:, 0:33],
                                    u64b[:, VTM + jt * BW + pp * 128:
                                         VTM + jt * BW + (pp + 1) * 128],
                                    start=(jj == 0), stop=(jj == 3))
                        inc(last, s_mm)
                    E("tensor", _vch)
                    bump(s_mm)

                    wait("vector", s_mm)

                    def _vev(v):
                        v.tensor_copy(vpre[0:33, 128:256], ps_r[0:33, 0:128])
                        for g4 in range(1, NG):
                            v.tensor_copy(vpre[0:33, 0:128],
                                          ps_r[0:33, g4 * 128:(g4 + 1) * 128])
                            v.drain()
                            v.tensor_add(vpre[0:33, (g4 + 1) * 128:(g4 + 2) * 128],
                                         vpre[0:33, g4 * 128:(g4 + 1) * 128],
                                         vpre[0:33, 0:128])
                            v.drain()
                        last = None
                        for g4 in range(NG):
                            last = v.tensor_sub(
                                vtailf[0:33, g4 * 128:(g4 + 1) * 128],
                                vpre[0:33, NG * 128:(NG + 1) * 128],
                                vpre[0:33, (g4 + 1) * 128:(g4 + 2) * 128])
                        inc(last, s_ve)
                    E("vector", _vev)
                    bump(s_ve)

                    for g in range(NG):
                        JT = 4 * g + 4
                        i0 = g * 512

                        def phA(jt):
                            return u64b[:, PHA + jt * 512:PHA + (jt + 1) * 512]

                        def phB(jt):
                            return u64b[:, PHB + jt * 512:PHB + (jt + 1) * 512]

                        exp_base = tgt(s_sc)
                        prev_pv_sc = st8.get("grp_sc_ready", 0)

                        def _st(t, g=g, JT=JT, i0=i0, qof=qof, kof=kof,
                                exp_base=exp_base, gbn=GB_THIS, negve=NEGM_VE):
                            t.wait_ge(semh[s_gb.name], gbn)
                            t.wait_ge(semh[s_ve.name], negve)
                            for jt in range(JT):
                                if jt >= 2:
                                    t.wait_ge(semh[s_sc.name], exp_base + jt - 1)
                                bA = ps[2 * (jt % 2)]
                                bB = ps[2 * (jt % 2) + 1]
                                lA = u64b[0:64, kof + jt * 128:kof + (jt + 1) * 128]
                                lB = u64b[64:128, kof + jt * 128:kof + (jt + 1) * 128]
                                rA = u64b[0:64, qof + i0:qof + i0 + 512]
                                rB = u64b[64:128, qof + i0:qof + i0 + 512]
                                oi = gb_off_index(g, jt)
                                t.matmul(bA[:, :], lA, rA, start=True,
                                         stop=(oi is None), tile_position=(0, 0))
                                lastm = t.matmul(bB[:, :], lB, rB, start=True,
                                                 stop=(oi is None),
                                                 tile_position=(64, 0))
                                if oi is not None:
                                    t.matmul(bA[:, :], eye8[:, :],
                                             u74[:, GBS + oi * 512:
                                                  GBS + (oi + 1) * 512],
                                             start=False, stop=True)
                                    lastm = t.matmul(
                                        bB[:, :], eye8[:, :],
                                        u74[:, GBS + 4096 + oi * 512:
                                             GBS + 4096 + (oi + 1) * 512],
                                        start=False, stop=True)
                                inc(lastm, s_mm)
                                if jt >= 1:
                                    t.wait_ge(semh[s_sc.name], exp_base + jt)
                                    t.matmul(ps[4][0:32, :], onesT[:, 0:32],
                                             phA(jt - 1), start=(jt == 1),
                                             stop=False, tile_position=(0, 0))
                                    t.matmul(ps_r[32:64, :], onesT[:, 0:32],
                                             phB(jt - 1), start=(jt == 1),
                                             stop=False, tile_position=(0, 32))
                            t.wait_ge(semh[s_sc.name], exp_base + JT)
                            t.matmul(ps[4][0:32, :], onesT[:, 0:32], phA(JT - 1),
                                     start=(JT == 1), stop=True,
                                     tile_position=(0, 0))
                            inc(t.matmul(ps_r[32:64, :], onesT[:, 0:32], phB(JT - 1),
                                         start=(JT == 1), stop=True,
                                         tile_position=(0, 32)), s_mm)
                        E("tensor", _st)
                        bump(s_mm, JT + 1)
                        mm_st0 = tgt(s_mm) - JT - 1

                        def _exp(sc, JT=JT, mm0=mm_st0, scr=prev_pv_sc):
                            if scr > 0:
                                sc.wait_ge(semh[s_mm.name], scr)
                            for jt in range(JT):
                                sc.wait_ge(semh[s_mm.name], mm0 + jt + 1)
                                bA = ps[2 * (jt % 2)]
                                bB = ps[2 * (jt % 2) + 1]
                                sc.activation(out=phA(jt), in_=bA[:, :], func=AF.Exp,
                                              scale=QK_SCALE, bias=negmc[:, 0:1])
                                inc(sc.activation(out=phB(jt), in_=bB[:, :],
                                                  func=AF.Exp, scale=QK_SCALE,
                                                  bias=negmc[:, 0:1]), s_sc)
                        E("scalar", _exp)
                        bump(s_sc, JT)

                        wait("vector", s_mm)

                        def _rows1(v):
                            v.tensor_copy(serow[0:32, :], ps[4][0:32, :])
                            inc(v.tensor_copy(serow[32:64, :], ps_r[32:64, :]), s_ve)
                        E("vector", _rows1)
                        bump(s_ve)
                        wait("scalar", s_ve)

                        def _rows2(sc):
                            sc.activation(out=lnsc[0:64, 0:512], in_=serow[:, :],
                                          func=AF.Ln)
                            inc(sc.activation(out=lnsc[0:64, 0:512],
                                              in_=lnsc[0:64, 0:512],
                                              func=AF.Exp, scale=-1.0), s_sc)
                        E("scalar", _rows2)
                        bump(s_sc)
                        wait("vector", s_sc)

                        def _rows3(v, p=p, i0=i0):
                            v.tensor_scalar(csrow[:, :], recipbc[:, i0:i0 + 512],
                                            taupair[:, p:p + 1], None, op0=OP.mult)
                            v.tensor_relu(lnsc[0:64, 512:1024], csrow[:, :])
                            v.tensor_tensor(csrow[:, :], csrow[:, :], serow[:, :],
                                            op=OP.mult)
                            v.tensor_tensor(lnsc[0:64, 512:1024],
                                            lnsc[0:64, 512:1024], serow[:, :],
                                            op=OP.mult)
                            v.tensor_copy(rowb[0:1, 0:512], csrow[0:1, :])
                            v.tensor_copy(rowb[32:33, 0:512], csrow[32:33, :])
                            v.tensor_copy(rowb[0:1, 512:1024], lnsc[0:1, 512:1024])
                            v.tensor_copy(rowb[32:33, 512:1024],
                                          lnsc[32:33, 512:1024])
                            v.tensor_copy(rowb[0:1, 1024:1536], lnsc[0:1, 0:512])
                            v.tensor_copy(rowb[32:33, 1024:1536],
                                          lnsc[32:33, 0:512])
                            v.tensor_copy(csrow[0:1, :], rowb[0:1, 0:512])
                            v.tensor_copy(csrow[32:33, :], rowb[32:33, 0:512])
                            v.tensor_copy(lnsc[0:1, 512:1024], rowb[0:1, 512:1024])
                            inc(v.tensor_copy(lnsc[32:33, 512:1024],
                                              rowb[32:33, 512:1024]), s_ve)
                        E("vector", _rows3)
                        bump(s_ve)
                        wait("tensor", s_ve)

                        def _ct1(t):
                            t.matmul(ps_tf[:, :], negrow[0:1, :], rowb[0:1, 0:512],
                                     start=True, stop=True)
                            inc(t.matmul(ps[0][:, :], negrow[32:33, :],
                                         rowb[32:33, 0:512], start=True, stop=True),
                                s_mm)
                        E("tensor", _ct1)
                        bump(s_mm)
                        wait("vector", s_mm)

                        def _ce1(v):
                            v.tensor_copy(u74[:, CTA:CTA + 512], ps_tf[:, :])
                            inc(v.tensor_copy(u74[:, CTB:CTB + 512], ps[0][:, :]),
                                s_ve)
                        E("vector", _ce1)
                        bump(s_ve)

                        def _max(v, JT=JT):
                            last = None
                            for jt in range(JT):
                                v.tensor_tensor(phA(jt), phA(jt),
                                                u74[:, CTA:CTA + 512], op=OP.max)
                                last = v.tensor_tensor(phB(jt), phB(jt),
                                                       u74[:, CTB:CTB + 512],
                                                       op=OP.max)
                            inc(last, s_ve)
                        E("vector", _max)
                        bump(s_ve)
                        wait("tensor", s_ve)

                        def _pv(t, JT=JT, pp=pp, g=g):
                            for jt in range(JT):
                                vA = u64b[:, VTM + jt * BW + pp * 128:
                                          VTM + jt * BW + pp * 128 + 64]
                                vB = u64b[:, VTM + jt * BW + pp * 128 + 64:
                                          VTM + jt * BW + (pp + 1) * 128]
                                t.matmul(ps[4][0:64, :], vA, phA(jt),
                                         start=(jt == 0), stop=False,
                                         tile_position=(0, 0))
                                t.matmul(ps_r[64:128, :], vB, phB(jt),
                                         start=(jt == 0), stop=False,
                                         tile_position=(0, 64))
                            t.matmul(ps[4][0:64, :],
                                     vpre[0:1, (g + 1) * 128:(g + 1) * 128 + 64],
                                     csrow[0:1, :], start=False, stop=False,
                                     tile_position=(0, 0))
                            t.matmul(ps_r[64:128, :],
                                     vpre[32:33, (g + 1) * 128 + 64:(g + 2) * 128],
                                     csrow[32:33, :], start=False, stop=False,
                                     tile_position=(32, 64))
                            t.matmul(ps[4][0:64, :],
                                     vtailf[0:1, g * 128:g * 128 + 64],
                                     lnsc[0:1, 512:1024], start=False, stop=True,
                                     tile_position=(0, 0))
                            t.matmul(ps_r[64:128, :],
                                     vtailf[32:33, g * 128 + 64:(g + 1) * 128],
                                     lnsc[32:33, 512:1024], start=False, stop=True,
                                     tile_position=(32, 64))
                            t.matmul(ps_tf[0:64, :], onesrow[0:1, 0:64],
                                     rowb[0:1, 1024:1536], start=True, stop=True,
                                     tile_position=(0, 0))
                            inc(t.matmul(ps_tf[64:128, :], onesrow[32:33, 0:64],
                                         rowb[32:33, 1024:1536], start=True,
                                         stop=True, tile_position=(32, 64)), s_mm)
                        E("tensor", _pv)
                        bump(s_mm)
                        st8["grp_sc_ready"] = tgt(s_mm)
                        wait("vector", s_mm)

                        def _oev(v, p=p, i0=i0):
                            v.tensor_copy(lnsc[:, 0:512], ps_tf[:, 0:512])
                            v.tensor_tensor(
                                u74[:, AFM + p * L + i0:AFM + p * L + i0 + 512]
                                .rearrange("p n -> p n") if False else
                                u74[0:64, AFM + p * L + i0:AFM + p * L + i0 + 512],
                                ps[4][0:64, :], lnsc[0:64, 0:512], op=OP.mult)
                            inc(v.tensor_tensor(
                                u74[64:128, AFM + p * L + i0:AFM + p * L + i0 + 512],
                                ps_r[64:128, :], lnsc[64:128, 0:512], op=OP.mult),
                                s_ve)
                        E("vector", _oev)
                        bump(s_ve)
                        ckpt(f"grp_{b}_{p}_{g}")

                        if b == 0 and p == 0 and g == 0:
                            if "serow00" in dbg:
                                wait("gpsimd", s_ve)

                                def _ds(sy):
                                    inc(sy.dma_start(out=dbg["serow00"][:, :],
                                                     in_=serow[:, :]), s_gb, 16)
                                E("gpsimd", _ds)
                                bump(s_gb, 16)
                            if "phat00" in dbg:
                                wait("gpsimd", s_sc, tgt(s_sc))

                                def _dp(sy):
                                    inc(sy.dma_start(out=dbg["phat00"][:, :],
                                                     in_=u64b[:, PHA:PHA + 8192]),
                                        s_gb, 16)
                                E("gpsimd", _dp)
                                bump(s_gb, 16)
                    ckpt(f"pair_{b}_{p}")
            ATTN_VE_DONE = tgt(s_ve)
            ATTN_MM_DONE = tgt(s_mm)

            if b == 0 and "attnfm" in dbg:
                wait("gpsimd", s_ve)

                def _da(gp):
                    inc(gp.dma_start(out=dbg["attnfm"][:, :],
                                     in_=u74[:, AFM:AFM + DC * L]), s_wg, 16)
                E("gpsimd", _da)
                bump(s_wg, 16)

            ckpt(f"attn_{b}")
            wait("sync", s_mm)

            def _wo(sy, cvw=CV["wu"]):
                sy.wait_ge(semh[s_wg.name], cvw)
                inc(sy.dma_start(
                    out=u74[:, WOS:WOS + DC * D],
                    in_=bass.AP(woB.tensor, 0, [[D, 128], [128 * D, DC], [1, D]])),
                    s_wl, 16)
            E("sync", _wo)
            bump(s_wl, 16)
            ckpt(f"p3wo_{b}")
            wait("tensor", s_wl)
            wait("tensor", s_ve, ATTN_VE_DONE)
            wait("sync", s_sc)
            wait("sync", s_ve)
            for it in range(TT):
                vp = st8.get(("x2ve", it % 2), 0)
                psem = s_x2 if it % 2 == 0 else s_x3

                def _lx(sy, b=b, it=it, vp=vp, psem=psem):
                    if vp > 0:
                        sy.wait_ge(semh[s_ve.name], vp)
                    inc(sy.dma_start(out=xio[:, (it % 2) * D:(it % 2 + 1) * D],
                                     in_=xin[b, it * 128:(it + 1) * 128, :]),
                        psem, 16)
                E("sync", _lx)
                bump(psem, 16)
                XLT = tgt(psem)

                scp = st8.get(("wosc", it % 2), 0)

                def _wop(t, it=it, scp=scp):
                    if scp > 0:
                        t.wait_ge(semh[s_sc.name], scp)
                    b0 = ps[2 * (it % 2)]
                    b1 = ps[2 * (it % 2) + 1]
                    last = None
                    for kc in range(DC):
                        lhs = u74[:, AFM + kc * L + it * 128:
                                  AFM + kc * L + (it + 1) * 128]
                        t.matmul(b0[:, :], lhs,
                                 u74[:, WOS + kc * D:WOS + kc * D + 512],
                                 start=(kc == 0), stop=(kc == DC - 1))
                        last = t.matmul(b1[:, :], lhs,
                                        u74[:, WOS + kc * D + 512:WOS + (kc + 1) * D],
                                        start=(kc == 0), stop=(kc == DC - 1))
                    inc(last, s_mm)
                E("tensor", _wop)
                bump(s_mm)
                ckpt(f"p3mm_{b}_{it}")
                wait("scalar", s_mm)
                wait("scalar", s_ve)
                wait("vector", s_mm)
                cc = 192 + (it % 2) * 24

                def _2sq(sc, it=it, cc=cc):
                    b0 = ps[2 * (it % 2)]
                    b1 = ps[2 * (it % 2) + 1]
                    sc.activation(out=lnsc[:, 0:512], in_=b0[:, :], func=AF.Square,
                                  accum_out=stat[:, cc + 2:cc + 3])
                    inc(sc.activation(out=lnsc[:, 0:512], in_=b1[:, :],
                                      func=AF.Square,
                                      accum_out=stat[:, cc + 3:cc + 4]), s_sc)
                E("scalar", _2sq)
                bump(s_sc)
                wait("vector", s_sc)

                def _2ns(v, it=it, cc=cc):
                    b0 = ps[2 * (it % 2)]
                    b1 = ps[2 * (it % 2) + 1]
                    v.reduce_sum(stat[:, cc + 4:cc + 5], b0[:, :], axis=AX,
                                 negate=True)
                    inc(v.reduce_sum(stat[:, cc + 5:cc + 6], b1[:, :], axis=AX,
                                     negate=True), s_ve)
                E("vector", _2ns)
                bump(s_ve)
                ckpt(f"p3s1_{b}_{it}")
                wait("vector", s_sc)

                def _2var(v, cc=cc):
                    v.tensor_add(stat[:, cc + 6:cc + 7], stat[:, cc + 2:cc + 3],
                                 stat[:, cc + 3:cc + 4])
                    v.tensor_add(stat[:, cc + 7:cc + 8], stat[:, cc + 4:cc + 5],
                                 stat[:, cc + 5:cc + 6])
                    v.drain()
                    v.tensor_scalar(stat[:, cc + 8:cc + 9], stat[:, cc + 7:cc + 8],
                                    1.0 / D, None, op0=OP.mult)
                    v.tensor_scalar(stat[:, cc + 9:cc + 10], stat[:, cc + 6:cc + 7],
                                    1.0 / D, None, op0=OP.mult)
                    v.drain()
                    v.tensor_tensor(stat[:, cc + 10:cc + 11], stat[:, cc + 8:cc + 9],
                                    stat[:, cc + 8:cc + 9], op=OP.mult)
                    v.drain()
                    inc(v.tensor_sub(stat[:, cc + 11:cc + 12],
                                     stat[:, cc + 9:cc + 10],
                                     stat[:, cc + 10:cc + 11]), s_ve)
                E("vector", _2var)
                bump(s_ve)
                ckpt(f"p3s2_{b}_{it}")
                wait("scalar", s_ve)

                def _2rs(sc, cc=cc):
                    sc.activation(out=stat[:, cc + 12:cc + 13],
                                  in_=stat[:, cc + 11:cc + 12], func=AF.Ln,
                                  bias=ccol[:, 0:1])
                    inc(sc.activation(out=stat[:, cc + 0:cc + 1],
                                      in_=stat[:, cc + 12:cc + 13], func=AF.Exp,
                                      scale=-0.5), s_sc)
                E("scalar", _2rs)
                bump(s_sc)
                ckpt(f"p3s3_{b}_{it}")
                wait("vector", s_sc)

                def _2b(v, cc=cc):
                    inc(v.tensor_tensor(stat[:, cc + 1:cc + 2], stat[:, cc + 8:cc + 9],
                                        stat[:, cc + 0:cc + 1], op=OP.mult), s_ve)
                E("vector", _2b)
                bump(s_ve)
                wait("scalar", s_ve)

                def _2ap(sc, it=it, cc=cc):
                    b0 = ps[2 * (it % 2)]
                    b1 = ps[2 * (it % 2) + 1]
                    sc.activation(out=lnsc[:, 0:512], in_=b0[:, :], func=AF.Identity,
                                  scale=stat[:, cc + 0:cc + 1],
                                  bias=stat[:, cc + 1:cc + 2])
                    inc(sc.activation(out=lnsc[:, 512:1024], in_=b1[:, :],
                                      func=AF.Identity,
                                      scale=stat[:, cc + 0:cc + 1],
                                      bias=stat[:, cc + 1:cc + 2]), s_sc)
                E("scalar", _2ap)
                bump(s_sc)
                st8[("wosc", it % 2)] = tgt(s_sc)
                ckpt(f"p3ap_{b}_{it}")
                wait("vector", s_sc)
                wait("vector", psem, XLT)
                stp = st8.get(("x2st", it % 2), 0)
                if stp > 0:
                    wait("vector", s_st, stp)

                def _x2(v, it=it):
                    inc(v.tensor_add(outsb[:, (it % 2) * D:(it % 2 + 1) * D],
                                     xio[:, (it % 2) * D:(it % 2 + 1) * D],
                                     lnsc[:, 0:D]), s_ve)
                E("vector", _x2)
                bump(s_ve)
                st8[("x2ve", it % 2)] = tgt(s_ve)
                ckpt(f"p3x2_{b}_{it}")
                c3 = it * 12
                wait("scalar", s_ve)

                def _3sq(sc, it=it, c3=c3):
                    inc(sc.activation(out=lnsc[:, 0:D],
                                      in_=outsb[:, (it % 2) * D:(it % 2 + 1) * D],
                                      func=AF.Square,
                                      accum_out=stat[:, c3 + 2:c3 + 3]), s_sc)
                E("scalar", _3sq)
                bump(s_sc)

                def _3ns(v, it=it, c3=c3):
                    inc(v.reduce_sum(stat[:, c3 + 3:c3 + 4],
                                     outsb[:, (it % 2) * D:(it % 2 + 1) * D],
                                     axis=AX, negate=True), s_ve)
                E("vector", _3ns)
                bump(s_ve)
                wait("sync", s_ve)

                def _sx2(sy, it=it):
                    inc(sy.dma_start(out=x2d[it * 128:(it + 1) * 128, :],
                                     in_=outsb[:, (it % 2) * D:(it % 2 + 1) * D]),
                        s_st, 16)
                E("sync", _sx2)
                bump(s_st, 16)
                st8[("x2st", it % 2)] = tgt(s_st)
                ckpt(f"p3t_{b}_{it}")
            wait("vector", s_sc)
            ln_tail(0, range(TT), D, EPS)

            if b == 0 and "x2" in dbg:
                wait("gpsimd", s_st)

                def _dx2(sy):
                    last = None
                    for it in range(TT):
                        last = sy.dma_start(
                            out=dbg["x2"][it * 128:(it + 1) * 128, :],
                            in_=x2d[it * 128:(it + 1) * 128, :])
                        inc(last, s_gb, 16)
                E("sync", _dx2)
                bump(s_gb, 16 * TT)

            ckpt(f"p3_{b}")
            wait("sync", s_mm)

            def _wdl(sy, cvw=CV["wu"]):
                sy.wait_ge(semh[s_wg.name], cvw)
                inc(sy.dma_start(
                    out=u64b[:, WD0:WD0 + 24 * D],
                    in_=bass.AP(wdB.tensor, 0, [[D, 128], [128 * D, 24], [1, D]])),
                    s_wl, 16)
            E("sync", _wdl)
            bump(s_wl, 16)
            WL_WD = tgt(s_wl)
            wait("sync", s_st)

            for ic in range(NG):
                for k in range(4):
                    it = 4 * ic + k
                    scp = st8.get(("l3sc", k % 2), 0)
                    mmp3 = st8.get(("xtm_mm", it % 2), 0)
                    vp3 = st8.get(("x3ve", it % 2), 0)

                    psem = s_x2 if it % 2 == 0 else s_x3

                    def _lx3(sy, it=it, scp=scp, vp3=vp3, psem=psem):
                        if scp > 0:
                            sy.wait_ge(semh[s_sc.name], scp)
                        if vp3 > 0:
                            sy.wait_ge(semh[s_ve.name], vp3)
                        inc(sy.dma_start(out=xio[:, (it % 2) * D:(it % 2 + 1) * D],
                                         in_=x2d[it * 128:(it + 1) * 128, :]),
                            psem, 16)
                    E("sync", _lx3)
                    bump(psem, 16)
                    wait("scalar", psem)

                    def _3ap(sc, it=it, mmp3=mmp3):
                        if mmp3 > 0:
                            sc.wait_ge(semh[s_mm.name], mmp3)
                        c = it * 12
                        inc(sc.activation(
                            out=xhat_tm[:, (it % 2) * D:(it % 2 + 1) * D],
                            in_=xio[:, (it % 2) * D:(it % 2 + 1) * D],
                            func=AF.Identity, scale=stat[:, c + 0:c + 1],
                            bias=stat[:, c + 1:c + 2]), s_sc)
                    E("scalar", _3ap)
                    bump(s_sc)
                    st8[("l3sc", k % 2)] = tgt(s_sc)
                    sc_now = tgt(s_sc)
                    vp = st8.get("p1evac", 0)

                    def _tp3(t, it=it, sc_now=sc_now, vp=vp):
                        t.wait_ge(semh[s_sc.name], sc_now)
                        if vp > 0:
                            t.wait_ge(semh[s_ve.name], vp)
                        last = None
                        for c in range(DC):
                            last = t.transpose(ps_t[:, c * 128:(c + 1) * 128],
                                               xhat_tm[:, (it % 2) * D + c * 128:
                                                       (it % 2) * D + (c + 1) * 128],
                                               eye[:, :])
                        inc(last, s_mm)
                    E("tensor", _tp3)
                    bump(s_mm)
                    st8[("xtm_mm", it % 2)] = tgt(s_mm)
                    wait("vector", s_mm)

                    def _ev3(v, it=it):
                        dst = fmview(xhat_fm[:, :], DC, L, it * 128, 128)
                        inc(v.tensor_copy(dst, ps_t[:, :].rearrange(
                            "p (c n) -> p c n", n=128)), s_ve)
                    E("vector", _ev3)
                    bump(s_ve)
                    st8["p1evac"] = tgt(s_ve)

                wait("tensor", s_ve)
                for hc in range(24):
                    buf = hc % 2
                    mmp = st8.get(("gumm", buf), 0)

                    wsem = s_wh if hc % 2 == 0 else s_w2

                    def _wgl(sy, hc=hc, buf=buf, mmp=mmp, cvw=CV["wu"],
                             wsem=wsem):
                        sy.wait_ge(semh[s_wg.name], cvw)
                        if mmp > 0:
                            sy.wait_ge(semh[s_mm.name], mmp)
                        inc(sy.dma_start(
                            out=u64b[:, WG0 + buf * 1024:WG0 + (buf + 1) * 1024],
                            in_=bass.AP(wgB.tensor, hc * 128,
                                        [[HID, 128], [128 * HID, DC], [1, 128]])),
                            wsem, 16)
                        inc(sy.dma_start(
                            out=u64b[:, WU0 + buf * 1024:WU0 + (buf + 1) * 1024],
                            in_=bass.AP(wuB.tensor, hc * 128,
                                        [[HID, 128], [128 * HID, DC], [1, 128]])),
                            wsem, 16)
                    E("sync", _wgl)
                    bump(wsem, 32)
                    wait("tensor", wsem)
                    vep = st8.get(("guve", hc % 2), 0)

                    def _gu(t, ic=ic, hc=hc, buf=buf, vep=vep):
                        bg = ps[2 * (hc % 2)]
                        bu = ps[2 * (hc % 2) + 1]
                        if vep > 0:
                            t.wait_ge(semh[s_ve.name], vep)
                        last = None
                        for kc in range(DC):
                            rhs = xhat_fm[:, kc * L + ic * 512:kc * L + (ic + 1) * 512]
                            t.matmul(bg[:, :],
                                     u64b[:, WG0 + buf * 1024 + kc * 128:
                                          WG0 + buf * 1024 + (kc + 1) * 128],
                                     rhs, start=(kc == 0), stop=(kc == DC - 1))
                            last = t.matmul(bu[:, :],
                                            u64b[:, WU0 + buf * 1024 + kc * 128:
                                                 WU0 + buf * 1024 + (kc + 1) * 128],
                                            rhs, start=(kc == 0), stop=(kc == DC - 1))
                        inc(last, s_mm)
                    E("tensor", _gu)
                    bump(s_mm)
                    st8[("gumm", buf)] = tgt(s_mm)
                    wait("scalar", s_mm)

                    hmp = st8.get("hmve", 0)

                    def _si(sc, hc=hc, hmp=hmp):
                        if hmp > 0:
                            sc.wait_ge(semh[s_ve.name], hmp)
                        bg = ps[2 * (hc % 2)]
                        inc(sc.activation(out=gusb[:, :], in_=bg[:, :], func=AF.Silu,
                                          scale=1.0 / 127.0), s_sc)
                    E("scalar", _si)
                    bump(s_sc)
                    wait("vector", s_sc)

                    def _hm(v, hc=hc):
                        bu = ps[2 * (hc % 2) + 1]
                        inc(v.tensor_tensor(
                            u74[:, HFM + hc * 512:HFM + (hc + 1) * 512],
                            bu[:, :], gusb[:, :], op=OP.mult), s_ve)
                    E("vector", _hm)
                    bump(s_ve)
                    st8[("guve", hc % 2)] = tgt(s_ve)
                    st8["hmve"] = tgt(s_ve)

                if b == 0 and ic == 1 and "hfm1" in dbg:
                    wait("gpsimd", s_ve)

                    def _dh(sy):
                        inc(sy.dma_start(out=dbg["hfm1"][:, :],
                                         in_=u74[:, HFM:HFM + 24 * 512]), s_gb, 16)
                    E("gpsimd", _dh)
                    bump(s_gb, 16)
                wait("tensor", s_wl, WL_WD)
                for k in range(4):
                    it = 4 * ic + k
                    scp = st8.get("dnsc", 0)

                    def _dn(t, k=k, scp=scp):
                        if scp > 0:
                            t.wait_ge(semh[s_sc.name], scp)
                        last = None
                        for hc in range(24):
                            lhs = u74[:, HFM + hc * 512 + k * 128:
                                      HFM + hc * 512 + (k + 1) * 128]
                            t.matmul(ps[4][:, :], lhs,
                                     u64b[:, WD0 + hc * D:WD0 + hc * D + 512],
                                     start=(hc == 0), stop=(hc == 23))
                            last = t.matmul(ps[5][:, :], lhs,
                                            u64b[:, WD0 + hc * D + 512:
                                                 WD0 + (hc + 1) * D],
                                            start=(hc == 0), stop=(hc == 23))
                        inc(last, s_mm)
                    E("tensor", _dn)
                    bump(s_mm)
                    wait("scalar", s_mm)
                    wait("scalar", s_ve)
                    wait("vector", s_mm)
                    cc = 192 + (k % 2) * 24

                    def _4sq(sc, cc=cc):
                        sc.activation(out=lnsc[:, 0:512], in_=ps[4][:, :],
                                      func=AF.Square,
                                      accum_out=stat[:, cc + 2:cc + 3])
                        inc(sc.activation(out=lnsc[:, 0:512], in_=ps[5][:, :],
                                          func=AF.Square,
                                          accum_out=stat[:, cc + 3:cc + 4]), s_sc)
                    E("scalar", _4sq)
                    bump(s_sc)
                    wait("vector", s_sc)

                    def _4ns(v, cc=cc):
                        v.reduce_sum(stat[:, cc + 4:cc + 5], ps[4][:, :], axis=AX,
                                     negate=True)
                        inc(v.reduce_sum(stat[:, cc + 5:cc + 6], ps[5][:, :],
                                         axis=AX, negate=True), s_ve)
                    E("vector", _4ns)
                    bump(s_ve)
                    wait("vector", s_sc)

                    def _4var(v, cc=cc):
                        v.tensor_add(stat[:, cc + 6:cc + 7], stat[:, cc + 2:cc + 3],
                                     stat[:, cc + 3:cc + 4])
                        v.tensor_add(stat[:, cc + 7:cc + 8], stat[:, cc + 4:cc + 5],
                                     stat[:, cc + 5:cc + 6])
                        v.drain()
                        v.tensor_scalar(stat[:, cc + 8:cc + 9],
                                        stat[:, cc + 7:cc + 8], 1.0 / D, None,
                                        op0=OP.mult)
                        v.tensor_scalar(stat[:, cc + 9:cc + 10],
                                        stat[:, cc + 6:cc + 7], 1.0 / D, None,
                                        op0=OP.mult)
                        v.drain()
                        v.tensor_tensor(stat[:, cc + 10:cc + 11],
                                        stat[:, cc + 8:cc + 9],
                                        stat[:, cc + 8:cc + 9], op=OP.mult)
                        v.drain()
                        inc(v.tensor_sub(stat[:, cc + 11:cc + 12],
                                         stat[:, cc + 9:cc + 10],
                                         stat[:, cc + 10:cc + 11]), s_ve)
                    E("vector", _4var)
                    bump(s_ve)
                    wait("scalar", s_ve)

                    def _4rs(sc, cc=cc):
                        sc.activation(out=stat[:, cc + 12:cc + 13],
                                      in_=stat[:, cc + 11:cc + 12], func=AF.Ln,
                                      bias=ccol[:, 1:2])
                        inc(sc.activation(out=stat[:, cc + 0:cc + 1],
                                          in_=stat[:, cc + 12:cc + 13], func=AF.Exp,
                                          scale=-0.5), s_sc)
                    E("scalar", _4rs)
                    bump(s_sc)
                    wait("vector", s_sc)

                    def _4b(v, cc=cc):
                        inc(v.tensor_tensor(stat[:, cc + 1:cc + 2],
                                            stat[:, cc + 8:cc + 9],
                                            stat[:, cc + 0:cc + 1], op=OP.mult),
                            s_ve)
                    E("vector", _4b)
                    bump(s_ve)
                    wait("scalar", s_ve)

                    def _4ap(sc, cc=cc):
                        sc.activation(out=lnsc[:, 0:512], in_=ps[4][:, :],
                                      func=AF.Identity, scale=stat[:, cc + 0:cc + 1],
                                      bias=stat[:, cc + 1:cc + 2])
                        inc(sc.activation(out=lnsc[:, 512:1024], in_=ps[5][:, :],
                                          func=AF.Identity,
                                          scale=stat[:, cc + 0:cc + 1],
                                          bias=stat[:, cc + 1:cc + 2]), s_sc)
                    E("scalar", _4ap)
                    bump(s_sc)
                    st8["dnsc"] = tgt(s_sc)
                    if b == 0 and ic == 1 and "ffn_d" in dbg:
                        wait("gpsimd", s_sc)

                        def _df(sy, k=k):
                            inc(sy.dma_start(out=dbg["ffn_d"][k, :, :],
                                             in_=lnsc[:, 0:D]), s_gb, 16)
                        E("gpsimd", _df)
                        bump(s_gb, 16)
                        wait("vector", s_gb)

                    vp4 = st8.get(("x3ve", it % 2), 0)
                    scp4 = st8.get(("l3sc", it % 2), 0)

                    psem = s_x2 if it % 2 == 0 else s_x3

                    def _lx4(sy, it=it, vp4=vp4, scp4=scp4, psem=psem):
                        if vp4 > 0:
                            sy.wait_ge(semh[s_ve.name], vp4)
                        if scp4 > 0:
                            sy.wait_ge(semh[s_sc.name], scp4)
                        inc(sy.dma_start(out=xio[:, (it % 2) * D:(it % 2 + 1) * D],
                                         in_=x2d[it * 128:(it + 1) * 128, :]),
                            psem, 16)
                    E("sync", _lx4)
                    bump(psem, 16)
                    wait("vector", psem)
                    wait("vector", s_sc)
                    stp = st8.get(("x3st", it % 2), 0)
                    if stp > 0:
                        wait("vector", s_st, stp)

                    def _x3(v, it=it):
                        inc(v.tensor_add(outsb[:, (it % 2) * D:(it % 2 + 1) * D],
                                         xio[:, (it % 2) * D:(it % 2 + 1) * D],
                                         lnsc[:, 0:D]), s_ve)
                    E("vector", _x3)
                    bump(s_ve)
                    st8[("x3ve", it % 2)] = tgt(s_ve)
                    wait("sync", s_ve)

                    def _so(sy, b=b, it=it):
                        inc(sy.dma_start(out=out[b, it * 128:(it + 1) * 128, :],
                                         in_=outsb[:, (it % 2) * D:(it % 2 + 1) * D]),
                            s_st, 16)
                    E("sync", _so)
                    bump(s_st, 16)
                    st8[("x3st", it % 2)] = tgt(s_st)
                ckpt(f"ffn_{b}_{ic}")

        wait("sync", s_st)

        block = stack.enter_context(nc.Block())

        def run(eng_name):
            def body(e):
                for fn in prog[eng_name]:
                    fn(e)
            return body
        block.sync(run("sync"))
        block.gpsimd(run("gpsimd"))
        block.tensor(run("tensor"))
        block.scalar(run("scalar"))
        block.vector(run("vector"))

    return nc


def shard_inputs(L0_attn_x, L0_attn_Wq, L0_attn_Wk, L0_attn_Wv, L0_attn_Wo,
                 L0_attn_biases, L0_attn_tau, L0_ffn_w_gate, L0_ffn_w_up,
                 L0_ffn_w_down):
    shared = {
        "wqT": np.ascontiguousarray(np.asarray(L0_attn_Wq).T),
        "wkT": np.ascontiguousarray(np.asarray(L0_attn_Wk).T),
        "wvT": np.ascontiguousarray(np.asarray(L0_attn_Wv).T),
        "woT": np.ascontiguousarray(np.asarray(L0_attn_Wo).T),
        "wgT": np.ascontiguousarray(np.asarray(L0_ffn_w_gate).T),
        "wuT": np.ascontiguousarray(np.asarray(L0_ffn_w_up).T),
        "wdT": np.ascontiguousarray(np.asarray(L0_ffn_w_down).T),
        "gbt": host_gb_tiles(np.asarray(L0_attn_biases)),
        "biasd": np.ascontiguousarray(np.asarray(L0_attn_biases, dtype=np.float32)),
        "taud": np.ascontiguousarray(
            np.asarray(L0_attn_tau, dtype=np.float32)[None, :]),
    }
    shared.update(host_constants())
    x = np.ascontiguousarray(np.asarray(L0_attn_x, dtype=np.float32))
    return [dict(shared, xin=x[2 * c:2 * c + 2]) for c in range(8)]


def kernel(**inputs):
    maps = shard_inputs(**inputs)
    nc = build_program()
    res = run_bass_kernel_spmd(nc, maps, core_ids=list(range(8)))
    return np.concatenate([r["out"] for r in res.results], axis=0)


if __name__ == "__main__":
    nc = build_program()
    print("built ok,", nc.n_instructions() if hasattr(nc, "n_instructions") else "")


# version 08
# speedup vs baseline: 1.1919x; 1.0041x over previous
import contextlib
import os
import numpy as np
import ml_dtypes

import concourse.bass as bass
import concourse.mybir as mybir
from concourse.bass_utils import run_bass_kernel_spmd

F32 = mybir.dt.float32
BF16 = mybir.dt.bfloat16
I32 = mybir.dt.int32
AX = mybir.AxisListType.X
OP = mybir.AluOpType
AF = mybir.ActivationFunctionType
bf16 = ml_dtypes.bfloat16

B, L, D, H, HD, HID = 2, 2048, 1024, 16, 64, 3072
TT, DC, NG, NP = 16, 8, 4, 8
NBLK, PPB = 4, 2
BW = PPB * 128
EPS = 1e-5
EPS4 = 1e-5 * float(127 ** 4)
QK_SCALE = 0.125
GB_OFFS = [-384, -256, -128, 0, 128, 256, 384, 512]

DEBUG_TAPS = ()


def host_constants():
    t = np.arange(L, dtype=np.float64)
    invf = 10000.0 ** (-np.arange(0, HD, 2, dtype=np.float64) / HD)
    emb = np.concatenate([t[:, None] * invf[None, :]] * 2, axis=1)
    cos, sin = np.cos(emb), np.sin(emb)

    def tmaj(a):
        return np.ascontiguousarray(
            a.reshape(TT, 128, HD).transpose(1, 0, 2).reshape(128, TT * HD)
        ).astype(bf16)

    sin_s = sin.copy()
    sin_s[:, :HD // 2] *= -1.0
    return {
        "c_cosq": tmaj(cos),
        "c_sinq": tmaj(sin_s),
        "c_eye": np.eye(128, dtype=np.float32).astype(bf16),
        "c_eye8": (8.0 * np.eye(128, dtype=np.float32)).astype(bf16),
        "c_onesT": np.ones((128, 64), dtype=np.float32).astype(bf16),
        "c_onesrow": np.ones((64, 128), dtype=np.float32).astype(bf16),
        "c_negrow": np.full((64, 128), -1.0, dtype=np.float32).astype(bf16),
        "c_recipbc": np.ascontiguousarray(np.broadcast_to(
            1.0 / (np.arange(L, dtype=np.float64) + 1.0), (64, L))
        ).astype(np.float32),
    }


CONST_SPECS = [
    ("c_cosq", [128, TT * HD], BF16), ("c_sinq", [128, TT * HD], BF16),
    ("c_eye", [128, 128], BF16), ("c_eye8", [128, 128], BF16),
    ("c_onesT", [128, 64], BF16),
    ("c_onesrow", [64, 128], BF16), ("c_negrow", [64, 128], BF16),
    ("c_recipbc", [64, L], F32),
]


def host_gb_tiles(biases):
    base = 1024
    ext = np.zeros((H, 4096), dtype=np.float32)
    ext[:, :base] = -1e30
    ext[:, base:base + 513] = np.asarray(biases, dtype=np.float32)
    oi = np.asarray(GB_OFFS)[:, None, None]
    j = np.arange(128)[None, :, None]
    i = np.arange(512)[None, None, :]
    return np.ascontiguousarray(ext[:, base + oi + i - j])


def gb_off_index(g, jt):
    off = 512 * g - 128 * jt
    return GB_OFFS.index(off) if off <= 512 + 127 else None


def build_program():
    nc = bass.Bass("TRN2", target_bir_lowering=False,
                   detect_race_conditions=False)

    xin = nc.dram_tensor("xin", [B, L, D], F32, kind="ExternalInput").ap()
    wqT = nc.dram_tensor("wqT", [D, D], F32, kind="ExternalInput").ap()
    wkT = nc.dram_tensor("wkT", [D, D], F32, kind="ExternalInput").ap()
    wvT = nc.dram_tensor("wvT", [D, D], F32, kind="ExternalInput").ap()
    woT = nc.dram_tensor("woT", [D, D], F32, kind="ExternalInput").ap()
    wgT = nc.dram_tensor("wgT", [D, HID], I32, kind="ExternalInput").ap()
    wuT = nc.dram_tensor("wuT", [D, HID], I32, kind="ExternalInput").ap()
    wdT = nc.dram_tensor("wdT", [HID, D], I32, kind="ExternalInput").ap()
    gbt = nc.dram_tensor("gbt", [H, 8, 128, 512], F32, kind="ExternalInput").ap()
    biasd = nc.dram_tensor("biasd", [H, 513], F32, kind="ExternalInput").ap()
    taud = nc.dram_tensor("taud", [1, H], F32, kind="ExternalInput").ap()
    cn = {}
    for name, shape, dt in CONST_SPECS:
        cn[name] = nc.dram_tensor(name, shape, dt, kind="ExternalInput").ap()

    out = nc.dram_tensor("out", [B, L, D], F32, kind="ExternalOutput").ap()
    x2d = nc.dram_tensor("x2d", [L, D], F32, kind="Internal").ap()
    wqB = nc.dram_tensor("wqB", [D * D], BF16, kind="Internal").ap()
    wkB = nc.dram_tensor("wkB", [D * D], BF16, kind="Internal").ap()
    wvB = nc.dram_tensor("wvB", [D * D], BF16, kind="Internal").ap()
    woB = nc.dram_tensor("woB", [D * D], BF16, kind="Internal").ap()
    wgB = nc.dram_tensor("wgB", [D * HID], BF16, kind="Internal").ap()
    wuB = nc.dram_tensor("wuB", [D * HID], BF16, kind="Internal").ap()
    wdB = nc.dram_tensor("wdB", [HID * D], BF16, kind="Internal").ap()
    gbB = nc.dram_tensor("gbB", [H * 8 * 128 * 512], BF16, kind="Internal").ap()
    dbg = {}
    dbg_shapes = {"stat1": [128, 256], "xraw": [128, 14336],
                  "xhatfm": [128, DC * L], "qfm0": [128, PPB * L],
                  "kfm0": [128, PPB * L], "vtm0": [128, TT * BW],
                  "attnfm": [128, DC * L], "x2": [L, D], "serow00": [64, 512],
                  "phat00": [128, 16 * 512], "hfm1": [128, 24 * 512],
                  "ffn_d": [4, 128, D]}
    for name in DEBUG_TAPS:
        dbg[name] = nc.dram_tensor("dbg_" + name, dbg_shapes[name], F32,
                                   kind="ExternalOutput").ap()

    ENGS = ("sync", "gpsimd", "tensor", "scalar", "vector")
    prog = {k: [] for k in ENGS}
    cnt = {}
    semh = {}

    class Sem:
        def __init__(self, name):
            self.name = name

    (s_wl, s_wg, s_xl, s_gb, s_mm, s_sc, s_ve, s_st, s_wh, s_x2, s_x3,
     s_w2) = (Sem(n) for n in (
         "s_wl", "s_wg", "s_xl", "s_gb", "s_mm", "s_sc", "s_ve", "s_st",
         "s_wh", "s_x2", "s_x3", "s_w2"))
    ALL_SEMS = [s_wl, s_wg, s_xl, s_gb, s_mm, s_sc, s_ve, s_st, s_wh, s_x2,
                s_x3, s_w2]

    KSTOP = os.environ.get("KSTOP", "")
    stopped = [False]

    def ckpt(name):
        if KSTOP and name == KSTOP:
            stopped[0] = True

    def E(eng, fn):
        if not stopped[0]:
            prog[eng].append(fn)

    def inc(instr, sr, k=1):
        instr.then_inc(semh[sr.name], k)

    def bump(sr, k=1):
        cnt[sr.name] = cnt.get(sr.name, 0) + k
        return cnt[sr.name]

    def tgt(sr):
        return cnt.get(sr.name, 0)

    def wait(eng, sr, n=None):
        n = tgt(sr) if n is None else n
        if n > 0:
            E(eng, lambda e, s=sr, n=n: e.wait_ge(semh[s.name], n))

    with contextlib.ExitStack() as stack:
        def sbuf(name, shape, dtype):
            return stack.enter_context(nc.sbuf_tensor(name, shape, dtype))

        def psum(name, shape, dtype=F32):
            return stack.enter_context(nc.psum_tensor(name, shape, dtype))

        eye = sbuf("eye", [128, 128], BF16)
        eye8 = sbuf("eye8", [128, 128], BF16)
        onesT = sbuf("onesT", [128, 64], BF16)
        onesrow = sbuf("onesrow", [64, 128], BF16)
        negrow = sbuf("negrow", [64, 128], BF16)
        cosq = sbuf("cosq", [128, TT * HD], BF16)
        sinq = sbuf("sinq", [128, TT * HD], BF16)
        recipbc = sbuf("recipbc", [64, L], F32)
        taurow = sbuf("taurow", [1, H], F32)
        taupair = sbuf("taupair", [64, NP], F32)
        biassb = sbuf("biassb", [16, 513], F32)
        maxgbrow = sbuf("maxgbrow", [1, 128], F32)
        colmb = sbuf("colmb", [128, 1], BF16)

        xhat_fm = sbuf("xhat_fm", [128, DC * L], BF16)
        u64f = sbuf("u64f", [128, 14 * 1024], F32)
        u64b = u64f[:, :].bitcast(BF16)
        PHA, PHB = 0, 8192
        QFM, KFM, VTM = 16384, 20480, 24576
        WD0, WG0, WU0 = 0, 24576, 26624
        u74 = sbuf("u74", [128, 31744], BF16)
        AFM, GBS = 0, 16384
        WQS, WKS, WVS = 24576, 26624, 28672
        CTA, CTB, WOS, HFM = 30720, 31232, 16384, 0

        serow = sbuf("serow", [64, 512], F32)
        csrow = sbuf("csrow", [64, 512], F32)
        rowb = sbuf("rowb", [64, 1536], BF16)
        vpre = sbuf("vpre", [64, (NG + 1) * 128], F32)
        vtailf = sbuf("vtailf", [64, NG * 128], F32)
        mrow = sbuf("mrow", [1, 32], F32)
        ccol = sbuf("ccol", [128, 4], F32)
        colb = sbuf("colb", [128, 4], BF16)
        negmc = sbuf("negmc", [128, 1], F32)
        xio = sbuf("xio", [128, 2 * D], F32)
        xhat_tm = sbuf("xhat_tm", [128, 2 * D], BF16)
        lnsc = sbuf("lnsc", [128, D], F32)
        stat = sbuf("stat", [128, 256], F32)
        ropeA = sbuf("ropeA", [128, BW], F32)
        ropeB = sbuf("ropeB", [128, BW], F32)
        outsb = sbuf("outsb", [128, 2 * D], F32)
        gusb = sbuf("gusb", [128, 512], BF16)
        ropeQ = sbuf("ropeQ", [128, BW], BF16)
        ropeK = sbuf("ropeK", [128, BW], BF16)

        ps = [psum(f"ps{i}", [128, 512]) for i in range(6)]
        ps_r = psum("ps_r", [128, 512])
        ps_t = psum("ps_t", [128, 1024], BF16)
        ps_tf = ps_t[:, :].bitcast(F32)

        for sr in ALL_SEMS:
            semh[sr.name] = stack.enter_context(nc.semaphore(sr.name))

        def fmview(base_ap, nchunk, chunklen, sub0, subn):
            v = base_ap.rearrange("p (c m) -> p c m", m=chunklen)
            return v[:, 0:nchunk, sub0:sub0 + subn]

        CONST_DSTS = [("c_cosq", cosq), ("c_sinq", sinq),
                      ("c_recipbc", recipbc), ("c_eye", eye),
                      ("c_eye8", eye8), ("c_onesT", onesT),
                      ("c_onesrow", onesrow), ("c_negrow", negrow)]

        def _consts(sy):
            for name, dst in CONST_DSTS:
                inc(sy.dma_start(out=dst[:, :], in_=cn[name][:, :]), s_wl, 16)
            inc(sy.dma_start(out=taurow[:, :], in_=taud[:, :]), s_wl, 16)
            inc(sy.dma_start(out=biassb[:, :], in_=biasd[:, :]), s_wl, 16)
        E("sync", _consts)
        bump(s_wl, 16 * (len(CONST_DSTS) + 2))

        def _taus(gp):
            for p in range(NP):
                inc(gp.dma_start(
                    out=taupair[0:32, p:p + 1],
                    in_=taud[0:1, 2 * p:2 * p + 1].partition_broadcast(32)),
                    s_wg, 16)
                inc(gp.dma_start(
                    out=taupair[32:64, p:p + 1],
                    in_=taud[0:1, 2 * p + 1:2 * p + 2].partition_broadcast(32)),
                    s_wg, 16)
        E("gpsimd", _taus)
        bump(s_wg, 16 * 16)

        GBCH = 8 * 128 * 512
        CONV_SEQ = [("wq", wqT, wqB, 1), ("wk", wkT, wkB, 1),
                    ("wv", wvT, wvB, 1)]
        CONV_SEQ += [(("gb", h), gbt, gbB, None) for h in range(H)]
        CONV_SEQ += [("wo", woT, woB, 1), ("wd", wdT, wdB, 4),
                     ("wg", wgT, wgB, 4), ("wu", wuT, wuB, 4)]
        CV = {}

        cv_gates = {"wv", ("gb", 1), ("gb", 15)}

        def _conv(gp, base=tgt(s_wg)):
            cum = base
            for key, src, dst, nch in CONV_SEQ:
                n = int(np.prod(src.tensor.shape))
                if nch is None:
                    h = key[1]
                    inc(gp.dma_start(out=dst[h * GBCH:(h + 1) * GBCH],
                                     in_=bass.AP(src.tensor, h * GBCH,
                                                 [[1, GBCH]])), s_wg, 16)
                    cum += 16
                else:
                    c = n // nch
                    for j in range(nch):
                        inc(gp.dma_start(out=dst[j * c:(j + 1) * c],
                                         in_=bass.AP(src.tensor, j * c,
                                                     [[1, c]])), s_wg, 16)
                    cum += 16 * nch
                if key in cv_gates:
                    gp.wait_ge(semh[s_wg.name], cum)
        E("gpsimd", _conv)
        for key, src, dst, nch in CONV_SEQ:
            bump(s_wg, 16 * (1 if nch is None else nch))
            CV[key] = tgt(s_wg)
        wait("vector", s_wl)
        wait("vector", s_wg, 16 * 16)

        def _maxgb(v):
            v.drain()
            v.memset(ccol[:, 0:1], float(EPS))
            v.memset(ccol[:, 1:2], float(EPS4))
            v.memset(ccol[:, 2:3], 1e-38)
            v.memset(ccol[:, 3:4], -2.0794415416798357)
            v.memset(colmb[:, :], 0.0)
            v.reduce_max(stat[0:16, 0:1], biassb[:, :], axis=AX,
                         apply_absolute_value=True)
            v.drain()
            inc(v.tensor_copy(colmb[0:16, 0:1], stat[0:16, 0:1]), s_ve)
        E("vector", _maxgb)
        bump(s_ve)
        wait("tensor", s_ve)

        def _maxgbt(t):
            inc(t.transpose(ps_t[0:1, 0:128], colmb[:, :], eye[:, :]), s_mm)
        E("tensor", _maxgbt)
        bump(s_mm)
        wait("vector", s_mm)

        def _maxgbr(v):
            inc(v.tensor_copy(maxgbrow[0:1, 0:128], ps_t[0:1, 0:128]), s_ve)
        E("vector", _maxgbr)
        bump(s_ve)
        st8_init_p1evac = tgt(s_ve)

        def ln_tail(colbase, idxs, w, eps):
            wait("vector", s_sc)

            def _var(v, idxs=tuple(idxs), w=w, colbase=colbase):
                n = len(idxs)
                for k in range(n):
                    c = colbase + k * 12
                    v.tensor_scalar(stat[:, c + 4:c + 5], stat[:, c + 3:c + 4],
                                    1.0 / w, None, op0=OP.mult)
                    v.tensor_scalar(stat[:, c + 5:c + 6], stat[:, c + 2:c + 3],
                                    1.0 / w, None, op0=OP.mult)
                if n < 4:
                    v.drain()
                for k in range(n):
                    c = colbase + k * 12
                    v.tensor_tensor(stat[:, c + 6:c + 7], stat[:, c + 4:c + 5],
                                    stat[:, c + 4:c + 5], op=OP.mult)
                if n < 4:
                    v.drain()
                last = None
                for k in range(n):
                    c = colbase + k * 12
                    last = v.tensor_sub(stat[:, c + 7:c + 8],
                                        stat[:, c + 5:c + 6], stat[:, c + 6:c + 7])
                inc(last, s_ve)
            E("vector", _var)
            bump(s_ve)
            wait("scalar", s_ve)

            def _rstd(sc, idxs=tuple(idxs), colbase=colbase,
                      eps_ap=(ccol[:, 0:1] if eps == EPS else ccol[:, 1:2])):
                last = None
                for k in range(len(idxs)):
                    c = colbase + k * 12
                    sc.activation(out=stat[:, c + 8:c + 9], in_=stat[:, c + 7:c + 8],
                                  func=AF.Ln, bias=eps_ap)
                    last = sc.activation(out=stat[:, c + 0:c + 1],
                                         in_=stat[:, c + 8:c + 9],
                                         func=AF.Exp, scale=-0.5)
                inc(last, s_sc)
            E("scalar", _rstd)
            bump(s_sc)
            wait("vector", s_sc)

            def _bias(v, idxs=tuple(idxs), colbase=colbase):
                last = None
                for k in range(len(idxs)):
                    c = colbase + k * 12
                    last = v.tensor_tensor(stat[:, c + 1:c + 2],
                                           stat[:, c + 4:c + 5],
                                           stat[:, c + 0:c + 1], op=OP.mult)
                inc(last, s_ve)
            E("vector", _bias)
            bump(s_ve)
            wait("scalar", s_ve)

        st8 = {}

        st8["p1evac"] = st8_init_p1evac
        for b in range(B):
            def xsrc(i):
                if i < 14:
                    return u64f[:, i * 1024:(i + 1) * 1024]
                return xio[:, (i - 14) * D:(i - 13) * D]

            wait("sync", s_mm)
            wait("sync", s_sc)
            wait("sync", s_ve)

            def _ldx(sy, b=b):
                last = None
                for i in range(TT):
                    last = sy.dma_start(out=xsrc(i),
                                        in_=xin[b, i * 128:(i + 1) * 128, :])
                    inc(last, s_xl, 16)
            E("sync", _ldx)
            bump(s_xl, 16 * TT)
            wait("scalar", s_xl)
            wait("vector", s_xl)

            def _sq1(sc):
                last = None
                for i in range(TT):
                    c = i * 12
                    last = sc.activation(out=lnsc[:, :], in_=xsrc(i), func=AF.Square,
                                         accum_out=stat[:, c + 2:c + 3])
                inc(last, s_sc)
            E("scalar", _sq1)
            bump(s_sc)

            def _ns1(v):
                last = None
                for i in range(TT):
                    c = i * 12
                    last = v.reduce_sum(stat[:, c + 3:c + 4], xsrc(i), axis=AX,
                                        negate=True)
                inc(last, s_ve)
            E("vector", _ns1)
            bump(s_ve)
            ln_tail(0, range(TT), D, EPS)

            for i in range(TT):
                mmp = st8.get(("xtm_mm", i % 2), 0)

                def _ap(sc, i=i, mmp=mmp):
                    if mmp > 0:
                        sc.wait_ge(semh[s_mm.name], mmp)
                    c = i * 12
                    inc(sc.activation(out=xhat_tm[:, (i % 2) * D:(i % 2 + 1) * D],
                                      in_=xsrc(i), func=AF.Identity,
                                      scale=stat[:, c + 0:c + 1],
                                      bias=stat[:, c + 1:c + 2]), s_sc)
                E("scalar", _ap)
                bump(s_sc)
                sc_now = tgt(s_sc)
                ve_prev_evac = st8.get("p1evac", 0)

                def _tp(t, i=i, sc_now=sc_now, vp=ve_prev_evac):
                    t.wait_ge(semh[s_sc.name], sc_now)
                    if vp > 0:
                        t.wait_ge(semh[s_ve.name], vp)
                    last = None
                    for c in range(DC):
                        last = t.transpose(ps_t[:, c * 128:(c + 1) * 128],
                                           xhat_tm[:, (i % 2) * D + c * 128:
                                                   (i % 2) * D + (c + 1) * 128],
                                           eye[:, :])
                    inc(last, s_mm)
                E("tensor", _tp)
                bump(s_mm)
                st8[("xtm_mm", i % 2)] = tgt(s_mm)
                wait("vector", s_mm)

                def _ev(v, i=i):
                    dst = fmview(xhat_fm[:, :], DC, L, i * 128, 128)
                    inc(v.tensor_copy(dst, ps_t[:, :].rearrange(
                        "p (c n) -> p c n", n=128)), s_ve)
                E("vector", _ev)
                bump(s_ve)
                st8["p1evac"] = tgt(s_ve)
            P1_EVAC_ALL = tgt(s_ve)
            P1_APPLY_ALL = tgt(s_sc)

            if b == 0 and ("stat1" in dbg or "xraw" in dbg):
                wait("gpsimd", s_ve)
                wait("gpsimd", s_sc)

                def _dst(sy):
                    if "stat1" in dbg:
                        inc(sy.dma_start(out=dbg["stat1"][:, :], in_=stat[:, :]),
                            s_gb, 16)
                    if "xraw" in dbg:
                        inc(sy.dma_start(out=dbg["xraw"][:, :], in_=u64f[:, :]),
                            s_gb, 16)
                E("gpsimd", _dst)
                if "stat1" in dbg:
                    bump(s_gb, 16)
                if "xraw" in dbg:
                    bump(s_gb, 16)
            ckpt(f"p1_{b}")
            if b == 0 and "xhatfm" in dbg:
                wait("gpsimd", s_ve)

                def _d1(sy):
                    inc(sy.dma_start(out=dbg["xhatfm"][:, :], in_=xhat_fm[:, :]),
                        s_gb, 16)
                E("gpsimd", _d1)
                bump(s_gb, 16)

            for bi in range(NBLK):
                col0 = bi * BW
                wait("sync", s_mm)

                def _wblk(sy, col0=col0, cvw=CV["wv"]):
                    sy.wait_ge(semh[s_wg.name], cvw)
                    for wsrc, off in ((wqB, WQS), (wkB, WKS), (wvB, WVS)):
                        inc(sy.dma_start(
                            out=u74[:, off:off + DC * BW],
                            in_=bass.AP(wsrc.tensor, col0,
                                        [[D, 128], [128 * D, DC], [1, BW]])),
                            s_wl, 16)
                E("sync", _wblk)
                bump(s_wl, 48)
                wait("tensor", s_wl)
                wait("tensor", s_ve, P1_EVAC_ALL)
                wait("tensor", s_sc)

                for it in range(TT):
                    pq, pk, pv = ps[3 * (it % 2)], ps[3 * (it % 2) + 1], ps[3 * (it % 2) + 2]
                    vp = st8.get(("projve", it % 2), 0)

                    def _proj(t, it=it, pq=pq, pk=pk, pv=pv, vp=vp):
                        if vp > 0:
                            t.wait_ge(semh[s_ve.name], vp)
                        last = None
                        for kc in range(DC):
                            lhs = xhat_fm[:, kc * L + it * 128:
                                          kc * L + (it + 1) * 128]
                            t.matmul(pq[:, 0:BW], lhs,
                                     u74[:, WQS + kc * BW:WQS + (kc + 1) * BW],
                                     start=(kc == 0), stop=(kc == DC - 1))
                            t.matmul(pk[:, 0:BW], lhs,
                                     u74[:, WKS + kc * BW:WKS + (kc + 1) * BW],
                                     start=(kc == 0), stop=(kc == DC - 1))
                            last = t.matmul(pv[:, 0:BW], lhs,
                                            u74[:, WVS + kc * BW:WVS + (kc + 1) * BW],
                                            start=(kc == 0), stop=(kc == DC - 1))
                        inc(last, s_mm)
                    E("tensor", _proj)
                    bump(s_mm)
                    wait("vector", s_mm)

                    def _rope(v, it=it, pq=pq, pk=pk, pv=pv):
                        for src, ctab, stab, dst in ((pq, cosq, sinq, ropeQ),
                                                     (pk, cosq, sinq, ropeK)):
                            i0 = src[:, 0:BW].rearrange("p (h k) -> p h k", k=HD)
                            tc = ctab[:, it * HD:(it + 1) * HD]
                            ts_ = stab[:, it * HD:(it + 1) * HD]
                            cbc = bass.AP(tc.tensor, tc.offset,
                                          [list(tc.ap[0]), [0, BW // HD], [1, HD]])
                            sbc0 = bass.AP(ts_.tensor, ts_.offset,
                                           [list(ts_.ap[0]), [0, BW // HD], [1, 32]])
                            sbc1 = bass.AP(ts_.tensor, ts_.offset + 32,
                                           [list(ts_.ap[0]), [0, BW // HD], [1, 32]])
                            v.tensor_tensor(
                                ropeA[:, :].rearrange("p (h k) -> p h k", k=HD),
                                i0, cbc, op=OP.mult)
                            ih = src[:, 0:BW].rearrange("p (h t k) -> p h t k",
                                                        t=2, k=32)
                            rh = ropeB[:, :].rearrange("p (h t k) -> p h t k",
                                                       t=2, k=32)
                            v.tensor_tensor(rh[:, :, 0, :], ih[:, :, 1, :], sbc0,
                                            op=OP.mult)
                            v.tensor_tensor(rh[:, :, 1, :], ih[:, :, 0, :], sbc1,
                                            op=OP.mult)
                            v.tensor_add(dst[:, :], ropeA[:, :], ropeB[:, :])
                        last = v.tensor_copy(
                            u64b[:, VTM + it * BW:VTM + (it + 1) * BW], pv[:, 0:BW])
                        inc(last, s_ve)
                    E("vector", _rope)
                    bump(s_ve)
                    st8[("projve", it % 2)] = tgt(s_ve)
                    wait("tensor", s_ve)

                    vp2 = st8.get("qkevac", 0)

                    def _tpqk(t, vp2=vp2):
                        if vp2 > 0:
                            t.wait_ge(semh[s_ve.name], vp2)
                        last = None
                        for c in range(PPB):
                            t.transpose(ps_t[:, c * 128:(c + 1) * 128],
                                        ropeQ[:, c * 128:(c + 1) * 128], eye[:, :])
                            last = t.transpose(ps_t[:, 256 + c * 128:256 + (c + 1) * 128],
                                               ropeK[:, c * 128:(c + 1) * 128],
                                               eye[:, :])
                        inc(last, s_mm)
                    E("tensor", _tpqk)
                    bump(s_mm)
                    wait("vector", s_mm)

                    def _evqk(v, it=it):
                        dq = fmview(u64b[:, QFM:QFM + PPB * L], PPB, L, it * 128, 128)
                        dk = fmview(u64b[:, KFM:KFM + PPB * L], PPB, L, it * 128, 128)
                        v.tensor_copy(dq, ps_t[:, 0:256].rearrange(
                            "p (c n) -> p c n", n=128))
                        inc(v.tensor_copy(dk, ps_t[:, 256:512].rearrange(
                            "p (c n) -> p c n", n=128)), s_ve)
                    E("vector", _evqk)
                    bump(s_ve)
                    st8["qkevac"] = tgt(s_ve)
                VE_BLOCK_QKV = tgt(s_ve)

                if b == 0 and bi == 0:
                    for nm, off, ln in (("qfm0", QFM, PPB * L), ("kfm0", KFM, PPB * L),
                                        ("vtm0", VTM, TT * BW)):
                        if nm in dbg:
                            wait("gpsimd", s_ve)

                            def _dq(sy, nm=nm, off=off, ln=ln):
                                inc(sy.dma_start(out=dbg[nm][:, :],
                                                 in_=u64b[:, off:off + ln]), s_gb, 16)
                            E("gpsimd", _dq)
                            bump(s_gb, 16)

                ckpt(f"proj_{b}_{bi}")
                wait("scalar", s_st)
                wait("tensor", s_st)
                for pp in range(PPB):
                    p = bi * PPB + pp
                    hA, hB = 2 * p, 2 * p + 1
                    qof = QFM + pp * L
                    kof = KFM + pp * L

                    wait("sync", s_mm)

                    def _gbl(sy, hA=hA, hB=hB, cvg=CV[("gb", hB)]):
                        sy.wait_ge(semh[s_wg.name], cvg)
                        for hh, half in ((hA, 0), (hB, 1)):
                            inc(sy.dma_start(
                                out=u74[:, GBS + half * 4096:GBS + (half + 1) * 4096],
                                in_=bass.AP(gbB.tensor, hh * 8 * 128 * 512,
                                            [[512, 128], [128 * 512, 8], [1, 512]])),
                                s_gb, 16)
                    E("sync", _gbl)
                    bump(s_gb, 32)
                    GB_THIS = tgt(s_gb)

                    wait("vector", s_ve, VE_BLOCK_QKV)
                    wait("scalar", s_ve, VE_BLOCK_QKV)
                    ovb = outsb[:, :].bitcast(BF16)
                    for mi, of in ((0, qof), (1, kof)):
                        scp_m = tgt(s_mm)
                        vep_m = tgt(s_ve)

                        def _msqr(sc, of=of, scp_m=scp_m, ovb=ovb):
                            if scp_m > 0:
                                sc.wait_ge(semh[s_mm.name], scp_m)
                            for c in range(4):
                                inc(sc.activation(out=ovb[:, c * 512:
                                                          (c + 1) * 512],
                                                  in_=u64b[:, of + c * 512:
                                                           of + (c + 1) * 512],
                                                  func=AF.Square), s_sc)
                        E("scalar", _msqr)
                        sq0 = tgt(s_sc)
                        bump(s_sc, 4)

                        ve0 = tgt(s_ve)

                        def _msum(t, vep_m=vep_m, sq0=sq0, ve0=ve0, ovb=ovb):
                            if vep_m > 0:
                                t.wait_ge(semh[s_ve.name], vep_m)
                            for c in range(4):
                                bank = [ps_tf, ps[0][:, :], ps[1][:, :],
                                        ps[2][:, :]][c]
                                t.wait_ge(semh[s_sc.name], sq0 + c + 1)
                                inc(t.matmul(bank[0:1, 0:512],
                                             onesT[:, 0:1],
                                             ovb[:, c * 512:(c + 1) * 512],
                                             start=True, stop=True), s_mm)
                        E("tensor", _msum)
                        mm0 = tgt(s_mm)
                        bump(s_mm, 4)

                        def _mred(v, mi=mi, mm0=mm0):
                            for c in range(4):
                                bank = [ps_tf, ps[0][:, :], ps[1][:, :],
                                        ps[2][:, :]][c]
                                v.wait_ge(semh[s_mm.name], mm0 + c + 1)
                                inc(v.reduce_max(stat[0:1, 240 + 4 * mi + c:
                                                      241 + 4 * mi + c],
                                                 bank[0:1, 0:512], axis=AX),
                                    s_ve)
                        E("vector", _mred)
                        bump(s_ve, 4)

                    def _mfin(v, hA=hA, hB=hB):
                        v.drain()
                        v.reduce_max(mrow[0:1, 0:1], stat[0:1, 240:244], axis=AX)
                        v.reduce_max(mrow[0:1, 1:2], stat[0:1, 244:248], axis=AX)
                        v.drain()
                        v.tensor_tensor(mrow[0:1, 2:3], mrow[0:1, 0:1],
                                        mrow[0:1, 1:2], op=OP.mult)
                        inc(v.tensor_tensor(mrow[0:1, 3:4],
                                            maxgbrow[0:1, hA:hA + 1],
                                            maxgbrow[0:1, hB:hB + 1], op=OP.max),
                            s_ve)
                    E("vector", _mfin)
                    bump(s_ve)
                    wait("scalar", s_ve)

                    def _msq(sc):
                        sc.activation(out=mrow[0:1, 4:5], in_=mrow[0:1, 2:3],
                                      func=AF.Ln, bias=ccol[0:1, 2:3])
                        inc(sc.activation(out=mrow[0:1, 5:6], in_=mrow[0:1, 4:5],
                                          func=AF.Exp, scale=0.5,
                                          bias=ccol[0:1, 3:4]), s_sc)
                    E("scalar", _msq)
                    bump(s_sc)
                    wait("vector", s_sc)

                    def _mneg(v):
                        v.tensor_add(mrow[0:1, 6:7], mrow[0:1, 5:6], mrow[0:1, 3:4])
                        v.drain()
                        v.tensor_scalar(mrow[0:1, 7:8], mrow[0:1, 6:7], -1.0, None,
                                        op0=OP.mult)
                        v.drain()
                        inc(v.tensor_copy(colb[0:1, 2:3], mrow[0:1, 7:8]), s_ve)
                    E("vector", _mneg)
                    bump(s_ve)
                    wait("tensor", s_ve)

                    def _mbc(t):
                        inc(t.matmul(ps_tf[:, 2:3], onesrow[0:1, :], colb[0:1, 2:3],
                                     start=True, stop=True), s_mm)
                    E("tensor", _mbc)
                    bump(s_mm)
                    wait("vector", s_mm)

                    def _mev(v):
                        inc(v.tensor_copy(negmc[:, :], ps_tf[:, 2:3]), s_ve)
                    E("vector", _mev)
                    bump(s_ve)
                    NEGM_VE = tgt(s_ve)


                    def _vch(t, pp=pp):
                        last = None
                        for g4 in range(NG):
                            for jj in range(4):
                                jt = 4 * g4 + jj
                                last = t.matmul(
                                    ps_r[0:33, g4 * 128:(g4 + 1) * 128],
                                    onesT[:, 0:33],
                                    u64b[:, VTM + jt * BW + pp * 128:
                                         VTM + jt * BW + (pp + 1) * 128],
                                    start=(jj == 0), stop=(jj == 3))
                        inc(last, s_mm)
                    E("tensor", _vch)
                    bump(s_mm)

                    wait("vector", s_mm)

                    def _vev(v):
                        v.tensor_copy(vpre[0:33, 128:256], ps_r[0:33, 0:128])
                        for g4 in range(1, NG):
                            v.tensor_copy(vpre[0:33, 0:128],
                                          ps_r[0:33, g4 * 128:(g4 + 1) * 128])
                            v.drain()
                            v.tensor_add(vpre[0:33, (g4 + 1) * 128:(g4 + 2) * 128],
                                         vpre[0:33, g4 * 128:(g4 + 1) * 128],
                                         vpre[0:33, 0:128])
                            v.drain()
                        last = None
                        for g4 in range(NG):
                            last = v.tensor_sub(
                                vtailf[0:33, g4 * 128:(g4 + 1) * 128],
                                vpre[0:33, NG * 128:(NG + 1) * 128],
                                vpre[0:33, (g4 + 1) * 128:(g4 + 2) * 128])
                        inc(last, s_ve)
                    E("vector", _vev)
                    bump(s_ve)

                    for g in range(NG):
                        JT = 4 * g + 4
                        i0 = g * 512

                        def phA(jt):
                            return u64b[:, PHA + jt * 512:PHA + (jt + 1) * 512]

                        def phB(jt):
                            return u64b[:, PHB + jt * 512:PHB + (jt + 1) * 512]

                        exp_base = tgt(s_sc)
                        prev_pv_sc = st8.get("grp_sc_ready", 0)

                        def _st(t, g=g, JT=JT, i0=i0, qof=qof, kof=kof,
                                exp_base=exp_base, gbn=GB_THIS, negve=NEGM_VE):
                            t.wait_ge(semh[s_gb.name], gbn)
                            t.wait_ge(semh[s_ve.name], negve)
                            for jt in range(JT):
                                if jt >= 2:
                                    t.wait_ge(semh[s_sc.name], exp_base + jt - 1)
                                bA = ps[2 * (jt % 2)]
                                bB = ps[2 * (jt % 2) + 1]
                                lA = u64b[0:64, kof + jt * 128:kof + (jt + 1) * 128]
                                lB = u64b[64:128, kof + jt * 128:kof + (jt + 1) * 128]
                                rA = u64b[0:64, qof + i0:qof + i0 + 512]
                                rB = u64b[64:128, qof + i0:qof + i0 + 512]
                                oi = gb_off_index(g, jt)
                                t.matmul(bA[:, :], lA, rA, start=True,
                                         stop=(oi is None), tile_position=(0, 0))
                                lastm = t.matmul(bB[:, :], lB, rB, start=True,
                                                 stop=(oi is None),
                                                 tile_position=(64, 0))
                                if oi is not None:
                                    t.matmul(bA[:, :], eye8[:, :],
                                             u74[:, GBS + oi * 512:
                                                  GBS + (oi + 1) * 512],
                                             start=False, stop=True)
                                    lastm = t.matmul(
                                        bB[:, :], eye8[:, :],
                                        u74[:, GBS + 4096 + oi * 512:
                                             GBS + 4096 + (oi + 1) * 512],
                                        start=False, stop=True)
                                inc(lastm, s_mm)
                                if jt >= 1:
                                    t.wait_ge(semh[s_sc.name], exp_base + jt)
                                    t.matmul(ps[4][0:32, :], onesT[:, 0:32],
                                             phA(jt - 1), start=(jt == 1),
                                             stop=False, tile_position=(0, 0))
                                    t.matmul(ps_r[32:64, :], onesT[:, 0:32],
                                             phB(jt - 1), start=(jt == 1),
                                             stop=False, tile_position=(0, 32))
                            t.wait_ge(semh[s_sc.name], exp_base + JT)
                            t.matmul(ps[4][0:32, :], onesT[:, 0:32], phA(JT - 1),
                                     start=(JT == 1), stop=True,
                                     tile_position=(0, 0))
                            inc(t.matmul(ps_r[32:64, :], onesT[:, 0:32], phB(JT - 1),
                                         start=(JT == 1), stop=True,
                                         tile_position=(0, 32)), s_mm)
                        E("tensor", _st)
                        bump(s_mm, JT + 1)
                        mm_st0 = tgt(s_mm) - JT - 1

                        def _exp(sc, JT=JT, mm0=mm_st0, scr=prev_pv_sc):
                            if scr > 0:
                                sc.wait_ge(semh[s_mm.name], scr)
                            for jt in range(JT):
                                sc.wait_ge(semh[s_mm.name], mm0 + jt + 1)
                                bA = ps[2 * (jt % 2)]
                                bB = ps[2 * (jt % 2) + 1]
                                sc.activation(out=phA(jt), in_=bA[:, :], func=AF.Exp,
                                              scale=QK_SCALE, bias=negmc[:, 0:1])
                                inc(sc.activation(out=phB(jt), in_=bB[:, :],
                                                  func=AF.Exp, scale=QK_SCALE,
                                                  bias=negmc[:, 0:1]), s_sc)
                        E("scalar", _exp)
                        bump(s_sc, JT)

                        wait("vector", s_mm)

                        def _rows1(v):
                            v.tensor_copy(serow[0:32, :], ps[4][0:32, :])
                            inc(v.tensor_copy(serow[32:64, :], ps_r[32:64, :]), s_ve)
                        E("vector", _rows1)
                        bump(s_ve)
                        wait("scalar", s_ve)

                        def _rows2(sc):
                            sc.activation(out=lnsc[0:64, 0:512], in_=serow[:, :],
                                          func=AF.Ln)
                            inc(sc.activation(out=lnsc[0:64, 0:512],
                                              in_=lnsc[0:64, 0:512],
                                              func=AF.Exp, scale=-1.0), s_sc)
                        E("scalar", _rows2)
                        bump(s_sc)
                        wait("vector", s_sc)

                        def _rows3(v, p=p, i0=i0):
                            v.tensor_scalar(csrow[:, :], recipbc[:, i0:i0 + 512],
                                            taupair[:, p:p + 1], None, op0=OP.mult)
                            v.tensor_relu(lnsc[0:64, 512:1024], csrow[:, :])
                            v.tensor_tensor(csrow[:, :], csrow[:, :], serow[:, :],
                                            op=OP.mult)
                            v.tensor_tensor(lnsc[0:64, 512:1024],
                                            lnsc[0:64, 512:1024], serow[:, :],
                                            op=OP.mult)
                            v.tensor_copy(rowb[0:1, 0:512], csrow[0:1, :])
                            v.tensor_copy(rowb[32:33, 0:512], csrow[32:33, :])
                            v.tensor_copy(rowb[0:1, 512:1024], lnsc[0:1, 512:1024])
                            v.tensor_copy(rowb[32:33, 512:1024],
                                          lnsc[32:33, 512:1024])
                            v.tensor_copy(rowb[0:1, 1024:1536], lnsc[0:1, 0:512])
                            v.tensor_copy(rowb[32:33, 1024:1536],
                                          lnsc[32:33, 0:512])
                            v.tensor_copy(csrow[0:1, :], rowb[0:1, 0:512])
                            v.tensor_copy(csrow[32:33, :], rowb[32:33, 0:512])
                            v.tensor_copy(lnsc[0:1, 512:1024], rowb[0:1, 512:1024])
                            inc(v.tensor_copy(lnsc[32:33, 512:1024],
                                              rowb[32:33, 512:1024]), s_ve)
                        E("vector", _rows3)
                        bump(s_ve)
                        wait("tensor", s_ve)

                        def _ct1(t):
                            t.matmul(ps_tf[:, :], negrow[0:1, :], rowb[0:1, 0:512],
                                     start=True, stop=True)
                            inc(t.matmul(ps[0][:, :], negrow[32:33, :],
                                         rowb[32:33, 0:512], start=True, stop=True),
                                s_mm)
                        E("tensor", _ct1)
                        bump(s_mm)
                        wait("vector", s_mm)

                        def _ce1(v):
                            v.tensor_copy(u74[:, CTA:CTA + 512], ps_tf[:, :])
                            inc(v.tensor_copy(u74[:, CTB:CTB + 512], ps[0][:, :]),
                                s_ve)
                        E("vector", _ce1)
                        bump(s_ve)

                        def _max(v, JT=JT):
                            last = None
                            for jt in range(JT):
                                v.tensor_tensor(phA(jt), phA(jt),
                                                u74[:, CTA:CTA + 512], op=OP.max)
                                last = v.tensor_tensor(phB(jt), phB(jt),
                                                       u74[:, CTB:CTB + 512],
                                                       op=OP.max)
                            inc(last, s_ve)
                        E("vector", _max)
                        bump(s_ve)
                        wait("tensor", s_ve)

                        def _pv(t, JT=JT, pp=pp, g=g):
                            for jt in range(JT):
                                vA = u64b[:, VTM + jt * BW + pp * 128:
                                          VTM + jt * BW + pp * 128 + 64]
                                vB = u64b[:, VTM + jt * BW + pp * 128 + 64:
                                          VTM + jt * BW + (pp + 1) * 128]
                                t.matmul(ps[4][0:64, :], vA, phA(jt),
                                         start=(jt == 0), stop=False,
                                         tile_position=(0, 0))
                                t.matmul(ps_r[64:128, :], vB, phB(jt),
                                         start=(jt == 0), stop=False,
                                         tile_position=(0, 64))
                            t.matmul(ps[4][0:64, :],
                                     vpre[0:1, (g + 1) * 128:(g + 1) * 128 + 64],
                                     csrow[0:1, :], start=False, stop=False,
                                     tile_position=(0, 0))
                            t.matmul(ps_r[64:128, :],
                                     vpre[32:33, (g + 1) * 128 + 64:(g + 2) * 128],
                                     csrow[32:33, :], start=False, stop=False,
                                     tile_position=(32, 64))
                            t.matmul(ps[4][0:64, :],
                                     vtailf[0:1, g * 128:g * 128 + 64],
                                     lnsc[0:1, 512:1024], start=False, stop=True,
                                     tile_position=(0, 0))
                            t.matmul(ps_r[64:128, :],
                                     vtailf[32:33, g * 128 + 64:(g + 1) * 128],
                                     lnsc[32:33, 512:1024], start=False, stop=True,
                                     tile_position=(32, 64))
                            t.matmul(ps_tf[0:64, :], onesrow[0:1, 0:64],
                                     rowb[0:1, 1024:1536], start=True, stop=True,
                                     tile_position=(0, 0))
                            inc(t.matmul(ps_tf[64:128, :], onesrow[32:33, 0:64],
                                         rowb[32:33, 1024:1536], start=True,
                                         stop=True, tile_position=(32, 64)), s_mm)
                        E("tensor", _pv)
                        bump(s_mm)
                        st8["grp_sc_ready"] = tgt(s_mm)
                        wait("vector", s_mm)

                        def _oev(v, p=p, i0=i0):
                            v.tensor_copy(lnsc[:, 0:512], ps_tf[:, 0:512])
                            v.tensor_tensor(
                                u74[:, AFM + p * L + i0:AFM + p * L + i0 + 512]
                                .rearrange("p n -> p n") if False else
                                u74[0:64, AFM + p * L + i0:AFM + p * L + i0 + 512],
                                ps[4][0:64, :], lnsc[0:64, 0:512], op=OP.mult)
                            inc(v.tensor_tensor(
                                u74[64:128, AFM + p * L + i0:AFM + p * L + i0 + 512],
                                ps_r[64:128, :], lnsc[64:128, 0:512], op=OP.mult),
                                s_ve)
                        E("vector", _oev)
                        bump(s_ve)
                        ckpt(f"grp_{b}_{p}_{g}")

                        if b == 0 and p == 0 and g == 0:
                            if "serow00" in dbg:
                                wait("gpsimd", s_ve)

                                def _ds(sy):
                                    inc(sy.dma_start(out=dbg["serow00"][:, :],
                                                     in_=serow[:, :]), s_gb, 16)
                                E("gpsimd", _ds)
                                bump(s_gb, 16)
                            if "phat00" in dbg:
                                wait("gpsimd", s_sc, tgt(s_sc))

                                def _dp(sy):
                                    inc(sy.dma_start(out=dbg["phat00"][:, :],
                                                     in_=u64b[:, PHA:PHA + 8192]),
                                        s_gb, 16)
                                E("gpsimd", _dp)
                                bump(s_gb, 16)
                    ckpt(f"pair_{b}_{p}")
            ATTN_VE_DONE = tgt(s_ve)
            ATTN_MM_DONE = tgt(s_mm)

            if b == 0 and "attnfm" in dbg:
                wait("gpsimd", s_ve)

                def _da(gp):
                    inc(gp.dma_start(out=dbg["attnfm"][:, :],
                                     in_=u74[:, AFM:AFM + DC * L]), s_wg, 16)
                E("gpsimd", _da)
                bump(s_wg, 16)

            ckpt(f"attn_{b}")
            wait("sync", s_mm)

            def _wo(sy, cvw=CV["wu"]):
                sy.wait_ge(semh[s_wg.name], cvw)
                inc(sy.dma_start(
                    out=u74[:, WOS:WOS + DC * D],
                    in_=bass.AP(woB.tensor, 0, [[D, 128], [128 * D, DC], [1, D]])),
                    s_wl, 16)
            E("sync", _wo)
            bump(s_wl, 16)
            ckpt(f"p3wo_{b}")
            wait("tensor", s_wl)
            wait("tensor", s_ve, ATTN_VE_DONE)
            wait("sync", s_sc)
            wait("sync", s_ve)
            for it in range(TT):
                vp = st8.get(("x2ve", it % 2), 0)
                psem = s_x2 if it % 2 == 0 else s_x3

                def _lx(sy, b=b, it=it, vp=vp, psem=psem):
                    if vp > 0:
                        sy.wait_ge(semh[s_ve.name], vp)
                    inc(sy.dma_start(out=xio[:, (it % 2) * D:(it % 2 + 1) * D],
                                     in_=xin[b, it * 128:(it + 1) * 128, :]),
                        psem, 16)
                E("sync", _lx)
                bump(psem, 16)
                XLT = tgt(psem)

                scp = st8.get(("wosc", it % 2), 0)

                def _wop(t, it=it, scp=scp):
                    if scp > 0:
                        t.wait_ge(semh[s_sc.name], scp)
                    b0 = ps[2 * (it % 2)]
                    b1 = ps[2 * (it % 2) + 1]
                    last = None
                    for kc in range(DC):
                        lhs = u74[:, AFM + kc * L + it * 128:
                                  AFM + kc * L + (it + 1) * 128]
                        t.matmul(b0[:, :], lhs,
                                 u74[:, WOS + kc * D:WOS + kc * D + 512],
                                 start=(kc == 0), stop=(kc == DC - 1))
                        last = t.matmul(b1[:, :], lhs,
                                        u74[:, WOS + kc * D + 512:WOS + (kc + 1) * D],
                                        start=(kc == 0), stop=(kc == DC - 1))
                    inc(last, s_mm)
                E("tensor", _wop)
                bump(s_mm)
                ckpt(f"p3mm_{b}_{it}")
                wait("scalar", s_mm)
                wait("scalar", s_ve)
                wait("vector", s_mm)
                cc = 192 + (it % 2) * 24

                def _2sq(sc, it=it, cc=cc):
                    b0 = ps[2 * (it % 2)]
                    b1 = ps[2 * (it % 2) + 1]
                    sc.activation(out=lnsc[:, 0:512], in_=b0[:, :], func=AF.Square,
                                  accum_out=stat[:, cc + 2:cc + 3])
                    inc(sc.activation(out=lnsc[:, 0:512], in_=b1[:, :],
                                      func=AF.Square,
                                      accum_out=stat[:, cc + 3:cc + 4]), s_sc)
                E("scalar", _2sq)
                bump(s_sc)
                wait("vector", s_sc)

                def _2ns(v, it=it, cc=cc):
                    b0 = ps[2 * (it % 2)]
                    b1 = ps[2 * (it % 2) + 1]
                    v.reduce_sum(stat[:, cc + 4:cc + 5], b0[:, :], axis=AX,
                                 negate=True)
                    inc(v.reduce_sum(stat[:, cc + 5:cc + 6], b1[:, :], axis=AX,
                                     negate=True), s_ve)
                E("vector", _2ns)
                bump(s_ve)
                ckpt(f"p3s1_{b}_{it}")
                wait("vector", s_sc)

                def _2var(v, cc=cc):
                    v.tensor_add(stat[:, cc + 6:cc + 7], stat[:, cc + 2:cc + 3],
                                 stat[:, cc + 3:cc + 4])
                    v.tensor_add(stat[:, cc + 7:cc + 8], stat[:, cc + 4:cc + 5],
                                 stat[:, cc + 5:cc + 6])
                    v.drain()
                    v.tensor_scalar(stat[:, cc + 8:cc + 9], stat[:, cc + 7:cc + 8],
                                    1.0 / D, None, op0=OP.mult)
                    v.tensor_scalar(stat[:, cc + 9:cc + 10], stat[:, cc + 6:cc + 7],
                                    1.0 / D, None, op0=OP.mult)
                    v.drain()
                    v.tensor_tensor(stat[:, cc + 10:cc + 11], stat[:, cc + 8:cc + 9],
                                    stat[:, cc + 8:cc + 9], op=OP.mult)
                    v.drain()
                    inc(v.tensor_sub(stat[:, cc + 11:cc + 12],
                                     stat[:, cc + 9:cc + 10],
                                     stat[:, cc + 10:cc + 11]), s_ve)
                E("vector", _2var)
                bump(s_ve)
                ckpt(f"p3s2_{b}_{it}")
                wait("scalar", s_ve)

                def _2rs(sc, cc=cc):
                    sc.activation(out=stat[:, cc + 12:cc + 13],
                                  in_=stat[:, cc + 11:cc + 12], func=AF.Ln,
                                  bias=ccol[:, 0:1])
                    inc(sc.activation(out=stat[:, cc + 0:cc + 1],
                                      in_=stat[:, cc + 12:cc + 13], func=AF.Exp,
                                      scale=-0.5), s_sc)
                E("scalar", _2rs)
                bump(s_sc)
                ckpt(f"p3s3_{b}_{it}")
                wait("vector", s_sc)

                def _2b(v, cc=cc):
                    inc(v.tensor_tensor(stat[:, cc + 1:cc + 2], stat[:, cc + 8:cc + 9],
                                        stat[:, cc + 0:cc + 1], op=OP.mult), s_ve)
                E("vector", _2b)
                bump(s_ve)
                wait("scalar", s_ve)

                def _2ap(sc, it=it, cc=cc):
                    b0 = ps[2 * (it % 2)]
                    b1 = ps[2 * (it % 2) + 1]
                    sc.activation(out=lnsc[:, 0:512], in_=b0[:, :], func=AF.Identity,
                                  scale=stat[:, cc + 0:cc + 1],
                                  bias=stat[:, cc + 1:cc + 2])
                    inc(sc.activation(out=lnsc[:, 512:1024], in_=b1[:, :],
                                      func=AF.Identity,
                                      scale=stat[:, cc + 0:cc + 1],
                                      bias=stat[:, cc + 1:cc + 2]), s_sc)
                E("scalar", _2ap)
                bump(s_sc)
                st8[("wosc", it % 2)] = tgt(s_sc)
                ckpt(f"p3ap_{b}_{it}")
                wait("vector", s_sc)
                wait("vector", psem, XLT)
                stp = st8.get(("x2st", it % 2), 0)
                if stp > 0:
                    wait("vector", s_st, stp)

                def _x2(v, it=it):
                    inc(v.tensor_add(outsb[:, (it % 2) * D:(it % 2 + 1) * D],
                                     xio[:, (it % 2) * D:(it % 2 + 1) * D],
                                     lnsc[:, 0:D]), s_ve)
                E("vector", _x2)
                bump(s_ve)
                st8[("x2ve", it % 2)] = tgt(s_ve)
                ckpt(f"p3x2_{b}_{it}")
                c3 = it * 12
                wait("scalar", s_ve)

                def _3sq(sc, it=it, c3=c3):
                    inc(sc.activation(out=lnsc[:, 0:D],
                                      in_=outsb[:, (it % 2) * D:(it % 2 + 1) * D],
                                      func=AF.Square,
                                      accum_out=stat[:, c3 + 2:c3 + 3]), s_sc)
                E("scalar", _3sq)
                bump(s_sc)

                def _3ns(v, it=it, c3=c3):
                    inc(v.reduce_sum(stat[:, c3 + 3:c3 + 4],
                                     outsb[:, (it % 2) * D:(it % 2 + 1) * D],
                                     axis=AX, negate=True), s_ve)
                E("vector", _3ns)
                bump(s_ve)
                wait("sync", s_ve)

                def _sx2(sy, it=it):
                    inc(sy.dma_start(out=x2d[it * 128:(it + 1) * 128, :],
                                     in_=outsb[:, (it % 2) * D:(it % 2 + 1) * D]),
                        s_st, 16)
                E("sync", _sx2)
                bump(s_st, 16)
                st8[("x2st", it % 2)] = tgt(s_st)
                ckpt(f"p3t_{b}_{it}")
            wait("vector", s_sc)
            ln_tail(0, range(TT), D, EPS)

            if b == 0 and "x2" in dbg:
                wait("gpsimd", s_st)

                def _dx2(sy):
                    last = None
                    for it in range(TT):
                        last = sy.dma_start(
                            out=dbg["x2"][it * 128:(it + 1) * 128, :],
                            in_=x2d[it * 128:(it + 1) * 128, :])
                        inc(last, s_gb, 16)
                E("sync", _dx2)
                bump(s_gb, 16 * TT)

            ckpt(f"p3_{b}")
            wait("sync", s_mm)

            def _wdl(sy, cvw=CV["wu"]):
                sy.wait_ge(semh[s_wg.name], cvw)
                inc(sy.dma_start(
                    out=u64b[:, WD0:WD0 + 24 * D],
                    in_=bass.AP(wdB.tensor, 0, [[D, 128], [128 * D, 24], [1, D]])),
                    s_wl, 16)
            E("sync", _wdl)
            bump(s_wl, 16)
            WL_WD = tgt(s_wl)
            wait("sync", s_st)

            for ic in range(NG):
                for k in range(4):
                    it = 4 * ic + k
                    scp = st8.get(("l3sc", k % 2), 0)
                    mmp3 = st8.get(("xtm_mm", it % 2), 0)
                    vp3 = st8.get(("x3ve", it % 2), 0)

                    psem = s_x2 if it % 2 == 0 else s_x3

                    def _lx3(sy, it=it, scp=scp, vp3=vp3, psem=psem):
                        if scp > 0:
                            sy.wait_ge(semh[s_sc.name], scp)
                        if vp3 > 0:
                            sy.wait_ge(semh[s_ve.name], vp3)
                        inc(sy.dma_start(out=xio[:, (it % 2) * D:(it % 2 + 1) * D],
                                         in_=x2d[it * 128:(it + 1) * 128, :]),
                            psem, 16)
                    E("sync", _lx3)
                    bump(psem, 16)
                    wait("scalar", psem)

                    def _3ap(sc, it=it, mmp3=mmp3):
                        if mmp3 > 0:
                            sc.wait_ge(semh[s_mm.name], mmp3)
                        c = it * 12
                        inc(sc.activation(
                            out=xhat_tm[:, (it % 2) * D:(it % 2 + 1) * D],
                            in_=xio[:, (it % 2) * D:(it % 2 + 1) * D],
                            func=AF.Identity, scale=stat[:, c + 0:c + 1],
                            bias=stat[:, c + 1:c + 2]), s_sc)
                    E("scalar", _3ap)
                    bump(s_sc)
                    st8[("l3sc", k % 2)] = tgt(s_sc)
                    sc_now = tgt(s_sc)
                    vp = st8.get("p1evac", 0)

                    def _tp3(t, it=it, sc_now=sc_now, vp=vp):
                        t.wait_ge(semh[s_sc.name], sc_now)
                        if vp > 0:
                            t.wait_ge(semh[s_ve.name], vp)
                        last = None
                        for c in range(DC):
                            last = t.transpose(ps_t[:, c * 128:(c + 1) * 128],
                                               xhat_tm[:, (it % 2) * D + c * 128:
                                                       (it % 2) * D + (c + 1) * 128],
                                               eye[:, :])
                        inc(last, s_mm)
                    E("tensor", _tp3)
                    bump(s_mm)
                    st8[("xtm_mm", it % 2)] = tgt(s_mm)
                    wait("vector", s_mm)

                    def _ev3(v, it=it):
                        dst = fmview(xhat_fm[:, :], DC, L, it * 128, 128)
                        inc(v.tensor_copy(dst, ps_t[:, :].rearrange(
                            "p (c n) -> p c n", n=128)), s_ve)
                    E("vector", _ev3)
                    bump(s_ve)
                    st8["p1evac"] = tgt(s_ve)

                wait("tensor", s_ve)
                for hc in range(24):
                    buf = hc % 2
                    mmp = st8.get(("gumm", buf), 0)

                    wsem = s_wh if hc % 2 == 0 else s_w2

                    def _wgl(sy, hc=hc, buf=buf, mmp=mmp, cvw=CV["wu"],
                             wsem=wsem):
                        sy.wait_ge(semh[s_wg.name], cvw)
                        if mmp > 0:
                            sy.wait_ge(semh[s_mm.name], mmp)
                        inc(sy.dma_start(
                            out=u64b[:, WG0 + buf * 1024:WG0 + (buf + 1) * 1024],
                            in_=bass.AP(wgB.tensor, hc * 128,
                                        [[HID, 128], [128 * HID, DC], [1, 128]])),
                            wsem, 16)
                        inc(sy.dma_start(
                            out=u64b[:, WU0 + buf * 1024:WU0 + (buf + 1) * 1024],
                            in_=bass.AP(wuB.tensor, hc * 128,
                                        [[HID, 128], [128 * HID, DC], [1, 128]])),
                            wsem, 16)
                    E("sync", _wgl)
                    bump(wsem, 32)
                    wait("tensor", wsem)
                    vep = st8.get(("guve", hc % 2), 0)

                    def _gu(t, ic=ic, hc=hc, buf=buf, vep=vep):
                        bg = ps[2 * (hc % 2)]
                        bu = ps[2 * (hc % 2) + 1]
                        if vep > 0:
                            t.wait_ge(semh[s_ve.name], vep)
                        last = None
                        for kc in range(DC):
                            rhs = xhat_fm[:, kc * L + ic * 512:kc * L + (ic + 1) * 512]
                            t.matmul(bg[:, :],
                                     u64b[:, WG0 + buf * 1024 + kc * 128:
                                          WG0 + buf * 1024 + (kc + 1) * 128],
                                     rhs, start=(kc == 0), stop=(kc == DC - 1))
                            last = t.matmul(bu[:, :],
                                            u64b[:, WU0 + buf * 1024 + kc * 128:
                                                 WU0 + buf * 1024 + (kc + 1) * 128],
                                            rhs, start=(kc == 0), stop=(kc == DC - 1))
                        inc(last, s_mm)
                    E("tensor", _gu)
                    bump(s_mm)
                    st8[("gumm", buf)] = tgt(s_mm)
                    wait("scalar", s_mm)

                    hmp = st8.get("hmve", 0)

                    def _si(sc, hc=hc, hmp=hmp):
                        if hmp > 0:
                            sc.wait_ge(semh[s_ve.name], hmp)
                        bg = ps[2 * (hc % 2)]
                        inc(sc.activation(out=gusb[:, :], in_=bg[:, :], func=AF.Silu,
                                          scale=1.0 / 127.0), s_sc)
                    E("scalar", _si)
                    bump(s_sc)
                    wait("vector", s_sc)

                    def _hm(v, hc=hc):
                        bu = ps[2 * (hc % 2) + 1]
                        inc(v.tensor_tensor(
                            u74[:, HFM + hc * 512:HFM + (hc + 1) * 512],
                            bu[:, :], gusb[:, :], op=OP.mult), s_ve)
                    E("vector", _hm)
                    bump(s_ve)
                    st8[("guve", hc % 2)] = tgt(s_ve)
                    st8["hmve"] = tgt(s_ve)

                if b == 0 and ic == 1 and "hfm1" in dbg:
                    wait("gpsimd", s_ve)

                    def _dh(sy):
                        inc(sy.dma_start(out=dbg["hfm1"][:, :],
                                         in_=u74[:, HFM:HFM + 24 * 512]), s_gb, 16)
                    E("gpsimd", _dh)
                    bump(s_gb, 16)
                wait("tensor", s_wl, WL_WD)
                for k in range(4):
                    it = 4 * ic + k
                    scp = st8.get("dnsc", 0)

                    def _dn(t, k=k, scp=scp):
                        if scp > 0:
                            t.wait_ge(semh[s_sc.name], scp)
                        last = None
                        for hc in range(24):
                            lhs = u74[:, HFM + hc * 512 + k * 128:
                                      HFM + hc * 512 + (k + 1) * 128]
                            t.matmul(ps[4][:, :], lhs,
                                     u64b[:, WD0 + hc * D:WD0 + hc * D + 512],
                                     start=(hc == 0), stop=(hc == 23))
                            last = t.matmul(ps[5][:, :], lhs,
                                            u64b[:, WD0 + hc * D + 512:
                                                 WD0 + (hc + 1) * D],
                                            start=(hc == 0), stop=(hc == 23))
                        inc(last, s_mm)
                    E("tensor", _dn)
                    bump(s_mm)
                    wait("scalar", s_mm)
                    wait("scalar", s_ve)
                    wait("vector", s_mm)
                    cc = 192 + (k % 2) * 24

                    def _4sq(sc, cc=cc):
                        sc.activation(out=lnsc[:, 0:512], in_=ps[4][:, :],
                                      func=AF.Square,
                                      accum_out=stat[:, cc + 2:cc + 3])
                        inc(sc.activation(out=lnsc[:, 0:512], in_=ps[5][:, :],
                                          func=AF.Square,
                                          accum_out=stat[:, cc + 3:cc + 4]), s_sc)
                    E("scalar", _4sq)
                    bump(s_sc)
                    wait("vector", s_sc)

                    def _4ns(v, cc=cc):
                        v.reduce_sum(stat[:, cc + 4:cc + 5], ps[4][:, :], axis=AX,
                                     negate=True)
                        inc(v.reduce_sum(stat[:, cc + 5:cc + 6], ps[5][:, :],
                                         axis=AX, negate=True), s_ve)
                    E("vector", _4ns)
                    bump(s_ve)
                    wait("vector", s_sc)

                    def _4var(v, cc=cc):
                        v.tensor_add(stat[:, cc + 6:cc + 7], stat[:, cc + 2:cc + 3],
                                     stat[:, cc + 3:cc + 4])
                        v.tensor_add(stat[:, cc + 7:cc + 8], stat[:, cc + 4:cc + 5],
                                     stat[:, cc + 5:cc + 6])
                        v.drain()
                        v.tensor_scalar(stat[:, cc + 8:cc + 9],
                                        stat[:, cc + 7:cc + 8], 1.0 / D, None,
                                        op0=OP.mult)
                        v.tensor_scalar(stat[:, cc + 9:cc + 10],
                                        stat[:, cc + 6:cc + 7], 1.0 / D, None,
                                        op0=OP.mult)
                        v.drain()
                        v.tensor_tensor(stat[:, cc + 10:cc + 11],
                                        stat[:, cc + 8:cc + 9],
                                        stat[:, cc + 8:cc + 9], op=OP.mult)
                        v.drain()
                        inc(v.tensor_sub(stat[:, cc + 11:cc + 12],
                                         stat[:, cc + 9:cc + 10],
                                         stat[:, cc + 10:cc + 11]), s_ve)
                    E("vector", _4var)
                    bump(s_ve)
                    wait("scalar", s_ve)

                    def _4rs(sc, cc=cc):
                        sc.activation(out=stat[:, cc + 12:cc + 13],
                                      in_=stat[:, cc + 11:cc + 12], func=AF.Ln,
                                      bias=ccol[:, 1:2])
                        inc(sc.activation(out=stat[:, cc + 0:cc + 1],
                                          in_=stat[:, cc + 12:cc + 13], func=AF.Exp,
                                          scale=-0.5), s_sc)
                    E("scalar", _4rs)
                    bump(s_sc)
                    wait("vector", s_sc)

                    def _4b(v, cc=cc):
                        inc(v.tensor_tensor(stat[:, cc + 1:cc + 2],
                                            stat[:, cc + 8:cc + 9],
                                            stat[:, cc + 0:cc + 1], op=OP.mult),
                            s_ve)
                    E("vector", _4b)
                    bump(s_ve)
                    wait("scalar", s_ve)

                    def _4ap(sc, cc=cc):
                        sc.activation(out=lnsc[:, 0:512], in_=ps[4][:, :],
                                      func=AF.Identity, scale=stat[:, cc + 0:cc + 1],
                                      bias=stat[:, cc + 1:cc + 2])
                        inc(sc.activation(out=lnsc[:, 512:1024], in_=ps[5][:, :],
                                          func=AF.Identity,
                                          scale=stat[:, cc + 0:cc + 1],
                                          bias=stat[:, cc + 1:cc + 2]), s_sc)
                    E("scalar", _4ap)
                    bump(s_sc)
                    st8["dnsc"] = tgt(s_sc)
                    if b == 0 and ic == 1 and "ffn_d" in dbg:
                        wait("gpsimd", s_sc)

                        def _df(sy, k=k):
                            inc(sy.dma_start(out=dbg["ffn_d"][k, :, :],
                                             in_=lnsc[:, 0:D]), s_gb, 16)
                        E("gpsimd", _df)
                        bump(s_gb, 16)
                        wait("vector", s_gb)

                    vp4 = st8.get(("x3ve", it % 2), 0)
                    scp4 = st8.get(("l3sc", it % 2), 0)

                    psem = s_x2 if it % 2 == 0 else s_x3

                    def _lx4(sy, it=it, vp4=vp4, scp4=scp4, psem=psem):
                        if vp4 > 0:
                            sy.wait_ge(semh[s_ve.name], vp4)
                        if scp4 > 0:
                            sy.wait_ge(semh[s_sc.name], scp4)
                        inc(sy.dma_start(out=xio[:, (it % 2) * D:(it % 2 + 1) * D],
                                         in_=x2d[it * 128:(it + 1) * 128, :]),
                            psem, 16)
                    E("sync", _lx4)
                    bump(psem, 16)
                    wait("vector", psem)
                    wait("vector", s_sc)
                    stp = st8.get(("x3st", it % 2), 0)
                    if stp > 0:
                        wait("vector", s_st, stp)

                    def _x3(v, it=it):
                        inc(v.tensor_add(outsb[:, (it % 2) * D:(it % 2 + 1) * D],
                                         xio[:, (it % 2) * D:(it % 2 + 1) * D],
                                         lnsc[:, 0:D]), s_ve)
                    E("vector", _x3)
                    bump(s_ve)
                    st8[("x3ve", it % 2)] = tgt(s_ve)
                    wait("sync", s_ve)

                    def _so(sy, b=b, it=it):
                        inc(sy.dma_start(out=out[b, it * 128:(it + 1) * 128, :],
                                         in_=outsb[:, (it % 2) * D:(it % 2 + 1) * D]),
                            s_st, 16)
                    E("sync", _so)
                    bump(s_st, 16)
                    st8[("x3st", it % 2)] = tgt(s_st)
                ckpt(f"ffn_{b}_{ic}")

        wait("sync", s_st)

        block = stack.enter_context(nc.Block())

        def run(eng_name):
            def body(e):
                for fn in prog[eng_name]:
                    fn(e)
            return body
        block.sync(run("sync"))
        block.gpsimd(run("gpsimd"))
        block.tensor(run("tensor"))
        block.scalar(run("scalar"))
        block.vector(run("vector"))

    return nc


def shard_inputs(L0_attn_x, L0_attn_Wq, L0_attn_Wk, L0_attn_Wv, L0_attn_Wo,
                 L0_attn_biases, L0_attn_tau, L0_ffn_w_gate, L0_ffn_w_up,
                 L0_ffn_w_down):
    shared = {
        "wqT": np.ascontiguousarray(np.asarray(L0_attn_Wq).T),
        "wkT": np.ascontiguousarray(np.asarray(L0_attn_Wk).T),
        "wvT": np.ascontiguousarray(np.asarray(L0_attn_Wv).T),
        "woT": np.ascontiguousarray(np.asarray(L0_attn_Wo).T),
        "wgT": np.ascontiguousarray(np.asarray(L0_ffn_w_gate).T),
        "wuT": np.ascontiguousarray(np.asarray(L0_ffn_w_up).T),
        "wdT": np.ascontiguousarray(np.asarray(L0_ffn_w_down).T),
        "gbt": host_gb_tiles(np.asarray(L0_attn_biases)),
        "biasd": np.ascontiguousarray(np.asarray(L0_attn_biases, dtype=np.float32)),
        "taud": np.ascontiguousarray(
            np.asarray(L0_attn_tau, dtype=np.float32)[None, :]),
    }
    shared.update(host_constants())
    x = np.ascontiguousarray(np.asarray(L0_attn_x, dtype=np.float32))
    return [dict(shared, xin=x[2 * c:2 * c + 2]) for c in range(8)]


def kernel(**inputs):
    maps = shard_inputs(**inputs)
    nc = build_program()
    res = run_bass_kernel_spmd(nc, maps, core_ids=list(range(8)))
    return np.concatenate([r["out"] for r in res.results], axis=0)


if __name__ == "__main__":
    nc = build_program()
    print("built ok,", nc.n_instructions() if hasattr(nc, "n_instructions") else "")
